# version 11
# baseline (speedup 1.0000x reference)
import contextlib

import numpy as np
import ml_dtypes

import concourse.bass as bass
import concourse.mybir as mybir
from concourse.bass_utils import run_bass_kernel_spmd

import concourse.bass_utils as _bu
if not getattr(_bu, "_ldw_patched", False):
    _orig_run_command = _bu.run_command

    def _run_command_ldw(cmd, *a, **kw):
        if isinstance(cmd, list):
            cmd = ["--enable-ldw-opt=true" if c == "--enable-ldw-opt=false"
                   else c for c in cmd]
        return _orig_run_command(cmd, *a, **kw)

    _bu.run_command = _run_command_ldw
    _bu._ldw_patched = True

F32 = mybir.dt.float32
BF16 = mybir.dt.bfloat16
ALU = mybir.AluOpType
ACT = mybir.ActivationFunctionType
AX = mybir.AxisListType
BF = ml_dtypes.bfloat16

C = 1024
TW = 512

ENGS = ("sync", "tensor", "vector", "scalar", "gpsimd")
OWN_SEM = {"tensor": "T", "vector": "V", "scalar": "S", "gpsimd": "G", "sync": "Y"}


class Ctx:
    def __init__(self, nc, sems):
        self.nc = nc
        self.sems = sems
        self.cur = None
        self.e = None
        self.counts = {}
        self.waited = {}
        self.base_dep = None

    def begin(self, name, engobj):
        self.cur = name
        self.e = engobj
        self.counts = {k: 0 for k in self.sems}
        self.waited = {k: -1 for k in self.sems}
        self.base_dep = None

    def op(self, eng, fn, deps=(), inc=None, inc_n=1):
        tok = None
        if inc is not None:
            self.counts[inc] += inc_n
            tok = (inc, self.counts[inc])
        if self.cur == eng:
            own = OWN_SEM.get(eng)
            deps = list(deps)
            if self.base_dep is not None:
                deps.append(self.base_dep)
            nw = 0
            for dep in deps:
                if dep is None:
                    continue
                s, v = dep
                if v > self.waited[s]:
                    if nw and nw % 2 == 0:
                        self.e.nop(nofuse=True)
                    self.e.wait_ge(self.sems[s], v)
                    self.waited[s] = v
                    nw += 1
            inst = fn(self.e)
            if inc is not None and inst is not None:
                inst.then_inc(self.sems[inc], inc_n)
        return tok

    def mm(self, fn, deps=(), inc=False):
        return self.op("tensor", fn, deps, "T" if inc else None)

    def vec(self, fn, deps=(), inc=True):
        return self.op("vector", fn, deps, "V" if inc else None)

    def sca(self, fn, deps=(), inc=True):
        return self.op("scalar", fn, deps, "S" if inc else None)

    def dma(self, queue, fn, deps=()):
        q = {"sync": "DQS", "scalar": "DQA", "gpsimd": "DQG"}[queue]
        return self.op(queue, fn, deps, q, 16)

    def tok(self, sem):
        return (sem, self.counts[sem])

    def guard(self, engs=("tensor", "vector", "scalar", "sync"),
              sems=("T", "V", "S", "DQS", "DQA")):
        snap = dict(self.counts)
        for eng in engs:
            deps = [(s, snap[s]) for s in sems if s != OWN_SEM.get(eng)]
            self.op(eng, lambda e: None, deps, None)

    def barrier(self):
        self.guard(engs=ENGS, sems=("T", "V", "S", "DQS", "DQA", "DQG"))


class Banks:
    def __init__(self):
        self.war = {}

    def acquire(self, b):
        d = self.war.get(b, [])
        self.war[b] = []
        return d

    def read(self, b, tok):
        if tok is not None:
            self.war.setdefault(b, []).append(tok)


SCHED = "full"


def build_program(T=8192, use_cc=True, stop_after=None, taps=False):
    nc = bass.Bass("TRN2", target_bir_lowering=False,
                   detect_race_conditions=False)
    NT = T // TW
    TOWN = T // 4
    PT = TOWN // 2
    PTW = min(512, PT)
    NTG = PT // PTW

    x_fm = nc.dram_tensor("x_fm", [C, T], F32, kind="ExternalInput")
    x_own = nc.dram_tensor("x_own", [C, TOWN], F32, kind="ExternalInput")
    wq_t = nc.dram_tensor("wq_t", [C, 256], F32, kind="ExternalInput")
    wkv_t = nc.dram_tensor("wkv_t", [C, 128], F32, kind="ExternalInput")
    wgbg_t = nc.dram_tensor("wgbg_t", [C, 258], F32, kind="ExternalInput")
    cwq_d = nc.dram_tensor("cwq", [256, 4], F32, kind="ExternalInput")
    cwkv_d = nc.dram_tensor("cwkv", [128, 4], F32, kind="ExternalInput")
    scal_d = nc.dram_tensor("scal", [1, 3], F32, kind="ExternalInput")
    rmsw_d = nc.dram_tensor("rmsw_row", [1, 256], F32, kind="ExternalInput")
    wpr_t = nc.dram_tensor("wpr_t", [256, C], F32, kind="ExternalInput")
    w1t = nc.dram_tensor("w1t", [C, 4096], F32, kind="ExternalInput")
    w2t = nc.dram_tensor("w2t", [C, 4096], F32, kind="ExternalInput")
    h1t = nc.dram_tensor("h1t", [4096, 4096], F32, kind="ExternalInput")
    h2t = nc.dram_tensor("h2t", [4096, 4096], F32, kind="ExternalInput")
    w3t = nc.dram_tensor("w3t", [4096, C], F32, kind="ExternalInput")
    sc_d = {}
    for nm, w in (("sc1", 32), ("sc2", 32), ("sch1", 32), ("sch2", 32), ("sc3", 8)):
        sc_d[nm] = nc.dram_tensor(nm, [128, w], F32, kind="ExternalInput")
    idn_d = nc.dram_tensor("idn_b", [128, 128], BF16, kind="ExternalInput")
    triu1_d = nc.dram_tensor("triu1", [128, 128], F32, kind="ExternalInput")
    tril0_d = nc.dram_tensor("tril0", [128, 128], F32, kind="ExternalInput")
    csq_d = nc.dram_tensor("csq", [2, 128], F32, kind="ExternalInput")
    cskv_d = nc.dram_tensor("cskv", [2, 128], F32, kind="ExternalInput")
    onesrow_d = nc.dram_tensor("onesrow", [1, TW], F32, kind="ExternalInput")

    out_fm = nc.dram_tensor("out_fm", [C, TOWN], F32, kind="ExternalOutput")
    py_in = nc.dram_tensor("py_in", [4, C, TOWN], BF16, kind="Internal")
    py_out = nc.dram_tensor("py_out", [C, TOWN], BF16, kind="Internal")
    PT_ = TOWN // 2
    tap_d = {}
    if isinstance(taps, (tuple, list, set)):
        shapes = {"A1": ([128, 32 * PT_], BF16), "B2": ([128, 32 * PT_], BF16),
                  "A3": ([128, 32 * PT_], BF16), "xn": ([128, 8 * TOWN], BF16),
                  "py": ([C, TOWN], BF16)}
        for nm in taps:
            tap_d[nm] = nc.dram_tensor("tap_" + nm, list(shapes[nm][0]),
                                       shapes[nm][1], kind="ExternalOutput")
    elif taps == "min":
        for nm in ("A1", "B2", "A3"):
            tap_d[nm] = nc.dram_tensor("tap_" + nm, [128, 32 * PT_], BF16,
                                       kind="ExternalOutput")
        tap_d["xn"] = nc.dram_tensor("tap_xn", [128, 8 * TOWN], BF16,
                                     kind="ExternalOutput")
        tap_d["py"] = nc.dram_tensor("tap_py", [C, TOWN], BF16,
                                     kind="ExternalOutput")
    elif taps:
        tap_d["xn"] = nc.dram_tensor("tap_xn", [128, 8 * TOWN], BF16,
                                     kind="ExternalOutput")
        tap_d["A1"] = nc.dram_tensor("tap_A1", [128, 32 * PT_], BF16,
                                     kind="ExternalOutput")
        tap_d["B2"] = nc.dram_tensor("tap_B2", [128, 32 * PT_], BF16,
                                     kind="ExternalOutput")
        tap_d["A3"] = nc.dram_tensor("tap_A3", [128, 32 * PT_], BF16,
                                     kind="ExternalOutput")
        tap_d["py"] = nc.dram_tensor("tap_py", [C, TOWN], BF16,
                                     kind="ExternalOutput")
        tap_d["pyin"] = nc.dram_tensor("tap_pyin", [4, C, TOWN], BF16,
                                       kind="ExternalOutput")
        for nm, shape, dt_ in (("qb0", [128, TW], BF16), ("kvb", [128, TW], BF16),
                               ("gate", [128, 1024], BF16), ("bg", [128, 8], F32),
                               ("Sf", [64, 64], F32), ("Of", [128, 256], F32),
                               ("ogfm", [128, 1024], BF16), ("xraw", [128, TW], F32),
                               ("rr", [128, TW], F32), ("nrc", [128, 4], F32),
                               ("nbb", [128, 1], F32), ("screp", [128, 3], F32),
                               ("cp0", [128, 8], F32), ("cp1", [128, 8], F32)):
            tap_d[nm] = nc.dram_tensor("tap_" + nm, shape, dt_,
                                       kind="ExternalOutput")

    st = contextlib.ExitStack()
    with st:
        sems = {}
        for s in ("T", "V", "S", "G", "Y", "DQS", "DQA", "DQG", "CC"):
            sems[s] = st.enter_context(nc.semaphore(s))
        ctx = Ctx(nc, sems)
        bk = Banks()

        def sb(stk, name, shape, dtype):
            return stk.enter_context(nc.sbuf_tensor(name, list(shape), dtype))

        PF = [st.enter_context(nc.psum_tensor(f"pf{i}", [128, 512], F32))
              for i in range(6)]
        PG = [st.enter_context(nc.psum_tensor(f"pg{i}", [128, 256], BF16))
              for i in range(2)]

        constpool = sb(st, "constpool", (128, 8), F32)
        CONSTS = {1e-5: 0, 1e-12: 1, 1.0: 2, -1.0: 3, -0.5: 4,
                  1.0 / C: 5, 1.0 / 64: 6, 0.5: 7}
        for v, c in CONSTS.items():
            nc.const_aps.register(F32, v, constpool[:, c:c + 1])
        ffn_scales = sb(st, "ffn_scales", (128, 136), F32)
        xn_own = sb(st, "xn_own", (128, 8 * TOWN), BF16)
        S_f = sb(st, "S_f", (64, 64), F32)
        S_b = sb(st, "S_b", (64, 64), BF16)

        p1 = contextlib.ExitStack()
        idb = sb(p1, "idb", (128, 128), BF16)
        triu1_f = sb(p1, "triu1_f", (128, 128), F32)
        tril0_f = sb(p1, "tril0_f", (128, 128), F32)
        ones_f = sb(p1, "ones_f", (1, 128), F32)
        onescol_b = sb(p1, "onescol_b", (128, 1), BF16)
        blk2_b = sb(p1, "blk2_b", (128, 2), BF16)
        colsel_q = sb(p1, "colsel_q", (2, 128), F32)
        colsel_kv = sb(p1, "colsel_kv", (2, 128), F32)
        kv_rhs = sb(p1, "kv_rhs", (2, TW), F32)
        rmsw_rep = sb(p1, "rmsw_rep", (128, 256), F32)
        srow = sb(p1, "srow", (1, 3), F32)
        screp = sb(p1, "screp", (128, 3), F32)
        negbb_rep = sb(p1, "negbb_rep", (128, 1), F32)
        negexpA = sb(p1, "negexpA", (128, 1), F32)
        wqt_b = sb(p1, "wqt_b", (128, 2048), BF16)
        wkvt_b = sb(p1, "wkvt_b", (128, 1024), BF16)
        wgbg_b = sb(p1, "wgbg_b", (128, 8 * 258), BF16)
        wpr_b = sb(p1, "wpr_b", (128, 2048), BF16)
        cwq0 = sb(p1, "cwq0", (128, 4), F32)
        cwq1 = sb(p1, "cwq1", (128, 4), F32)
        cwkv_s = sb(p1, "cwkv_s", (128, 4), F32)
        wsum_fm = sb(p1, "wsum_fm", (1, 384), BF16)
        wsum_gbg = sb(p1, "wsum_gbg", (1, 258), BF16)
        xq = [sb(p1, f"xq{p}", (128, 8 * TW), F32) for p in range(2)]
        xbf = [sb(p1, f"xbf{p}", (128, 8 * TW), BF16) for p in range(2)]
        xsqb = sb(p1, "xsqb", (128, 8 * TW), BF16)
        m_row = sb(p1, "m_row", (1, TW), F32)
        var_row = sb(p1, "var_row", (1, TW), F32)
        rstd_row = sb(p1, "rstd_row", (1, TW), F32)
        mr_row = sb(p1, "mr_row", (1, TW), F32)
        negm_b = sb(p1, "negm_b", (1, TW), BF16)
        rrep = sb(p1, "rrep", (128, TW), F32)
        mrrep = sb(p1, "mrrep", (128, TW), F32)
        qraw0 = sb(p1, "qraw0", (128, TW + 3), F32)
        qraw1 = sb(p1, "qraw1", (128, TW + 3), F32)
        kvraw = sb(p1, "kvraw", (128, TW + 3), F32)
        qc0 = sb(p1, "qc0", (128, TW), F32)
        qc1 = sb(p1, "qc1", (128, TW), F32)
        kvc = sb(p1, "kvc", (128, TW), F32)
        rsq01 = sb(p1, "rsq01", (2, TW), F32)
        rsq23 = sb(p1, "rsq23", (2, TW), F32)
        TB = []
        for ti in range(2):
            tb = {}
            for nm, shape, dt_ in (
                    ("qb0", (128, TW), BF16), ("qb1", (128, TW), BF16),
                    ("kvb", (128, TW), BF16), ("qodd0", (64, TW), BF16),
                    ("qodd1", (64, TW), BF16), ("gate_tm", (128, 1024), BF16),
                    ("beta_c", (128, 4), F32), ("g_c", (128, 4), F32),
                    ("rcol", (128, 4), F32), ("nrcol", (128, 4), F32),
                    ("ogfm", (128, 1024), BF16)):
                tb[nm] = sb(p1, f"{nm}_t{ti}", shape, dt_)
            TB.append(tb)
        CB = []
        for pi in range(4):
            cb = {}
            for nm, shape, dt_ in (
                    ("KKb", (128, 128), F32), ("dect", (128, 128), F32),
                    ("dmst", (128, 128), F32), ("tmpf", (128, 128), F32),
                    ("M_b", (128, 128), BF16), ("B0_b", (128, 128), BF16),
                    ("Y_b", (128, 128), BF16),
                    ("AB0", (128, 256), BF16), ("AB1", (128, 256), BF16),
                    ("RHS2", (128, 128), BF16), ("khat", (128, 64), BF16),
                    ("U2f", (128, 128), F32), ("U2b", (128, 128), BF16),
                    ("resb", (128, 128), BF16), ("amsk", (128, 512), BF16),
                    ("rtmp", (128, 128), F32), ("WtU", (128, 128), BF16),
                    ("U_b", (128, 64), BF16), ("O_f", (128, 256), F32),
                    ("Osq", (128, 256), F32), ("rms_in", (128, 4), F32),
                    ("rms4", (128, 4), F32), ("grw", (128, 256), F32),
                    ("og_bf", (128, 256), BF16), ("ebcol", (128, 1), F32),
                    ("dlcol", (128, 1), F32), ("eblcol", (128, 1), F32),
                    ("bcol", (128, 1), F32), ("nbcol", (128, 1), F32),
                    ("bebe", (128, 1), F32), ("brow", (1, 128), F32),
                    ("ebtmp", (128, 64), F32)):
                cb[nm] = sb(p1, f"{nm}{pi}", shape, dt_)
            CB.append(cb)
        pystage = sb(p1, "pystage", (128, 1024), BF16)
        p1.close()

        ffnA = sb(st, "ffnA", (128, 32 * PT), BF16)
        ffnB = sb(st, "ffnB", (128, 32 * PT), BF16)
        Wst = [sb(st, f"fw{p}", (128, 4096), BF16) for p in range(2)]
        ft1 = sb(st, "ffnt1", (128, PTW), F32)
        ft2 = sb(st, "ffnt2", (128, PTW), F32)
        fxs = sb(st, "xstage", (128, PT), F32)
        fys_b = sb(st, "ystage_b", (128, PT), BF16)
        fys = sb(st, "ystage", (128, PT), F32)
        fos = sb(st, "ostage", (128, PT), F32)

        kv = {}

        def setup():
            dma, mm, vec, sca = ctx.dma, ctx.mm, ctx.vec, ctx.sca
            dma("sync", lambda e: e.dma_start(out=idb[:], in_=idn_d.ap()))
            dma("sync", lambda e: e.dma_start(out=triu1_f[:], in_=triu1_d.ap()))
            dma("sync", lambda e: e.dma_start(out=tril0_f[:], in_=tril0_d.ap()))
            dma("sync", lambda e: e.dma_start(out=srow[:], in_=scal_d.ap()))
            dma("sync", lambda e: e.dma_start(out=cwq0[:], in_=cwq_d.ap()[0:128, :]))
            dma("sync", lambda e: e.dma_start(out=cwq1[:], in_=cwq_d.ap()[128:256, :]))
            dma("sync", lambda e: e.dma_start(out=cwkv_s[:], in_=cwkv_d.ap()))
            dma("sync", lambda e: e.dma_start(out=rmsw_rep[0:1, 0:256], in_=rmsw_d.ap()))
            for j, (nm, c0, w) in enumerate([("sc1", 0, 32), ("sc2", 32, 32),
                                             ("sch1", 64, 32), ("sch2", 96, 32),
                                             ("sc3", 128, 8)]):
                dma("sync", lambda e, nm=nm, c0=c0, w=w: e.dma_start(
                    out=ffn_scales[:, c0:c0 + w], in_=sc_d[nm].ap()))
            dma("sync", lambda e: e.dma_start(out=colsel_q[:], in_=csq_d.ap()))
            dma("sync", lambda e: e.dma_start(out=colsel_kv[:], in_=cskv_d.ap()))
            dma("sync", lambda e: e.dma_start(out=kv_rhs[1:2, 0:TW], in_=onesrow_d.ap()))
            t_cl = ctx.tok("DQS")
            kv["t_scales"] = t_cl
            for r in range(8):
                dma("gpsimd", lambda e, r=r: e.dma_start(
                    out=wqt_b[:, 256 * r:256 * (r + 1)],
                    in_=wq_t.ap()[128 * r:128 * (r + 1), :]))
                dma("gpsimd", lambda e, r=r: e.dma_start(
                    out=wkvt_b[:, 128 * r:128 * (r + 1)],
                    in_=wkv_t.ap()[128 * r:128 * (r + 1), :]))
                dma("gpsimd", lambda e, r=r: e.dma_start(
                    out=wgbg_b[:, 258 * r:258 * (r + 1)],
                    in_=wgbg_t.ap()[128 * r:128 * (r + 1), :]))
            for r in range(2):
                dma("gpsimd", lambda e, r=r: e.dma_start(
                    out=wpr_b[:, 1024 * r:1024 * (r + 1)],
                    in_=wpr_t.ap()[128 * r:128 * (r + 1), :]))
            kv["t_wload"] = ctx.tok("DQG")
            for f in (
                lambda e: e.memset(ones_f[:], 1.0),
                lambda e: e.memset(onescol_b[:], 1.0),
                lambda e: e.memset(blk2_b[:], 0.0),
                lambda e: e.memset(blk2_b[0:64, 0:1], 1.0),
                lambda e: e.memset(blk2_b[64:128, 1:2], 1.0),
                lambda e: e.memset(S_f[:], 0.0),
                lambda e: e.memset(S_b[:], 0.0),
                lambda e: e.memset(qraw0[:, 0:3], 0.0),
                lambda e: e.memset(qraw1[:, 0:3], 0.0),
                lambda e: e.memset(kvraw[:, 0:3], 0.0),
            ):
                vec(f, inc=False)
            for v, c in CONSTS.items():
                vec(lambda e, v=v, c=c: e.memset(constpool[:, c:c + 1], v), inc=False)
            if taps:
                tcp = vec(lambda e: e.tensor_copy(rrep[:, 0:8], constpool[:, :]))
                tap("cp0", lambda: rrep[:, 0:8], [tcp])
            vec(lambda e: e.memset(TB[0]["rcol"][:], 0.0), inc=False)
            t_ms = vec(lambda e: e.memset(TB[1]["rcol"][:], 0.0))
            kv["t_consts"] = t_ms
            kv["t_Sb"] = t_ms
            kv["t_Sf"] = t_ms
            ctx.base_dep = t_ms
            t_mm1 = mm(lambda e: e.matmul(PF[1][:, 0:3], ones_f[:, :], srow[:, :],
                                          start=True, stop=True),
                       deps=[t_cl, t_ms] + bk.acquire(1), inc=True)
            t_ev = vec(lambda e: e.tensor_copy(screp[:], PF[1][:, 0:3]), deps=[t_mm1])
            bk.read(1, t_ev)
            sca(lambda e: e.mul(negbb_rep[:], screp[:, 0:1], -1.0), deps=[t_ev],
                inc=False)
            t_e1 = sca(lambda e: e.activation(negexpA[:], screp[:, 1:2], ACT.Exp),
                       deps=[t_ev])
            t_e2 = vec(lambda e: e.tensor_scalar_mul(negexpA[:], negexpA[:], -1.0),
                       deps=[t_e1])
            kv["t_reps"] = t_e2
            t_mm2 = mm(lambda e: e.matmul(PF[2][:, 0:256], ones_f[:, :],
                                          rmsw_rep[0:1, 0:256], start=True, stop=True),
                       deps=[t_cl, t_ms] + bk.acquire(2), inc=True)
            t_rw = vec(lambda e: e.tensor_copy(rmsw_rep[:], PF[2][:, 0:256]),
                       deps=[t_mm2])
            bk.read(2, t_rw)
            kv["t_rmsw"] = t_rw
            d0 = bk.acquire(0)
            for r in range(8):
                mm(lambda e, r=r: e.matmul(PF[0][0:1, 0:256], onescol_b[:, :],
                                           wqt_b[:, 256 * r:256 * (r + 1)],
                                           start=(r == 0), stop=(r == 7)),
                   deps=([kv["t_wload"], t_ms] + d0) if r == 0 else ())
            for r in range(8):
                mm(lambda e, r=r: e.matmul(PF[0][0:1, 256:384], onescol_b[:, :],
                                           wkvt_b[:, 128 * r:128 * (r + 1)],
                                           start=(r == 0), stop=(r == 7)),
                   inc=(r == 7))
            t_q = ctx.tok("T")
            d3 = bk.acquire(3)
            for r in range(8):
                mm(lambda e, r=r: e.matmul(PF[3][0:1, 0:258], onescol_b[:, :],
                                           wgbg_b[:, 258 * r:258 * (r + 1)],
                                           start=(r == 0), stop=(r == 7)),
                   deps=d3 if r == 0 else (), inc=(r == 7))
            t_g = ctx.tok("T")
            t_w1 = vec(lambda e: e.tensor_copy(wsum_fm[:], PF[0][0:1, 0:384]),
                       deps=[t_q])
            bk.read(0, t_w1)
            t_w2 = vec(lambda e: e.tensor_copy(wsum_gbg[:], PF[3][0:1, 0:258]),
                       deps=[t_g])
            bk.read(3, t_w2)
            kv["t_wsum"] = t_w2
            kv["t_wsumf"] = t_w1
            kv["xq_war"] = [[], []]
            kv["xbf_war"] = [[], []]
            kv["py_done"] = []
            kv["og_done"] = []
            kv["py_war"] = [None, None]
            kv["qscr_war"] = []

        def emit_ln(par, t_xload, t_xn_dep=None):
            mm, vec, sca = ctx.mm, ctx.vec, ctx.sca
            t_sq = sca(lambda e: e.activation(xsqb[:], xq[par][:], ACT.Square),
                       deps=[t_xload])
            t_cast = sca(lambda e: e.copy(xbf[par][:], xq[par][:]),
                         deps=[t_xload] + kv["xbf_war"][par])
            kv["xbf_war"][par] = []
            kv["xq_war"][par] += [t_cast, t_sq]
            d0 = bk.acquire(0)
            d1 = bk.acquire(1)
            for r in range(8):
                mm(lambda e, r=r, par=par: e.matmul(
                    PF[0][0:1, 0:TW], onescol_b[:, :], xbf[par][:, TW * r:TW * (r + 1)],
                    start=(r == 0), stop=(r == 7)),
                    deps=([t_cast, kv["t_consts"]] + d0) if r == 0 else (),
                    inc=(r == 7))
            t_s1 = ctx.tok("T")
            for r in range(8):
                mm(lambda e, r=r: e.matmul(
                    PF[1][0:1, 0:TW], onescol_b[:, :], xsqb[:, TW * r:TW * (r + 1)],
                    start=(r == 0), stop=(r == 7)),
                    deps=([t_sq] + d1) if r == 0 else (), inc=(r == 7))
            t_s2 = ctx.tok("T")
            t_m = sca(lambda e: e.activation(m_row[:], PF[0][0:1, 0:TW], ACT.Copy,
                                             scale=1.0 / C), deps=[t_s1])
            bk.read(0, t_m)
            t_v0 = sca(lambda e: e.activation(var_row[:], PF[1][0:1, 0:TW], ACT.Copy,
                                              scale=1.0 / C), deps=[t_s2])
            bk.read(1, t_v0)
            t_msq = vec(lambda e: e.tensor_mul(mr_row[:], m_row[:], m_row[:]),
                        deps=[t_m])
            t_var = vec(lambda e: e.tensor_sub(var_row[:], var_row[:], mr_row[:]),
                        deps=[t_v0, t_msq])
            t_ln = sca(lambda e: e.activation(rstd_row[:], var_row[:], ACT.Ln,
                                              bias=1e-5), deps=[t_var])
            t_r = sca(lambda e: e.activation(rstd_row[:], rstd_row[:], ACT.Exp,
                                             scale=-0.5), deps=[t_ln])
            return t_cast, t_m, t_r

        def emit_tile(i):
            mm, vec, sca, dma = ctx.mm, ctx.vec, ctx.sca, ctx.dma
            par = i % 2
            tb = TB[par]
            kv["tb"] = tb
            qb0, qb1, kvb = tb["qb0"], tb["qb1"], tb["kvb"]
            qodd0, qodd1 = tb["qodd0"], tb["qodd1"]
            gate_tm, beta_c, g_c = tb["gate_tm"], tb["beta_c"], tb["g_c"]
            rcol, nrcol, ogfm = tb["rcol"], tb["nrcol"], tb["ogfm"]
            snap = kv.get(f"tsnap{par}")
            if snap is not None:
                for eng in ("tensor", "vector", "scalar", "sync"):
                    deps = [(sm, snap[sm]) for sm in ("T", "V", "S", "DQS", "DQA")]
                    ctx.op(eng, lambda e: None, deps, None)
            t_xload = kv[f"t_x{i}"]
            if i + 1 < NT:
                npar = (i + 1) % 2
                war = kv["xq_war"][npar]
                kv["xq_war"][npar] = []
                for r in range(8):
                    dma("sync", lambda e, r=r, i=i, npar=npar: e.dma_start(
                        out=xq[npar][:, TW * r:TW * (r + 1)],
                        in_=x_fm.ap()[128 * r:128 * (r + 1), TW * (i + 1):TW * (i + 2)]),
                        deps=war)
                kv[f"t_x{i+1}"] = ctx.tok("DQS")
            t_cast, t_m, t_r = emit_ln(par, t_xload)
            t_nm = sca(lambda e: e.activation(negm_b[:], m_row[:], ACT.Copy,
                                              scale=-1.0), deps=[t_m])
            d5 = bk.acquire(5)
            t_rrmm = mm(lambda e: e.matmul(PF[5][:, 0:TW], ones_f[:, :], rstd_row[:, :],
                                           start=True, stop=True),
                        deps=[t_r] + d5, inc=True)
            t_rrep = vec(lambda e: e.tensor_copy(rrep[:], PF[5][:, 0:TW]),
                         deps=[t_rrmm])
            bk.read(5, t_rrep)
            d3 = bk.acquire(3)
            for cc in range(4):
                mm(lambda e, cc=cc: e.matmul(PF[3][:, cc:cc + 1],
                                             rstd_row[0:1, 128 * cc:128 * (cc + 1)],
                                             ones_f[0:1, 0:1], start=True, stop=True),
                   deps=([t_r] + d3) if cc == 0 else (), inc=(cc == 3))
            t_rcmm = ctx.tok("T")
            t_rc = vec(lambda e: e.tensor_copy(rcol[:], PF[3][:, 0:4]), deps=[t_rcmm])
            bk.read(3, t_rc)
            t_nrc = vec(lambda e: e.tensor_scalar_mul(nrcol[:], rcol[:], -1.0),
                        deps=[t_rc])
            proj = []
            for (bnk, coff) in ((0, 0), (1, 128), (4, None)):
                dd = bk.acquire(bnk)
                for r in range(8):
                    if coff is None:
                        lhs = lambda e, r=r: wkvt_b[:, 128 * r:128 * (r + 1)]
                    else:
                        lhs = lambda e, r=r, coff=coff: wqt_b[:, 256 * r + coff:256 * r + coff + 128]
                    mm(lambda e, r=r, par=par, bnk=bnk, lhs=lhs: e.matmul(
                        PF[bnk][:, 0:TW], lhs(e), xbf[par][:, TW * r:TW * (r + 1)],
                        start=(r == 0), stop=False),
                        deps=(dd + [t_cast, kv["t_wsumf"], kv["t_wload"]]) if r == 0 else ())
                woff = {0: (0, 128), 1: (128, 256), 4: (256, 384)}[bnk]
                mm(lambda e, bnk=bnk, woff=woff: e.matmul(
                    PF[bnk][:, 0:TW], wsum_fm[0:1, woff[0]:woff[1]], negm_b[:, :],
                    start=False, stop=True), deps=[t_nm], inc=True)
                proj.append(ctx.tok("T"))
            t_q0p, t_q1p, t_kvp = proj
            evs = []
            for (buf, bnk, tq) in ((qraw0, 0, t_q0p), (qraw1, 1, t_q1p), (kvraw, 4, t_kvp)):
                if i > 0:
                    vec(lambda e, buf=buf: e.tensor_copy(buf[:, 0:3], buf[:, TW:TW + 3]),
                        deps=kv["qscr_war"], inc=False)
                t_e = vec(lambda e, buf=buf, bnk=bnk: e.tensor_mul(
                    buf[:, 3:TW + 3], PF[bnk][:, 0:TW], rrep[:]), deps=[tq, t_rrep])
                bk.read(bnk, t_e)
                evs.append(t_e)
            t_evq0, t_evq1, t_evkv = evs
            kv["qscr_war"] = []
            gtoks = []
            for cc in range(4):
                bnk = (3, 5, 0, 1)[cc]
                dd = bk.acquire(bnk)
                for r in range(8):
                    mm(lambda e, r=r, cc=cc, par=par, bnk=bnk: e.matmul(
                        PF[bnk][:, 0:258],
                        xbf[par][:, TW * r + 128 * cc:TW * r + 128 * cc + 128],
                        wgbg_b[:, 258 * r:258 * (r + 1)], start=(r == 0), stop=False),
                        deps=(dd + [kv["t_wsum"]]) if r == 0 else ())
                mm(lambda e, cc=cc, bnk=bnk: e.matmul(
                    PF[bnk][:, 0:258], negm_b[0:1, 128 * cc:128 * (cc + 1)],
                    wsum_gbg[:, :], start=False, stop=True), deps=[t_nm], inc=True)
                gtoks.append((cc, bnk, ctx.tok("T")))
            for (cc, bnk, t_gc) in gtoks:
                t_si = sca(lambda e, cc=cc, bnk=bnk: e.activation(
                    gate_tm[:, 256 * cc:256 * (cc + 1)], PF[bnk][:, 0:256], ACT.Silu,
                    scale=rcol[:, cc:cc + 1]), deps=[t_gc, t_rc])
                bk.read(bnk, t_si)
            for (cc, bnk, t_gc) in gtoks:
                t_be = sca(lambda e, cc=cc, bnk=bnk: e.activation(
                    beta_c[:, cc:cc + 1], PF[bnk][:, 256:257], ACT.Exp,
                    scale=nrcol[:, cc:cc + 1], bias=negbb_rep[:, :]),
                    deps=[t_gc, t_nrc, kv["t_reps"]])
                t_b2 = vec(lambda e, cc=cc: e.tensor_scalar_add(
                    beta_c[:, cc:cc + 1], beta_c[:, cc:cc + 1], 1.0), deps=[t_be])
                t_b3 = vec(lambda e, cc=cc: e.reciprocal(
                    beta_c[:, cc:cc + 1], beta_c[:, cc:cc + 1]), deps=[t_b2])
                t_ge = sca(lambda e, cc=cc, bnk=bnk: e.activation(
                    g_c[:, cc:cc + 1], PF[bnk][:, 257:258], ACT.Exp,
                    scale=rcol[:, cc:cc + 1], bias=screp[:, 2:3]), deps=[t_gc])
                t_g2 = sca(lambda e, cc=cc: e.activation(
                    g_c[:, cc:cc + 1], g_c[:, cc:cc + 1], ACT.Ln, bias=1.0),
                    deps=[t_ge])
                t_g3 = vec(lambda e, cc=cc: e.tensor_mul(
                    g_c[:, cc:cc + 1], g_c[:, cc:cc + 1], negexpA[:]), deps=[t_g2])
                bk.read(bnk, t_ge)
                kv[f"t_beta{cc}"] = t_b3
                kv[f"t_g{cc}"] = t_g3
            convs = []
            for (cin, cout, w, tdep) in ((qraw0, qc0, cwq0, t_evq0),
                                         (qraw1, qc1, cwq1, t_evq1),
                                         (kvraw, kvc, cwkv_s, t_evkv)):
                t_cj = vec(lambda e, cin=cin, cout=cout, w=w: e.tensor_scalar_mul(
                    cout[:], cin[:, 0:TW], w[:, 0:1]),
                    deps=[tdep, kv["t_scales"]])
                for j in range(1, 4):
                    t_cj = vec(lambda e, cin=cin, cout=cout, w=w, j=j: e.scalar_tensor_tensor(
                        cout[:], cin[:, j:TW + j], w[:, j:j + 1], cout[:],
                        op0=ALU.mult, op1=ALU.add), deps=[t_cj])
                convs.append(t_cj)
            t_cq0, t_cq1, t_ckv = convs
            kv["qscr_war"] = list(convs)
            t_sq0 = sca(lambda e: e.activation(qb0[:], qc0[:], ACT.Square),
                        deps=[t_cq0])
            t_sq1 = sca(lambda e: e.activation(qb1[:], qc1[:], ACT.Square),
                        deps=[t_cq1])
            t_sqk = sca(lambda e: e.activation(kvb[0:64, :], kvc[0:64, :], ACT.Square),
                        deps=[t_ckv])
            d0 = bk.acquire(0)
            t_ssq0 = mm(lambda e: e.matmul(PF[0][0:2, 0:TW], blk2_b[:, :], qb0[:, :],
                                           start=True, stop=True),
                        deps=[t_sq0] + d0, inc=True)
            d1 = bk.acquire(1)
            t_ssq1 = mm(lambda e: e.matmul(PF[1][0:2, 0:TW], blk2_b[:, :], qb1[:, :],
                                           start=True, stop=True),
                        deps=[t_sq1] + d1, inc=True)
            d4 = bk.acquire(4)
            t_ssqk = mm(lambda e: e.matmul(PF[4][0:1, 0:TW], onescol_b[0:64, :],
                                           kvb[0:64, :], start=True, stop=True),
                        deps=[t_sqk] + d4, inc=True)
            t_l0 = sca(lambda e: e.activation(rsq01[:], PF[0][0:2, 0:TW], ACT.Ln,
                                              bias=1e-12), deps=[t_ssq0])
            bk.read(0, t_l0)
            t_r0 = sca(lambda e: e.activation(rsq01[:], rsq01[:], ACT.Exp, scale=-0.5),
                       deps=[t_l0])
            t_l1 = sca(lambda e: e.activation(rsq23[:], PF[1][0:2, 0:TW], ACT.Ln,
                                              bias=1e-12), deps=[t_ssq1])
            bk.read(1, t_l1)
            t_r1 = sca(lambda e: e.activation(rsq23[:], rsq23[:], ACT.Exp, scale=-0.5),
                       deps=[t_l1])
            t_lk = sca(lambda e: e.activation(kv_rhs[0:1, 0:TW], PF[4][0:1, 0:TW],
                                              ACT.Ln, bias=1e-12), deps=[t_ssqk])
            bk.read(4, t_lk)
            t_rk = sca(lambda e: e.activation(kv_rhs[0:1, 0:TW], kv_rhs[0:1, 0:TW],
                                              ACT.Exp, scale=-0.5), deps=[t_lk])
            qbt = []
            for (t_seed, rr, sel, src, dst, t_scr) in (
                    (t_r0, rsq01, colsel_q, qc0, qb0, t_sq0),
                    (t_r1, rsq23, colsel_q, qc1, qb1, t_sq1),
                    (t_rk, kv_rhs, colsel_kv, kvc, kvb, t_sqk)):
                d5 = bk.acquire(5)
                t_bq = mm(lambda e, sel=sel, rr=rr: e.matmul(
                    PF[5][:, 0:TW], sel[:, :], rr[:, :], start=True, stop=True),
                    deps=[t_seed] + d5, inc=True)
                t_qb = vec(lambda e, src=src, dst=dst: e.tensor_mul(
                    dst[:], src[:], PF[5][:, 0:TW]), deps=[t_bq, t_scr])
                bk.read(5, t_qb)
                qbt.append(t_qb)
            t_qb0, t_qb1, t_kvb = qbt
            t_o0 = dma("sync", lambda e: e.dma_start(out=qodd0[:, :], in_=qb0[64:128, :]),
                       deps=[t_qb0])
            t_o1 = dma("sync", lambda e: e.dma_start(out=qodd1[:, :], in_=qb1[64:128, :]),
                       deps=[t_qb1])
            kv["t_qb"] = (t_qb0, t_qb1, t_kvb, t_o0, t_o1)
            kv["xbf_war"][par].append(ctx.tok("T"))
            if i == 0 and taps:
                tap("cp1", lambda: constpool[:, :], [t_nrc])
                tap("nrc", lambda: nrcol[:, :], [t_nrc])
                tap("nbb", lambda: negbb_rep[:, :], [kv["t_reps"]])
                tap("screp", lambda: screp[:, :], [kv["t_reps"]])
                tap("qb0", lambda: qb0[:, :], [t_qb0])
                tap("kvb", lambda: kvb[:, :], [t_kvb])
                tap("xraw", lambda: qc0[:, :], [t_cq0])
                tap("rr", lambda: rrep[:, :], [t_rrep])
                tap("gate", lambda: gate_tm[:, :],
                    [kv["t_beta3"], kv["t_g3"], ctx.tok("S")])
                tap("bg", lambda: beta_c[:, :], [kv["t_beta3"]],
                    dst=lambda: tap_d["bg"].ap()[:, 0:4])
                tap("bg", lambda: g_c[:, :], [kv["t_g3"]],
                    dst=lambda: tap_d["bg"].ap()[:, 4:8])
            if SCHED == "seq":
                for cc in range(4):
                    drive(gen_indep(i, cc))
                    drive(gen_chain(i, cc))
            elif SCHED == "pair":
                drive(gen_indep(i, 0), gen_indep(i, 1))
                drive(gen_chain(i, 0))
                drive(gen_indep(i, 2))
                drive(gen_chain(i, 1))
                drive(gen_indep(i, 3))
                drive(gen_chain(i, 2))
                drive(gen_chain(i, 3))
            else:
                drive(gen_indep(i, 0), gen_indep(i, 1))
                drive(gen_chain(i, 0), gen_indep(i, 2))
                drive(gen_chain(i, 1), gen_indep(i, 3))
                drive(gen_chain(i, 2))
                drive(gen_chain(i, 3))
            if i == 0 and taps:
                tap("Sf", lambda: S_f[:, :], [kv["t_Sf"]])
                tap("Of", lambda: CB[1]["O_f"][:, :],
                    [kv["t_of0"], kv["t_of1"], kv["t_of2"], kv["t_of3"]])
                tap("ogfm", lambda: ogfm[:, :], list(kv["og_done"]))
            quarter, qoff = (TW * i) // TOWN, (TW * i) % TOWN
            ogdeps = kv["og_done"]
            kv["og_done"] = []
            for r in range(8):
                bnk = 1 if r % 2 == 0 else 2
                half = r % 2
                dd = bk.acquire(bnk)
                for j in range(2):
                    mm(lambda e, r=r, j=j, bnk=bnk: e.matmul(
                        PF[bnk][:, 0:TW],
                        wpr_b[:, 1024 * j + 128 * r:1024 * j + 128 * (r + 1)],
                        ogfm[:, TW * j:TW * (j + 1)], start=(j == 0), stop=(j == 1)),
                        deps=(dd + ogdeps + [kv["t_wload"]]) if j == 0 else (),
                        inc=(j == 1))
                t_pm = ctx.tok("T")
                t_pe = vec(lambda e, bnk=bnk, half=half: e.tensor_copy(
                    pystage[:, TW * half:TW * (half + 1)], PF[bnk][:, 0:TW]),
                    deps=[t_pm, kv["py_war"][half]])
                bk.read(bnk, t_pe)
                t_pd = dma("scalar", lambda e, r=r, quarter=quarter, qoff=qoff, half=half:
                           e.dma_start(
                               out=py_in.ap()[quarter, 128 * r:128 * (r + 1),
                                              qoff:qoff + TW],
                               in_=pystage[:, TW * half:TW * (half + 1)]),
                           deps=[t_pe])
                kv["py_war"][half] = t_pd
                kv["py_done"].append(t_pd)
            kv[f"tsnap{par}"] = dict(ctx.counts)

        def drive(*gens):
            gens = [g for g in gens if g is not None]
            while gens:
                nxt = []
                for g in gens:
                    try:
                        next(g)
                        nxt.append(g)
                    except StopIteration:
                        pass
                gens = nxt

        def gen_indep(i, cc):
            mm, vec, sca = ctx.mm, ctx.vec, ctx.sca
            tb = kv["tb"]
            qb0, qb1, kvb = tb["qb0"], tb["qb1"], tb["kvb"]
            qodd0, qodd1 = tb["qodd0"], tb["qodd1"]
            gate_tm, beta_c, g_c = tb["gate_tm"], tb["beta_c"], tb["g_c"]
            b = CB[cc]
            BA, BB, BC = (0, 1, 2) if cc % 2 == 0 else (3, 4, 5)
            gpi = cc % 2
            sl = slice(128 * cc, 128 * (cc + 1))
            t_qb0, t_qb1, t_kvb, t_o0, t_o1 = kv["t_qb"]
            t_beta = kv[f"t_beta{cc}"]
            t_g = kv[f"t_g{cc}"]
            d0 = bk.acquire(BA)
            t_kk = mm(lambda e: e.matmul(PF[BA][:, 0:128], kvb[0:64, sl], kvb[0:64, sl],
                                         start=True, stop=True),
                      deps=[t_kvb] + d0, inc=True)
            t_kkb = sca(lambda e: e.activation(b["KKb"][:], PF[BA][:, 0:128], ACT.Copy,
                                               scale=beta_c[:, cc:cc + 1]),
                        deps=[t_kk, t_beta])
            bk.read(BA, t_kkb)
            yield
            d1 = bk.acquire(BB)
            mm(lambda e: e.matmul(PF[BB][:, 0:1], triu1_f[:, :], g_c[:, cc:cc + 1],
                                  start=True, stop=True), deps=[t_g] + d1)
            mm(lambda e: e.matmul(PF[BB][:, 64:65], tril0_f[:, :], g_c[:, cc:cc + 1],
                                  start=True, stop=True))
            t_cs = mm(lambda e: e.matmul(PF[BB][0:1, 128:256], g_c[:, cc:cc + 1],
                                         triu1_f[:, :], start=True, stop=True), inc=True)
            t_eb = sca(lambda e: e.activation(b["ebcol"][:], PF[BB][:, 0:1], ACT.Exp),
                       deps=[t_cs])
            t_dl = sca(lambda e: e.activation(b["dlcol"][:], PF[BB][:, 64:65], ACT.Exp),
                       deps=[t_cs])
            t_bc = sca(lambda e: e.copy(b["bcol"][:], PF[BB][:, 0:1]), deps=[t_cs])
            t_nbc = sca(lambda e: e.mul(b["nbcol"][:], PF[BB][:, 0:1], -1.0), deps=[t_cs])
            t_brr = sca(lambda e: e.copy(b["brow"][:], PF[BB][0:1, 128:256]), deps=[t_cs])
            bk.read(BB, t_brr)
            bk.read(BB, t_nbc)
            yield
            t_ebl = vec(lambda e: e.tensor_mul(b["eblcol"][:], b["ebcol"][:], b["dlcol"][:]),
                        deps=[t_eb, t_dl])
            t_bb = vec(lambda e: e.tensor_mul(b["bebe"][:], beta_c[:, cc:cc + 1],
                                              b["ebcol"][:]), deps=[t_eb, t_beta])
            d2 = bk.acquire(BC)
            t_df = mm(lambda e: e.matmul(PF[BC][:, 0:128], ones_f[:, :], b["brow"][:, :],
                                         start=True, stop=True),
                      deps=[t_brr] + d2, inc=True)
            t_dts = sca(lambda e: e.activation(b["dect"][:], PF[BC][:, 0:128], ACT.Exp,
                                               scale=-1.0, bias=b["bcol"][:, :]),
                        deps=[t_df, t_bc])
            t_dst = sca(lambda e: e.activation(b["dmst"][:], PF[BC][:, 0:128], ACT.Exp,
                                               scale=1.0, bias=b["nbcol"][:, :]),
                        deps=[t_df, t_nbc])
            bk.read(BC, t_dts)
            bk.read(BC, t_dst)
            yield
            t_dm = vec(lambda e: e.tensor_mul(b["dmst"][:], b["dmst"][:], triu1_f[:]),
                       deps=[t_dst])
            t_m1 = vec(lambda e: e.tensor_mul(b["tmpf"][:], b["dect"][:], b["KKb"][:]),
                       deps=[t_dts, t_kkb])
            t_m2 = vec(lambda e: e.tensor_mul(b["M_b"][:], b["tmpf"][:], tril0_f[:]),
                       deps=[t_m1])
            yield
            dg0 = bk.acquire(6 + gpi)
            t_tm = mm(lambda e: e.transpose(PG[gpi][:, 0:128], b["M_b"][:, :], idb[:, :]),
                      deps=[t_m2] + dg0, inc=True)
            t_b0 = vec(lambda e: e.tensor_copy(b["B0_b"][:], PG[gpi][:, 0:128]),
                       deps=[t_tm])
            bk.read(6 + gpi, t_b0)
            t_Y = vec(lambda e: e.tensor_sub(b["Y_b"][:], idb[:], b["B0_b"][:]),
                      deps=[t_b0])
            yield
            dg1 = bk.acquire(6 + gpi)
            t_tk = mm(lambda e: e.transpose(PG[gpi][:, 0:128], kvb[:, sl], idb[:, :]),
                      deps=[t_kvb] + dg1, inc=True)
            t_r1 = vec(lambda e: e.tensor_scalar_mul(b["RHS2"][:, 0:64], PG[gpi][:, 0:64],
                                                     b["bebe"][:, :]), deps=[t_tk, t_bb])
            t_r2 = vec(lambda e: e.tensor_scalar_mul(b["RHS2"][:, 64:128],
                                                     PG[gpi][:, 64:128],
                                                     beta_c[:, cc:cc + 1]),
                       deps=[t_tk, t_beta])
            t_kh = vec(lambda e: e.tensor_scalar_mul(b["khat"][:], PG[gpi][:, 0:64],
                                                     b["dlcol"][:, :]), deps=[t_tk, t_dl])
            bk.read(6 + gpi, t_r1)
            bk.read(6 + gpi, t_r2)
            bk.read(6 + gpi, t_kh)
            yield
            Acur, Bcur = b["M_b"][:, 0:128], b["B0_b"][:, 0:128]
            t_A = [t_m2, t_b0]
            for kk in range(1, 7):
                ABn = (b["AB0"], b["AB1"])[kk % 2]
                d0 = bk.acquire(BA)
                t_am = mm(lambda e, Bcur=Bcur, Acur=Acur: e.matmul(
                    PF[BA][:, 0:128], Bcur, Acur, start=True, stop=True),
                    deps=t_A + d0, inc=(kk == 6))
                if kk < 6:
                    t_am = mm(lambda e, Acur=Acur, Bcur=Bcur: e.matmul(
                        PF[BA][:, 128:256], Acur, Bcur, start=True, stop=True),
                        inc=True)
                    t_An = vec(lambda e, ABn=ABn: e.tensor_copy(ABn[:, 0:256],
                                                               PF[BA][:, 0:256]),
                               deps=[t_am])
                else:
                    t_An = vec(lambda e, ABn=ABn: e.tensor_copy(ABn[:, 0:128],
                                                               PF[BA][:, 0:128]),
                               deps=[t_am])
                bk.read(BA, t_An)
                yield
                d2 = bk.acquire(BC)
                t_ym = mm(lambda e, ABn=ABn: e.matmul(
                    PF[BC][:, 0:128], ABn[:, 0:128], b["Y_b"][:, :],
                    start=True, stop=True),
                    deps=[t_An, t_Y] + d2, inc=True)
                t_Y = vec(lambda e: e.tensor_add(b["Y_b"][:], b["Y_b"][:],
                                                 PF[BC][:, 0:128]),
                          deps=[t_ym])
                bk.read(BC, t_Y)
                Acur, Bcur, t_A = ABn[:, 0:128], ABn[:, 128:256], [t_An]
                yield
            d0 = bk.acquire(BA)
            t_u2m = mm(lambda e: e.matmul(PF[BA][:, 0:128], b["Y_b"][:, :], b["RHS2"][:, :],
                                          start=True, stop=True),
                       deps=[t_Y, t_r1, t_r2] + d0, inc=True)
            t_u2f = vec(lambda e: e.tensor_copy(b["U2f"][:], PF[BA][:, 0:128]),
                        deps=[t_u2m])
            bk.read(BA, t_u2f)
            t_u2fin = t_u2f
            yield
            d1 = bk.acquire(BB)
            t_u2t = mm(lambda e: e.matmul(PF[BB][:, 0:128], b["RHS2"][:, :], b["Y_b"][:, :],
                                          start=True, stop=True), deps=d1, inc=True)
            t_wt = vec(lambda e: e.tensor_copy(b["WtU"][0:64, :], PF[BB][0:64, 0:128]),
                       deps=[t_u2t])
            bk.read(BB, t_wt)
            yield
            qsrc = ((qb0, t_qb0), (qodd0, t_o0), (qb1, t_qb1), (qodd1, t_o1))
            for j in range(4):
                qt, t_qs = qsrc[j]
                d0 = bk.acquire(BC)
                t_kq = mm(lambda e, qt=qt: e.matmul(
                    PF[BC][:, 0:128], kvb[0:64, sl], qt[0:64, sl],
                    start=True, stop=True), deps=[t_qs, t_kvb] + d0, inc=True)
                t_am = vec(lambda e, j=j: e.tensor_mul(
                    b["amsk"][:, 128 * j:128 * (j + 1)], PF[BC][:, 0:128], b["dmst"][:]),
                    deps=[t_kq, t_dm])
                bk.read(BC, t_am)
                kv[f"t_amsk{cc}_{j}"] = t_am
                if j % 2 == 1:
                    yield
            t_grw = vec(lambda e, cc=cc: e.tensor_mul(
                b["grw"][:], gate_tm[:, 256 * cc:256 * (cc + 1)], rmsw_rep[:]),
                deps=[kv["t_rmsw"], kv[f"t_beta{cc}"]])
            kv[f"t_ind{cc}"] = dict(t_wt=t_wt, t_u2fin=t_u2fin, t_kh=t_kh,
                                    t_eb=t_eb, t_ebl=t_ebl, t_grw=t_grw,
                                    t_dm=t_dm)

        def gen_chain(i, cc):
            mm, vec, sca = ctx.mm, ctx.vec, ctx.sca
            tb = kv["tb"]
            qb0, qb1, kvb = tb["qb0"], tb["qb1"], tb["kvb"]
            qodd0, qodd1 = tb["qodd0"], tb["qodd1"]
            gate_tm, beta_c, g_c = tb["gate_tm"], tb["beta_c"], tb["g_c"]
            ogfm = tb["ogfm"]
            b = CB[cc]
            BA, BB = (3, 4) if cc % 2 == 0 else (0, 1)
            gpi = (cc + 1) % 2
            ind = kv[f"t_ind{cc}"]
            sl = slice(128 * cc, 128 * (cc + 1))
            t_qb0, t_qb1, t_kvb, t_o0, t_o1 = kv["t_qb"]
            d0 = bk.acquire(BA)
            t_ws = mm(lambda e: e.matmul(PF[BA][:, 0:64], b["WtU"][0:64, :],
                                         S_b[0:64, 0:64], start=True, stop=True),
                      deps=[ind["t_wt"], kv["t_Sb"]] + d0, inc=True)
            t_ub = vec(lambda e: e.scalar_tensor_tensor(
                b["U_b"][:], PF[BA][:, 0:64], -1.0, b["U2f"][:, 64:128],
                op0=ALU.mult, op1=ALU.add), deps=[t_ws, ind["t_u2fin"]])
            bk.read(BA, t_ub)
            yield
            qsrc = ((qb0, t_qb0), (qodd0, t_o0), (qb1, t_qb1), (qodd1, t_o1))
            t_of_prev = None
            for j in range(4):
                qt, t_qs = qsrc[j]
                d4 = bk.acquire(BB)
                t_o1m = mm(lambda e, j=j: e.matmul(
                    PF[BB][:, 0:64], b["amsk"][:, 128 * j:128 * (j + 1)], b["U_b"][:, :],
                    start=True, stop=True),
                    deps=[kv[f"t_amsk{cc}_{j}"], t_ub] + d4, inc=True)
                t_o2m = mm(lambda e, qt=qt: e.matmul(
                    PF[BB][:, 64:128], qt[0:64, sl], S_b[0:64, 0:64],
                    start=True, stop=True), deps=[kv["t_Sb"]], inc=True)
                t_ebs = sca(lambda e: e.activation(b["ebtmp"][:], PF[BB][:, 64:128],
                                                   ACT.Copy, scale=b["ebcol"][:, :]),
                            deps=[t_o2m, ind["t_eb"], t_of_prev])
                t_of = vec(lambda e, j=j: e.tensor_add(
                    b["O_f"][:, 64 * j:64 * (j + 1)], PF[BB][:, 0:64], b["ebtmp"][:]),
                    deps=[t_o1m, t_ebs])
                bk.read(BB, t_ebs)
                bk.read(BB, t_of)
                kv[f"t_of{j}"] = t_of
                t_of_prev = t_of
                yield
            d0 = bk.acquire(BA)
            t_ku = mm(lambda e: e.matmul(PF[BA][0:64, 0:64], b["khat"][:, 0:64],
                                         b["U_b"][:, :], start=True, stop=True),
                      deps=[ind["t_kh"], t_ub] + d0, inc=True)
            t_Sf = vec(lambda e: e.scalar_tensor_tensor(
                S_f[:], S_f[:], b["eblcol"][0:64, :], PF[BA][0:64, 0:64],
                op0=ALU.mult, op1=ALU.add), deps=[t_ku, ind["t_ebl"], kv["t_Sf"]])
            bk.read(BA, t_Sf)
            t_Sb = vec(lambda e: e.tensor_copy(S_b[:], S_f[:]), deps=[t_Sf])
            kv["t_Sf"] = t_Sf
            kv["t_Sb"] = t_Sb
            yield
            t_osq = sca(lambda e: e.activation(b["Osq"][:], b["O_f"][:], ACT.Square),
                        deps=[kv["t_of0"], kv["t_of1"], kv["t_of2"], kv["t_of3"]])
            for j in range(4):
                vec(lambda e, j=j: e.reduce_sum(b["rms_in"][:, j:j + 1],
                                                b["Osq"][:, 64 * j:64 * (j + 1)], AX.X),
                    deps=[t_osq], inc=(j == 3))
            t_red = ctx.tok("V")
            t_rl = sca(lambda e: e.activation(b["rms4"][:], b["rms_in"][:], ACT.Ln,
                                              scale=1.0 / 64, bias=1e-5), deps=[t_red])
            t_rr = sca(lambda e: e.activation(b["rms4"][:], b["rms4"][:], ACT.Exp,
                                              scale=-0.5), deps=[t_rl])
            yield
            for j in range(4):
                vec(lambda e, j=j: e.scalar_tensor_tensor(
                    b["og_bf"][:, 64 * j:64 * (j + 1)], b["O_f"][:, 64 * j:64 * (j + 1)],
                    b["rms4"][:, j:j + 1], b["grw"][:, 64 * j:64 * (j + 1)],
                    op0=ALU.mult, op1=ALU.mult),
                    deps=[t_rr, ind["t_grw"], kv[f"t_of{j}"]], inc=(j == 3))
            t_og = ctx.tok("V")
            yield
            for half in range(2):
                dg = bk.acquire(6 + gpi)
                t_t = mm(lambda e, half=half: e.transpose(
                    PG[gpi][:, 0:128], b["og_bf"][:, 128 * half:128 * (half + 1)],
                    idb[:, :]), deps=[t_og] + dg, inc=True)
                t_e = vec(lambda e, half=half, cc=cc: e.tensor_copy(
                    ogfm[:, 512 * half + 128 * cc:512 * half + 128 * (cc + 1)],
                    PG[gpi][:, 0:128]), deps=[t_t])
                bk.read(6 + gpi, t_e)
                kv["og_done"].append(t_e)
                yield

        def phase15():
            mm, vec, sca, dma = ctx.mm, ctx.vec, ctx.sca, ctx.dma
            for ii in range(TOWN // TW):
                par = ii % 2
                ctx.guard()
                war = kv["xq_war"][par]
                kv["xq_war"][par] = []
                for r in range(8):
                    dma("sync", lambda e, r=r, ii=ii, par=par: e.dma_start(
                        out=xq[par][:, TW * r:TW * (r + 1)],
                        in_=x_own.ap()[128 * r:128 * (r + 1), TW * ii:TW * (ii + 1)]),
                        deps=war)
                t_xl = ctx.tok("DQS")
                t_cast, t_m, t_r = emit_ln(par, t_xl)
                t_mr = vec(lambda e: e.tensor_mul(mr_row[:], m_row[:], rstd_row[:]),
                           deps=[t_r])
                d5 = bk.acquire(5)
                t_rr = mm(lambda e: e.matmul(PF[5][:, 0:TW], ones_f[:, :],
                                             rstd_row[:, :], start=True, stop=True),
                          deps=[t_r] + d5, inc=True)
                t_rrep = vec(lambda e: e.tensor_copy(rrep[:], PF[5][:, 0:TW]),
                             deps=[t_rr])
                bk.read(5, t_rrep)
                d5 = bk.acquire(5)
                t_mm2 = mm(lambda e: e.matmul(PF[5][:, 0:TW], ones_f[:, :],
                                              mr_row[:, :], start=True, stop=True),
                           deps=[t_mr] + d5, inc=True)
                t_mrep = vec(lambda e: e.tensor_copy(mrrep[:], PF[5][:, 0:TW]),
                             deps=[t_mm2])
                bk.read(5, t_mrep)
                for r in range(8):
                    t_mx = vec(lambda e, r=r, par=par: e.tensor_mul(
                        qc0[:], xq[par][:, TW * r:TW * (r + 1)], rrep[:]),
                        deps=[t_rrep])
                    t_x = vec(lambda e, r=r, ii=ii: e.tensor_sub(
                        xn_own[:, TOWN * r + TW * ii:TOWN * r + TW * (ii + 1)],
                        qc0[:], mrrep[:]), deps=[t_mrep, t_mx])
                    kv["xq_war"][par].append(t_x)
            kv["t_xn"] = ctx.tok("V")

        def phase2():
            if use_cc:
                kv["t_cc"] = ctx.op("gpsimd", lambda e: e.collective_compute(
                    "ReduceScatter", ALU.add,
                    replica_groups=[[0, 1, 2, 3], [4, 5, 6, 7]],
                    ins=[py_in.ap().opt()], outs=[py_out.ap().opt()]),
                    deps=kv["py_done"], inc="CC", inc_n=1)
            else:
                kv["t_cc"] = ctx.op("gpsimd", lambda e: e.dma_start(
                    out=py_out.ap()[:, :], in_=py_in.ap()[0, :, :]),
                    deps=kv["py_done"], inc="CC", inc_n=1)

        def tap(name, src_ap_fn, deps, dst=None):
            if not taps or name not in tap_d:
                return None
            return ctx.dma("sync", lambda e: e.dma_start(
                out=tap_d[name].ap() if dst is None else dst(),
                in_=src_ap_fn()), deps=deps)

        def emit_ffn_pass(p):
            mm, vec, sca, dma = ctx.mm, ctx.vec, ctx.sca, ctx.dma
            war = kv.setdefault("fw_war", [[], []])
            scr_war = kv.setdefault("fscr_war", [None, None, None, None])
            l1war = kv.setdefault("l1_war", [[], [], [], []])
            for ob in range(32):
                slot = ob % 2
                co = 1024 * slot
                wt = [None, None]
                for wi, wdr in enumerate((w1t, w2t)):
                    wt[wi] = dma("gpsimd", lambda e, wdr=wdr, ob=ob, wi=wi, co=co:
                                 e.dma_start(
                                     out=Wst[wi][:, co:co + 1024].rearrange(
                                         "p (s c) -> p s c", s=8),
                                     in_=wdr.ap()[:, 128 * ob:128 * (ob + 1)].rearrange(
                                         "(s p) c -> p s c", p=128)),
                                 deps=l1war[2 * wi + slot] + [kv["t_cc"]])
                    l1war[2 * wi + slot] = []
                nb = 2 * NTG
                dd = []
                for b in range(nb):
                    dd += bk.acquire(b)
                for wi in range(2):
                    for k in range(8):
                        for tg in range(NTG):
                            b = wi * NTG + tg
                            last = (k == 7 and wi == 1 and tg == NTG - 1)
                            mm(lambda e, k=k, wi=wi, tg=tg, b=b, p=p, co=co: e.matmul(
                                PF[b][:, 0:PTW],
                                Wst[wi][:, co + 128 * k:co + 128 * (k + 1)],
                                xn_own[:, TOWN * k + PT * p + PTW * tg:
                                       TOWN * k + PT * p + PTW * (tg + 1)],
                                start=(k == 0), stop=(k == 7)),
                                deps=(dd + [wt[0], wt[1], kv["t_xn"]]) if (k == 0 and wi == 0 and tg == 0) else (),
                                inc=last)
                t_l1 = ctx.tok("T")
                l1war[2 * 0 + slot].append(t_l1)
                l1war[2 * 1 + slot].append(t_l1)
                war[0].append(t_l1) if ob == 31 else None
                war[1].append(t_l1) if ob == 31 else None
                for tg in range(NTG):
                    t_e1 = sca(lambda e, ob=ob, tg=tg: e.activation(
                        ft1[:], PF[tg][:, 0:PTW], ACT.Silu,
                        scale=ffn_scales[:, ob:ob + 1]),
                        deps=[t_l1, kv["t_scales"], scr_war[0]])
                    t_e2 = sca(lambda e, ob=ob, tg=tg: e.activation(
                        ft2[:], PF[NTG + tg][:, 0:PTW], ACT.Copy,
                        scale=ffn_scales[:, 32 + ob:33 + ob]),
                        deps=[scr_war[1]])
                    t_h = vec(lambda e, ob=ob, tg=tg: e.tensor_mul(
                        ffnA[:, PT * ob + PTW * tg:PT * ob + PTW * (tg + 1)],
                        ft1[:], ft2[:]), deps=[t_e1, t_e2])
                    bk.read(tg, t_e1)
                    bk.read(NTG + tg, t_e2)
                    scr_war[0] = t_h
                    scr_war[1] = t_h
            t_src = ctx.tok("V")
            kv["tapA1"] = tap("A1", lambda: ffnA[:, :], [t_src]) if p == 0 else None
            xs, osg = fxs, fos
            for lay, (wdr, src, dst, scoff, nob) in enumerate(
                    ((h1t, ffnA, ffnB, 64, 32), (h2t, ffnB, ffnA, 96, 32),
                     (w3t, ffnA, None, 128, 8))):
                nxt = None
                for ob in range(nob):
                    par = ob % 2
                    t_wl = dma("gpsimd", lambda e, wdr=wdr, ob=ob, par=par: e.dma_start(
                        out=Wst[par][:, :].rearrange("p (s c) -> p s c", s=32),
                        in_=wdr.ap()[:, 128 * ob:128 * (ob + 1)].rearrange(
                            "(s p) c -> p s c", p=128)), deps=war[par] + [kv["t_cc"]])
                    war[par] = []
                    if lay == 2:
                        t_xl = dma("sync", lambda e, ob=ob, p=p: e.dma_start(
                            out=xs[:, :], in_=x_own.ap()[128 * ob:128 * (ob + 1),
                                                         PT * p:PT * (p + 1)]),
                            deps=[kv.get("t_l4adds")])
                        t_yl0 = dma("sync", lambda e, ob=ob, p=p: e.dma_start(
                            out=fys_b[:, :], in_=py_out.ap()[128 * ob:128 * (ob + 1),
                                                             PT * p:PT * (p + 1)]),
                            deps=[kv["t_cc"], kv.get("t_fysc")])
                        t_yl = vec(lambda e: e.tensor_copy(fys[:, :], fys_b[:, :]),
                                   deps=[t_yl0, kv.get("t_l4adds")])
                        kv["t_fysc"] = t_yl
                    dd = []
                    for tg in range(NTG):
                        dd += bk.acquire(tg)
                    t_srcdep = t_src if lay != 2 else kv["l3_done"]
                    for k in range(32):
                        for tg in range(NTG):
                            last = (k == 31 and tg == NTG - 1)
                            mm(lambda e, k=k, tg=tg, par=par, src=src: e.matmul(
                                PF[tg][:, 0:PTW], Wst[par][:, 128 * k:128 * (k + 1)],
                                src[:, PT * k + PTW * tg:PT * k + PTW * (tg + 1)],
                                start=(k == 0), stop=(k == 31)),
                                deps=(dd + [t_wl, t_srcdep]) if (k == 0 and tg == 0) else (),
                                inc=last)
                    t_mmk = ctx.tok("T")
                    war[par].append(t_mmk)
                    for tg in range(NTG):
                        if lay != 2:
                            scr = (ft1, ft2)[tg % 2]
                            t_gx = sca(lambda e, ob=ob, tg=tg, scoff=scoff, scr=scr:
                                       e.activation(scr[:], PF[tg][:, 0:PTW], ACT.Gelu,
                                                    scale=ffn_scales[:, scoff + ob:scoff + ob + 1]),
                                       deps=[t_mmk, scr_war[2 + tg % 2]])
                            t_cx = vec(lambda e, ob=ob, tg=tg, dst=dst, scr=scr: e.tensor_copy(
                                dst[:, PT * ob + PTW * tg:PT * ob + PTW * (tg + 1)], scr[:]),
                                deps=[t_gx, kv["tapA1"] if lay == 1 else None])
                            bk.read(tg, t_gx)
                            scr_war[2 + tg % 2] = t_cx
                            nxt = t_cx
                        else:
                            t_y = sca(lambda e, ob=ob, tg=tg: e.activation(
                                osg[:, PTW * tg:PTW * (tg + 1)], PF[tg][:, 0:PTW], ACT.Copy,
                                scale=ffn_scales[:, 128 + ob:129 + ob]),
                                deps=[t_mmk, kv.get("t_l4dma")])
                            bk.read(tg, t_y)
                            t_a1 = vec(lambda e, tg=tg: e.tensor_add(
                                osg[:, PTW * tg:PTW * (tg + 1)], osg[:, PTW * tg:PTW * (tg + 1)],
                                xs[:, PTW * tg:PTW * (tg + 1)]), deps=[t_y, t_xl])
                            t_a2 = vec(lambda e, tg=tg: e.tensor_add(
                                osg[:, PTW * tg:PTW * (tg + 1)], osg[:, PTW * tg:PTW * (tg + 1)],
                                fys[:, PTW * tg:PTW * (tg + 1)]), deps=[t_a1, t_yl])
                    if lay == 2:
                        kv["t_l4adds"] = ctx.tok("V")
                        kv["t_l4dma"] = dma("scalar", lambda e, ob=ob, p=p: e.dma_start(
                            out=out_fm.ap()[128 * ob:128 * (ob + 1), PT * p:PT * (p + 1)],
                            in_=osg[:, :]), deps=[kv["t_l4adds"]])
                if lay == 0:
                    if p == 0:
                        tap("B2", lambda: ffnB[:, :], [nxt])
                    t_src = nxt
                elif lay == 1:
                    if p == 0:
                        tap("A3", lambda: ffnA[:, :], [nxt])
                    kv["l3_done"] = nxt
            kv["out_done"] = ctx.tok("DQA")

        def emit_all():
            setup()
            if stop_after == "setup":
                pass
            else:
                phase15()
                war = kv["xq_war"][0]
                kv["xq_war"][0] = []
                for r in range(8):
                    ctx.dma("sync", lambda e, r=r: e.dma_start(
                        out=xq[0][:, TW * r:TW * (r + 1)],
                        in_=x_fm.ap()[128 * r:128 * (r + 1), 0:TW]), deps=war)
                kv["t_x0"] = ctx.tok("DQS")
                n_tiles = 1 if stop_after == "tile0" else NT
                for i in range(n_tiles):
                    emit_tile(i)
                if stop_after not in ("tile0", "p1"):
                    tap("pyin", lambda: py_in.ap(), kv["py_done"])
                    tap("xn", lambda: xn_own[:, :], [kv["t_xn"]])
                    if stop_after != "p15":
                        if stop_after != "baronly":
                            phase2()
                        else:
                            kv["t_cc"] = kv["t_consts"]
                        tap("py", lambda: py_out.ap(), [kv["t_cc"]])
                        if stop_after != "cc":
                            ctx.barrier()
                            if stop_after not in ("bar", "baronly"):
                                emit_ffn_pass(0)
                                if stop_after != "ffn0":
                                    ctx.guard(engs=("tensor", "vector", "scalar",
                                                    "sync", "gpsimd"),
                                              sems=("T", "V", "S", "DQS", "DQA", "DQG"))
                                    emit_ffn_pass(1)
            if stop_after is not None:
                kv["out_done"] = ctx.dma("scalar", lambda e: e.dma_start(
                    out=out_fm.ap()[0:128, 0:TW], in_=pystage[:, 0:TW]),
                    deps=kv.get("py_done", []))
            for eng in ENGS:
                ctx.op(eng, lambda e: None, [kv["out_done"]], None)

        with nc.Block() as block:
            for engname in ENGS:
                def run_pass(engobj, engname=engname):
                    ctx.begin(engname, engobj)
                    bk.war = {}
                    kv.clear()
                    emit_all()
                getattr(block, engname)(run_pass)
    return nc


def _core_inputs(inp, core):
    b, g = core // 4, core % 4
    x = np.asarray(inp["L0_attn_x"], np.float32)
    T = x.shape[1]
    TOWN = T // 4
    d = {}
    xt = np.ascontiguousarray(x[b].T)
    d["x_fm"] = xt
    d["x_own"] = np.ascontiguousarray(xt[:, TOWN * g:TOWN * (g + 1)])
    d["wq_t"] = np.ascontiguousarray(
        np.asarray(inp["L0_attn_Wq"], np.float32).T[:, 256 * g:256 * (g + 1)])
    wk = np.asarray(inp["L0_attn_Wk"], np.float32).T[:, 64 * g:64 * (g + 1)]
    wv = np.asarray(inp["L0_attn_Wv"], np.float32).T[:, 64 * g:64 * (g + 1)]
    d["wkv_t"] = np.ascontiguousarray(np.concatenate([wk, wv], axis=1))
    wg = np.asarray(inp["L0_attn_Wgate"], np.float32).T[:, 256 * g:256 * (g + 1)]
    wb = np.asarray(inp["L0_attn_Wbeta"], np.float32).T[:, g:g + 1]
    wgk = np.asarray(inp["L0_attn_Wgk"], np.float32).T[:, g:g + 1]
    d["wgbg_t"] = np.ascontiguousarray(np.concatenate([wg, wb, wgk], axis=1))
    d["cwq"] = np.ascontiguousarray(
        np.asarray(inp["L0_attn_cwq"], np.float32)[256 * g:256 * (g + 1)])
    cwk = np.asarray(inp["L0_attn_cwk"], np.float32)[64 * g:64 * (g + 1)]
    cwv = np.asarray(inp["L0_attn_cwv"], np.float32)[64 * g:64 * (g + 1)]
    d["cwkv"] = np.ascontiguousarray(np.concatenate([cwk, cwv], axis=0))
    d["scal"] = np.array([[inp["L0_attn_b_beta"][g], inp["L0_attn_A_log"][g],
                           inp["L0_attn_dt_bias"][g]]], np.float32)
    d["rmsw_row"] = np.ascontiguousarray(
        np.tile(np.asarray(inp["L0_attn_rms_w"], np.float32), 4)[None, :])
    d["wpr_t"] = np.ascontiguousarray(
        np.asarray(inp["L0_attn_Wproj"], np.float32).T[256 * g:256 * (g + 1), :])
    return d


def _shared_inputs(inp):
    d = {}
    d["w1t"] = np.ascontiguousarray(np.asarray(inp["L0_ffn_w1_w"], np.float32).T)
    d["w2t"] = np.ascontiguousarray(np.asarray(inp["L0_ffn_w2_w"], np.float32).T)
    d["h1t"] = np.ascontiguousarray(np.asarray(inp["L0_ffn_h1_w"], np.float32).T)
    d["h2t"] = np.ascontiguousarray(np.asarray(inp["L0_ffn_h2_w"], np.float32).T)
    d["w3t"] = np.ascontiguousarray(np.asarray(inp["L0_ffn_w3_w"], np.float32).T)
    d["sc1"] = np.ascontiguousarray(
        np.asarray(inp["L0_ffn_w1_s"], np.float32).reshape(32, 128).T)
    d["sc2"] = np.ascontiguousarray(
        np.asarray(inp["L0_ffn_w2_s"], np.float32).reshape(32, 128).T)
    d["sch1"] = np.ascontiguousarray(
        np.asarray(inp["L0_ffn_h1_s"], np.float32).reshape(32, 128).T)
    d["sch2"] = np.ascontiguousarray(
        np.asarray(inp["L0_ffn_h2_s"], np.float32).reshape(32, 128).T)
    d["sc3"] = np.ascontiguousarray(
        np.asarray(inp["L0_ffn_w3_s"], np.float32).reshape(8, 128).T)
    d["idn_b"] = np.eye(128).astype(BF)
    d["triu1"] = np.triu(np.ones((128, 128), np.float32))
    d["tril0"] = np.tril(np.ones((128, 128), np.float32), -1)
    csq = np.zeros((2, 128), np.float32)
    csq[0, 0:64] = 0.125
    csq[1, 64:128] = 0.125
    d["csq"] = csq
    cskv = np.zeros((2, 128), np.float32)
    cskv[0, 0:64] = 1.0
    cskv[1, 64:128] = 1.0
    d["cskv"] = cskv
    d["onesrow"] = np.ones((1, TW), np.float32)
    return d


def kernel(**inputs):
    inp = {k: np.asarray(v) for k, v in inputs.items()}
    T = inp["L0_attn_x"].shape[1]
    TOWN = T // 4
    nc = build_program(T=T)
    shared = _shared_inputs(inp)
    in_maps = []
    for core in range(8):
        d = _core_inputs(inp, core)
        d.update(shared)
        in_maps.append(d)
    res = run_bass_kernel_spmd(nc, in_maps, core_ids=list(range(8)))
    out = np.empty((2, T, 1024), np.float32)
    for core in range(8):
        b, g = core // 4, core % 4
        out[b, TOWN * g:TOWN * (g + 1), :] = res.results[core]["out_fm"].T
    return out


# version 14
# speedup vs baseline: 1.0093x; 1.0093x over previous
import contextlib

import numpy as np
import ml_dtypes

import concourse.bass as bass
import concourse.mybir as mybir
from concourse.bass_utils import run_bass_kernel_spmd

F32 = mybir.dt.float32
BF16 = mybir.dt.bfloat16
ALU = mybir.AluOpType
ACT = mybir.ActivationFunctionType
AX = mybir.AxisListType
BF = ml_dtypes.bfloat16

C = 1024
TW = 512

ENGS = ("sync", "tensor", "vector", "scalar", "gpsimd")
OWN_SEM = {"tensor": "T", "vector": "V", "scalar": "S", "gpsimd": "G", "sync": "Y"}


class Ctx:
    def __init__(self, nc, sems):
        self.nc = nc
        self.sems = sems
        self.cur = None
        self.e = None
        self.counts = {}
        self.waited = {}
        self.base_dep = None

    def begin(self, name, engobj):
        self.cur = name
        self.e = engobj
        self.counts = {k: 0 for k in self.sems}
        self.waited = {k: -1 for k in self.sems}
        self.base_dep = None

    def op(self, eng, fn, deps=(), inc=None, inc_n=1):
        tok = None
        if inc is not None:
            self.counts[inc] += inc_n
            tok = (inc, self.counts[inc])
        if self.cur == eng:
            own = OWN_SEM.get(eng)
            deps = list(deps)
            if self.base_dep is not None:
                deps.append(self.base_dep)
            nw = 0
            for dep in deps:
                if dep is None:
                    continue
                s, v = dep
                if v > self.waited[s]:
                    if nw and nw % 2 == 0:
                        self.e.nop(nofuse=True)
                    self.e.wait_ge(self.sems[s], v)
                    self.waited[s] = v
                    nw += 1
            inst = fn(self.e)
            if inc is not None and inst is not None:
                inst.then_inc(self.sems[inc], inc_n)
        return tok

    def mm(self, fn, deps=(), inc=False):
        return self.op("tensor", fn, deps, "T" if inc else None)

    def vec(self, fn, deps=(), inc=True):
        return self.op("vector", fn, deps, "V" if inc else None)

    def sca(self, fn, deps=(), inc=True):
        return self.op("scalar", fn, deps, "S" if inc else None)

    def dma(self, queue, fn, deps=()):
        q = {"sync": "DQS", "scalar": "DQA", "gpsimd": "DQG"}[queue]
        return self.op(queue, fn, deps, q, 16)

    def tok(self, sem):
        return (sem, self.counts[sem])

    def guard(self, engs=("tensor", "vector", "scalar", "sync"),
              sems=("T", "V", "S", "DQS", "DQA")):
        snap = dict(self.counts)
        for eng in engs:
            deps = [(s, snap[s]) for s in sems if s != OWN_SEM.get(eng)]
            self.op(eng, lambda e: None, deps, None)

    def barrier(self):
        self.guard(engs=ENGS, sems=("T", "V", "S", "DQS", "DQA", "DQG"))


class Banks:
    def __init__(self):
        self.war = {}

    def acquire(self, b):
        d = self.war.get(b, [])
        self.war[b] = []
        return d

    def read(self, b, tok):
        if tok is not None:
            self.war.setdefault(b, []).append(tok)


SCHED = "full"


def build_program(T=8192, use_cc=True, stop_after=None, taps=False):
    nc = bass.Bass("TRN2", target_bir_lowering=False,
                   detect_race_conditions=False)
    NT = T // TW
    TOWN = T // 4
    PT = TOWN // 2
    PTW = min(512, PT)
    NTG = PT // PTW

    x_fm = nc.dram_tensor("x_fm", [C, T], F32, kind="ExternalInput")
    x_own = nc.dram_tensor("x_own", [C, TOWN], F32, kind="ExternalInput")
    wq_t = nc.dram_tensor("wq_t", [C, 256], F32, kind="ExternalInput")
    wkv_t = nc.dram_tensor("wkv_t", [C, 128], F32, kind="ExternalInput")
    wgbg_t = nc.dram_tensor("wgbg_t", [C, 258], F32, kind="ExternalInput")
    cwq_d = nc.dram_tensor("cwq", [256, 4], F32, kind="ExternalInput")
    cwkv_d = nc.dram_tensor("cwkv", [128, 4], F32, kind="ExternalInput")
    scal_d = nc.dram_tensor("scal", [1, 3], F32, kind="ExternalInput")
    rmsw_d = nc.dram_tensor("rmsw_row", [1, 256], F32, kind="ExternalInput")
    wpr_t = nc.dram_tensor("wpr_t", [256, C], F32, kind="ExternalInput")
    w1t = nc.dram_tensor("w1t", [C, 4096], F32, kind="ExternalInput")
    w2t = nc.dram_tensor("w2t", [C, 4096], F32, kind="ExternalInput")
    h1t = nc.dram_tensor("h1t", [4096, 4096], F32, kind="ExternalInput")
    h2t = nc.dram_tensor("h2t", [4096, 4096], F32, kind="ExternalInput")
    w3t = nc.dram_tensor("w3t", [4096, C], F32, kind="ExternalInput")
    sc_d = {}
    for nm, w in (("sc1", 32), ("sc2", 32), ("sch1", 32), ("sch2", 32), ("sc3", 8)):
        sc_d[nm] = nc.dram_tensor(nm, [128, w], F32, kind="ExternalInput")
    idn_d = nc.dram_tensor("idn_b", [128, 128], BF16, kind="ExternalInput")
    triu1_d = nc.dram_tensor("triu1", [128, 128], F32, kind="ExternalInput")
    tril0_d = nc.dram_tensor("tril0", [128, 128], F32, kind="ExternalInput")
    csq_d = nc.dram_tensor("csq", [2, 128], F32, kind="ExternalInput")
    cskv_d = nc.dram_tensor("cskv", [2, 128], F32, kind="ExternalInput")
    onesrow_d = nc.dram_tensor("onesrow", [1, TW], F32, kind="ExternalInput")

    out_fm = nc.dram_tensor("out_fm", [C, TOWN], F32, kind="ExternalOutput")
    py_in = nc.dram_tensor("py_in", [4, C, TOWN], BF16, kind="Internal")
    py_out = nc.dram_tensor("py_out", [C, TOWN], BF16, kind="Internal")
    PT_ = TOWN // 2
    tap_d = {}
    if isinstance(taps, (tuple, list, set)):
        shapes = {"A1": ([128, 32 * PT_], BF16), "B2": ([128, 32 * PT_], BF16),
                  "A3": ([128, 32 * PT_], BF16), "xn": ([128, 8 * TOWN], BF16),
                  "py": ([C, TOWN], BF16)}
        for nm in taps:
            tap_d[nm] = nc.dram_tensor("tap_" + nm, list(shapes[nm][0]),
                                       shapes[nm][1], kind="ExternalOutput")
    elif taps == "min":
        for nm in ("A1", "B2", "A3"):
            tap_d[nm] = nc.dram_tensor("tap_" + nm, [128, 32 * PT_], BF16,
                                       kind="ExternalOutput")
        tap_d["xn"] = nc.dram_tensor("tap_xn", [128, 8 * TOWN], BF16,
                                     kind="ExternalOutput")
        tap_d["py"] = nc.dram_tensor("tap_py", [C, TOWN], BF16,
                                     kind="ExternalOutput")
    elif taps:
        tap_d["xn"] = nc.dram_tensor("tap_xn", [128, 8 * TOWN], BF16,
                                     kind="ExternalOutput")
        tap_d["A1"] = nc.dram_tensor("tap_A1", [128, 32 * PT_], BF16,
                                     kind="ExternalOutput")
        tap_d["B2"] = nc.dram_tensor("tap_B2", [128, 32 * PT_], BF16,
                                     kind="ExternalOutput")
        tap_d["A3"] = nc.dram_tensor("tap_A3", [128, 32 * PT_], BF16,
                                     kind="ExternalOutput")
        tap_d["py"] = nc.dram_tensor("tap_py", [C, TOWN], BF16,
                                     kind="ExternalOutput")
        tap_d["pyin"] = nc.dram_tensor("tap_pyin", [4, C, TOWN], BF16,
                                       kind="ExternalOutput")
        for nm, shape, dt_ in (("qb0", [128, TW], BF16), ("kvb", [128, TW], BF16),
                               ("gate", [128, 1024], BF16), ("bg", [128, 8], F32),
                               ("Sf", [64, 64], F32), ("Of", [128, 256], F32),
                               ("ogfm", [128, 1024], BF16), ("xraw", [128, TW], F32),
                               ("rr", [128, TW], F32), ("nrc", [128, 4], F32),
                               ("nbb", [128, 1], F32), ("screp", [128, 3], F32),
                               ("cp0", [128, 8], F32), ("cp1", [128, 8], F32)):
            tap_d[nm] = nc.dram_tensor("tap_" + nm, shape, dt_,
                                       kind="ExternalOutput")

    st = contextlib.ExitStack()
    with st:
        sems = {}
        for s in ("T", "V", "S", "G", "Y", "DQS", "DQA", "DQG", "CC"):
            sems[s] = st.enter_context(nc.semaphore(s))
        ctx = Ctx(nc, sems)
        bk = Banks()

        def sb(stk, name, shape, dtype):
            return stk.enter_context(nc.sbuf_tensor(name, list(shape), dtype))

        PF = [st.enter_context(nc.psum_tensor(f"pf{i}", [128, 512], F32))
              for i in range(6)]
        PG = [st.enter_context(nc.psum_tensor(f"pg{i}", [128, 256], BF16))
              for i in range(2)]

        constpool = sb(st, "constpool", (128, 8), F32)
        CONSTS = {1e-5: 0, 1e-12: 1, 1.0: 2, -1.0: 3, -0.5: 4,
                  1.0 / C: 5, 1.0 / 64: 6, 0.5: 7}
        for v, c in CONSTS.items():
            nc.const_aps.register(F32, v, constpool[:, c:c + 1])
        ffn_scales = sb(st, "ffn_scales", (128, 136), F32)
        xn_own = sb(st, "xn_own", (128, 8 * TOWN), BF16)
        S_f = sb(st, "S_f", (64, 64), F32)
        S_b = sb(st, "S_b", (64, 64), BF16)

        p1 = contextlib.ExitStack()
        idb = sb(p1, "idb", (128, 128), BF16)
        triu1_f = sb(p1, "triu1_f", (128, 128), F32)
        tril0_f = sb(p1, "tril0_f", (128, 128), F32)
        ones_f = sb(p1, "ones_f", (1, 128), F32)
        onescol_b = sb(p1, "onescol_b", (128, 1), BF16)
        blk2_b = sb(p1, "blk2_b", (128, 2), BF16)
        colsel_q = sb(p1, "colsel_q", (2, 128), F32)
        colsel_kv = sb(p1, "colsel_kv", (2, 128), F32)
        kv_rhs = sb(p1, "kv_rhs", (2, TW), F32)
        rmsw_rep = sb(p1, "rmsw_rep", (128, 256), F32)
        srow = sb(p1, "srow", (1, 3), F32)
        screp = sb(p1, "screp", (128, 3), F32)
        negbb_rep = sb(p1, "negbb_rep", (128, 1), F32)
        negexpA = sb(p1, "negexpA", (128, 1), F32)
        wqt_b = sb(p1, "wqt_b", (128, 2048), BF16)
        wkvt_b = sb(p1, "wkvt_b", (128, 1024), BF16)
        wgbg_b = sb(p1, "wgbg_b", (128, 8 * 258), BF16)
        wpr_b = sb(p1, "wpr_b", (128, 2048), BF16)
        cwq0 = sb(p1, "cwq0", (128, 4), F32)
        cwq1 = sb(p1, "cwq1", (128, 4), F32)
        cwkv_s = sb(p1, "cwkv_s", (128, 4), F32)
        wsum_fm = sb(p1, "wsum_fm", (1, 384), BF16)
        wsum_gbg = sb(p1, "wsum_gbg", (1, 258), BF16)
        xq = [sb(p1, f"xq{p}", (128, 8 * TW), F32) for p in range(2)]
        xbf = [sb(p1, f"xbf{p}", (128, 8 * TW), BF16) for p in range(2)]
        xsqb = sb(p1, "xsqb", (128, 8 * TW), BF16)
        m_row = sb(p1, "m_row", (1, TW), F32)
        var_row = sb(p1, "var_row", (1, TW), F32)
        rstd_row = sb(p1, "rstd_row", (1, TW), F32)
        mr_row = sb(p1, "mr_row", (1, TW), F32)
        negm_b = sb(p1, "negm_b", (1, TW), BF16)
        rrep = sb(p1, "rrep", (128, TW), F32)
        mrrep = sb(p1, "mrrep", (128, TW), F32)
        qraw0 = sb(p1, "qraw0", (128, TW + 3), F32)
        qraw1 = sb(p1, "qraw1", (128, TW + 3), F32)
        kvraw = sb(p1, "kvraw", (128, TW + 3), F32)
        qc0 = sb(p1, "qc0", (128, TW), F32)
        qc1 = sb(p1, "qc1", (128, TW), F32)
        kvc = sb(p1, "kvc", (128, TW), F32)
        rsq01 = sb(p1, "rsq01", (2, TW), F32)
        rsq23 = sb(p1, "rsq23", (2, TW), F32)
        TB = []
        for ti in range(2):
            tb = {}
            for nm, shape, dt_ in (
                    ("qb0", (128, TW), BF16), ("qb1", (128, TW), BF16),
                    ("kvb", (128, TW), BF16), ("qodd0", (64, TW), BF16),
                    ("qodd1", (64, TW), BF16), ("gate_tm", (128, 1024), BF16),
                    ("beta_c", (128, 4), F32), ("g_c", (128, 4), F32),
                    ("rcol", (128, 4), F32), ("nrcol", (128, 4), F32),
                    ("ogfm", (128, 1024), BF16)):
                tb[nm] = sb(p1, f"{nm}_t{ti}", shape, dt_)
            TB.append(tb)
        CB = []
        for pi in range(4):
            cb = {}
            for nm, shape, dt_ in (
                    ("KKb", (128, 128), F32), ("dect", (128, 128), F32),
                    ("dmst", (128, 128), F32), ("tmpf", (128, 128), F32),
                    ("M_b", (128, 128), BF16), ("B0_b", (128, 128), BF16),
                    ("Y_b", (128, 128), BF16),
                    ("AB0", (128, 256), BF16), ("AB1", (128, 256), BF16),
                    ("RHS2", (128, 128), BF16), ("khat", (128, 64), BF16),
                    ("U2f", (128, 128), F32), ("amsk", (128, 512), BF16),
                    ("WtU", (128, 128), BF16),
                    ("U_b", (128, 64), BF16), ("O_f", (128, 256), F32),
                    ("Osq", (128, 256), F32), ("rms_in", (128, 4), F32),
                    ("rms4", (128, 4), F32), ("grw", (128, 256), F32),
                    ("og_bf", (128, 256), BF16), ("ebcol", (128, 1), F32),
                    ("dlcol", (128, 1), F32), ("eblcol", (128, 1), F32),
                    ("bcol", (128, 1), F32), ("nbcol", (128, 1), F32),
                    ("bebe", (128, 1), F32), ("brow", (1, 128), F32),
                    ("ebtmp", (128, 256), F32)):
                cb[nm] = sb(p1, f"{nm}{pi}", shape, dt_)
            CB.append(cb)
        pystage = sb(p1, "pystage", (128, 1024), BF16)
        p1.close()

        ffnA = sb(st, "ffnA", (128, 32 * PT), BF16)
        ffnB = sb(st, "ffnB", (128, 32 * PT), BF16)
        Wst = [sb(st, f"fw{p}", (128, 4096), BF16) for p in range(2)]
        ft1 = sb(st, "ffnt1", (128, PTW), F32)
        ft2 = sb(st, "ffnt2", (128, PTW), F32)
        fxs = sb(st, "xstage", (128, PT), F32)
        fys_b = sb(st, "ystage_b", (128, PT), BF16)
        fys = sb(st, "ystage", (128, PT), F32)
        fos = sb(st, "ostage", (128, 2 * PT), F32)

        kv = {}

        def setup():
            dma, mm, vec, sca = ctx.dma, ctx.mm, ctx.vec, ctx.sca
            dma("sync", lambda e: e.dma_start(out=idb[:], in_=idn_d.ap()))
            dma("sync", lambda e: e.dma_start(out=triu1_f[:], in_=triu1_d.ap()))
            dma("sync", lambda e: e.dma_start(out=tril0_f[:], in_=tril0_d.ap()))
            dma("sync", lambda e: e.dma_start(out=srow[:], in_=scal_d.ap()))
            dma("sync", lambda e: e.dma_start(out=cwq0[:], in_=cwq_d.ap()[0:128, :]))
            dma("sync", lambda e: e.dma_start(out=cwq1[:], in_=cwq_d.ap()[128:256, :]))
            dma("sync", lambda e: e.dma_start(out=cwkv_s[:], in_=cwkv_d.ap()))
            dma("sync", lambda e: e.dma_start(out=rmsw_rep[0:1, 0:256], in_=rmsw_d.ap()))
            for j, (nm, c0, w) in enumerate([("sc1", 0, 32), ("sc2", 32, 32),
                                             ("sch1", 64, 32), ("sch2", 96, 32),
                                             ("sc3", 128, 8)]):
                dma("sync", lambda e, nm=nm, c0=c0, w=w: e.dma_start(
                    out=ffn_scales[:, c0:c0 + w], in_=sc_d[nm].ap()))
            dma("sync", lambda e: e.dma_start(out=colsel_q[:], in_=csq_d.ap()))
            dma("sync", lambda e: e.dma_start(out=colsel_kv[:], in_=cskv_d.ap()))
            dma("sync", lambda e: e.dma_start(out=kv_rhs[1:2, 0:TW], in_=onesrow_d.ap()))
            t_cl = ctx.tok("DQS")
            kv["t_scales"] = t_cl
            for r in range(8):
                dma("gpsimd", lambda e, r=r: e.dma_start(
                    out=wqt_b[:, 256 * r:256 * (r + 1)],
                    in_=wq_t.ap()[128 * r:128 * (r + 1), :]))
                dma("gpsimd", lambda e, r=r: e.dma_start(
                    out=wkvt_b[:, 128 * r:128 * (r + 1)],
                    in_=wkv_t.ap()[128 * r:128 * (r + 1), :]))
                dma("gpsimd", lambda e, r=r: e.dma_start(
                    out=wgbg_b[:, 258 * r:258 * (r + 1)],
                    in_=wgbg_t.ap()[128 * r:128 * (r + 1), :]))
            for r in range(2):
                dma("gpsimd", lambda e, r=r: e.dma_start(
                    out=wpr_b[:, 1024 * r:1024 * (r + 1)],
                    in_=wpr_t.ap()[128 * r:128 * (r + 1), :]))
            kv["t_wload"] = ctx.tok("DQG")
            for f in (
                lambda e: e.memset(ones_f[:], 1.0),
                lambda e: e.memset(onescol_b[:], 1.0),
                lambda e: e.memset(blk2_b[:], 0.0),
                lambda e: e.memset(blk2_b[0:64, 0:1], 1.0),
                lambda e: e.memset(blk2_b[64:128, 1:2], 1.0),
                lambda e: e.memset(S_f[:], 0.0),
                lambda e: e.memset(S_b[:], 0.0),
                lambda e: e.memset(qraw0[:, 0:3], 0.0),
                lambda e: e.memset(qraw1[:, 0:3], 0.0),
                lambda e: e.memset(kvraw[:, 0:3], 0.0),
            ):
                vec(f, inc=False)
            for v, c in CONSTS.items():
                vec(lambda e, v=v, c=c: e.memset(constpool[:, c:c + 1], v), inc=False)
            if taps:
                tcp = vec(lambda e: e.tensor_copy(rrep[:, 0:8], constpool[:, :]))
                tap("cp0", lambda: rrep[:, 0:8], [tcp])
            vec(lambda e: e.memset(TB[0]["rcol"][:], 0.0), inc=False)
            t_ms = vec(lambda e: e.memset(TB[1]["rcol"][:], 0.0))
            kv["t_consts"] = t_ms
            kv["t_Sb"] = t_ms
            kv["t_Sf"] = t_ms
            ctx.base_dep = t_ms
            t_mm1 = mm(lambda e: e.matmul(PF[1][:, 0:3], ones_f[:, :], srow[:, :],
                                          start=True, stop=True),
                       deps=[t_cl, t_ms] + bk.acquire(1), inc=True)
            t_ev = vec(lambda e: e.tensor_copy(screp[:], PF[1][:, 0:3]), deps=[t_mm1])
            bk.read(1, t_ev)
            sca(lambda e: e.mul(negbb_rep[:], screp[:, 0:1], -1.0), deps=[t_ev],
                inc=False)
            t_e1 = sca(lambda e: e.activation(negexpA[:], screp[:, 1:2], ACT.Exp),
                       deps=[t_ev])
            t_e2 = vec(lambda e: e.tensor_scalar_mul(negexpA[:], negexpA[:], -1.0),
                       deps=[t_e1])
            kv["t_reps"] = t_e2
            t_mm2 = mm(lambda e: e.matmul(PF[2][:, 0:256], ones_f[:, :],
                                          rmsw_rep[0:1, 0:256], start=True, stop=True),
                       deps=[t_cl, t_ms] + bk.acquire(2), inc=True)
            t_rw = vec(lambda e: e.tensor_copy(rmsw_rep[:], PF[2][:, 0:256]),
                       deps=[t_mm2])
            bk.read(2, t_rw)
            kv["t_rmsw"] = t_rw
            d0 = bk.acquire(0)
            for r in range(8):
                mm(lambda e, r=r: e.matmul(PF[0][0:1, 0:256], onescol_b[:, :],
                                           wqt_b[:, 256 * r:256 * (r + 1)],
                                           start=(r == 0), stop=(r == 7)),
                   deps=([kv["t_wload"], t_ms] + d0) if r == 0 else ())
            for r in range(8):
                mm(lambda e, r=r: e.matmul(PF[0][0:1, 256:384], onescol_b[:, :],
                                           wkvt_b[:, 128 * r:128 * (r + 1)],
                                           start=(r == 0), stop=(r == 7)),
                   inc=(r == 7))
            t_q = ctx.tok("T")
            d3 = bk.acquire(3)
            for r in range(8):
                mm(lambda e, r=r: e.matmul(PF[3][0:1, 0:258], onescol_b[:, :],
                                           wgbg_b[:, 258 * r:258 * (r + 1)],
                                           start=(r == 0), stop=(r == 7)),
                   deps=d3 if r == 0 else (), inc=(r == 7))
            t_g = ctx.tok("T")
            t_w1 = vec(lambda e: e.tensor_copy(wsum_fm[:], PF[0][0:1, 0:384]),
                       deps=[t_q])
            bk.read(0, t_w1)
            t_w2 = vec(lambda e: e.tensor_copy(wsum_gbg[:], PF[3][0:1, 0:258]),
                       deps=[t_g])
            bk.read(3, t_w2)
            kv["t_wsum"] = t_w2
            kv["t_wsumf"] = t_w1
            kv["xq_war"] = [[], []]
            kv["xbf_war"] = [[], []]
            kv["py_done"] = []
            kv["og_done"] = []
            kv["py_war"] = [None, None]
            kv["qscr_war"] = []

        def emit_ln(par, t_xload, t_xn_dep=None):
            mm, vec, sca = ctx.mm, ctx.vec, ctx.sca
            t_sq = sca(lambda e: e.activation(xsqb[:], xq[par][:], ACT.Square),
                       deps=[t_xload])
            t_cast = sca(lambda e: e.copy(xbf[par][:], xq[par][:]),
                         deps=[t_xload] + kv["xbf_war"][par])
            kv["xbf_war"][par] = []
            kv["xq_war"][par] += [t_cast, t_sq]
            d0 = bk.acquire(0)
            d1 = bk.acquire(1)
            for r in range(8):
                mm(lambda e, r=r, par=par: e.matmul(
                    PF[0][0:1, 0:TW], onescol_b[:, :], xbf[par][:, TW * r:TW * (r + 1)],
                    start=(r == 0), stop=(r == 7)),
                    deps=([t_cast, kv["t_consts"]] + d0) if r == 0 else (),
                    inc=(r == 7))
            t_s1 = ctx.tok("T")
            for r in range(8):
                mm(lambda e, r=r: e.matmul(
                    PF[1][0:1, 0:TW], onescol_b[:, :], xsqb[:, TW * r:TW * (r + 1)],
                    start=(r == 0), stop=(r == 7)),
                    deps=([t_sq] + d1) if r == 0 else (), inc=(r == 7))
            t_s2 = ctx.tok("T")
            t_m = sca(lambda e: e.activation(m_row[:], PF[0][0:1, 0:TW], ACT.Copy,
                                             scale=1.0 / C), deps=[t_s1])
            bk.read(0, t_m)
            t_v0 = sca(lambda e: e.activation(var_row[:], PF[1][0:1, 0:TW], ACT.Copy,
                                              scale=1.0 / C), deps=[t_s2])
            bk.read(1, t_v0)
            t_msq = vec(lambda e: e.tensor_mul(mr_row[:], m_row[:], m_row[:]),
                        deps=[t_m])
            t_var = vec(lambda e: e.tensor_sub(var_row[:], var_row[:], mr_row[:]),
                        deps=[t_v0, t_msq])
            t_ln = sca(lambda e: e.activation(rstd_row[:], var_row[:], ACT.Ln,
                                              bias=1e-5), deps=[t_var])
            t_r = sca(lambda e: e.activation(rstd_row[:], rstd_row[:], ACT.Exp,
                                             scale=-0.5), deps=[t_ln])
            return t_cast, t_m, t_r

        def emit_tile(i):
            mm, vec, sca, dma = ctx.mm, ctx.vec, ctx.sca, ctx.dma
            par = i % 2
            tb = TB[par]
            kv["tb"] = tb
            qb0, qb1, kvb = tb["qb0"], tb["qb1"], tb["kvb"]
            qodd0, qodd1 = tb["qodd0"], tb["qodd1"]
            gate_tm, beta_c, g_c = tb["gate_tm"], tb["beta_c"], tb["g_c"]
            rcol, nrcol, ogfm = tb["rcol"], tb["nrcol"], tb["ogfm"]
            snap = kv.get(f"tsnap{par}")
            if snap is not None:
                for eng in ("tensor", "vector", "scalar", "sync"):
                    deps = [(sm, snap[sm]) for sm in ("T", "V", "S", "DQS", "DQA")]
                    ctx.op(eng, lambda e: None, deps, None)
            t_xload = kv[f"t_x{i}"]
            if i + 1 < NT:
                npar = (i + 1) % 2
                war = kv["xq_war"][npar]
                kv["xq_war"][npar] = []
                for r in range(8):
                    dma("sync", lambda e, r=r, i=i, npar=npar: e.dma_start(
                        out=xq[npar][:, TW * r:TW * (r + 1)],
                        in_=x_fm.ap()[128 * r:128 * (r + 1), TW * (i + 1):TW * (i + 2)]),
                        deps=war)
                kv[f"t_x{i+1}"] = ctx.tok("DQS")
            t_cast, t_m, t_r = emit_ln(par, t_xload)
            t_nm = sca(lambda e: e.activation(negm_b[:], m_row[:], ACT.Copy,
                                              scale=-1.0), deps=[t_m])
            d5 = bk.acquire(5)
            t_rrmm = mm(lambda e: e.matmul(PF[5][:, 0:TW], ones_f[:, :], rstd_row[:, :],
                                           start=True, stop=True),
                        deps=[t_r] + d5, inc=True)
            t_rrep = vec(lambda e: e.tensor_copy(rrep[:], PF[5][:, 0:TW]),
                         deps=[t_rrmm])
            bk.read(5, t_rrep)
            d3 = bk.acquire(3)
            for cc in range(4):
                mm(lambda e, cc=cc: e.matmul(PF[3][:, cc:cc + 1],
                                             rstd_row[0:1, 128 * cc:128 * (cc + 1)],
                                             ones_f[0:1, 0:1], start=True, stop=True),
                   deps=([t_r] + d3) if cc == 0 else (), inc=(cc == 3))
            t_rcmm = ctx.tok("T")
            t_rc = vec(lambda e: e.tensor_copy(rcol[:], PF[3][:, 0:4]), deps=[t_rcmm])
            bk.read(3, t_rc)
            t_nrc = vec(lambda e: e.tensor_scalar_mul(nrcol[:], rcol[:], -1.0),
                        deps=[t_rc])
            proj = []
            for (bnk, coff) in ((0, 0), (1, 128), (4, None)):
                dd = bk.acquire(bnk)
                for r in range(8):
                    if coff is None:
                        lhs = lambda e, r=r: wkvt_b[:, 128 * r:128 * (r + 1)]
                    else:
                        lhs = lambda e, r=r, coff=coff: wqt_b[:, 256 * r + coff:256 * r + coff + 128]
                    mm(lambda e, r=r, par=par, bnk=bnk, lhs=lhs: e.matmul(
                        PF[bnk][:, 0:TW], lhs(e), xbf[par][:, TW * r:TW * (r + 1)],
                        start=(r == 0), stop=False),
                        deps=(dd + [t_cast, kv["t_wsumf"], kv["t_wload"]]) if r == 0 else ())
                woff = {0: (0, 128), 1: (128, 256), 4: (256, 384)}[bnk]
                mm(lambda e, bnk=bnk, woff=woff: e.matmul(
                    PF[bnk][:, 0:TW], wsum_fm[0:1, woff[0]:woff[1]], negm_b[:, :],
                    start=False, stop=True), deps=[t_nm], inc=True)
                proj.append(ctx.tok("T"))
            t_q0p, t_q1p, t_kvp = proj
            evs = []
            for (buf, bnk, tq) in ((qraw0, 0, t_q0p), (qraw1, 1, t_q1p), (kvraw, 4, t_kvp)):
                if i > 0:
                    vec(lambda e, buf=buf: e.tensor_copy(buf[:, 0:3], buf[:, TW:TW + 3]),
                        deps=kv["qscr_war"], inc=False)
                t_e = vec(lambda e, buf=buf, bnk=bnk: e.tensor_mul(
                    buf[:, 3:TW + 3], PF[bnk][:, 0:TW], rrep[:]), deps=[tq, t_rrep])
                bk.read(bnk, t_e)
                evs.append(t_e)
            t_evq0, t_evq1, t_evkv = evs
            kv["qscr_war"] = []
            gtoks = []
            for cc in range(4):
                bnk = (3, 5, 0, 1)[cc]
                dd = bk.acquire(bnk)
                for r in range(8):
                    mm(lambda e, r=r, cc=cc, par=par, bnk=bnk: e.matmul(
                        PF[bnk][:, 0:258],
                        xbf[par][:, TW * r + 128 * cc:TW * r + 128 * cc + 128],
                        wgbg_b[:, 258 * r:258 * (r + 1)], start=(r == 0), stop=False),
                        deps=(dd + [kv["t_wsum"]]) if r == 0 else ())
                mm(lambda e, cc=cc, bnk=bnk: e.matmul(
                    PF[bnk][:, 0:258], negm_b[0:1, 128 * cc:128 * (cc + 1)],
                    wsum_gbg[:, :], start=False, stop=True), deps=[t_nm], inc=True)
                gtoks.append((cc, bnk, ctx.tok("T")))
            for (cc, bnk, t_gc) in gtoks:
                t_si = sca(lambda e, cc=cc, bnk=bnk: e.activation(
                    gate_tm[:, 256 * cc:256 * (cc + 1)], PF[bnk][:, 0:256], ACT.Silu,
                    scale=rcol[:, cc:cc + 1]), deps=[t_gc, t_rc])
                bk.read(bnk, t_si)
            for (cc, bnk, t_gc) in gtoks:
                t_be = sca(lambda e, cc=cc, bnk=bnk: e.activation(
                    beta_c[:, cc:cc + 1], PF[bnk][:, 256:257], ACT.Exp,
                    scale=nrcol[:, cc:cc + 1], bias=negbb_rep[:, :]),
                    deps=[t_gc, t_nrc, kv["t_reps"]])
                t_b2 = vec(lambda e, cc=cc: e.tensor_scalar_add(
                    beta_c[:, cc:cc + 1], beta_c[:, cc:cc + 1], 1.0), deps=[t_be])
                t_b3 = vec(lambda e, cc=cc: e.reciprocal(
                    beta_c[:, cc:cc + 1], beta_c[:, cc:cc + 1]), deps=[t_b2])
                t_ge = sca(lambda e, cc=cc, bnk=bnk: e.activation(
                    g_c[:, cc:cc + 1], PF[bnk][:, 257:258], ACT.Exp,
                    scale=rcol[:, cc:cc + 1], bias=screp[:, 2:3]), deps=[t_gc])
                t_g2 = sca(lambda e, cc=cc: e.activation(
                    g_c[:, cc:cc + 1], g_c[:, cc:cc + 1], ACT.Ln, bias=1.0),
                    deps=[t_ge])
                t_g3 = vec(lambda e, cc=cc: e.tensor_mul(
                    g_c[:, cc:cc + 1], g_c[:, cc:cc + 1], negexpA[:]), deps=[t_g2])
                bk.read(bnk, t_ge)
                kv[f"t_beta{cc}"] = t_b3
                kv[f"t_g{cc}"] = t_g3
            convs = []
            for (cin, cout, w, tdep) in ((qraw0, qc0, cwq0, t_evq0),
                                         (qraw1, qc1, cwq1, t_evq1),
                                         (kvraw, kvc, cwkv_s, t_evkv)):
                t_cj = vec(lambda e, cin=cin, cout=cout, w=w: e.tensor_scalar_mul(
                    cout[:], cin[:, 0:TW], w[:, 0:1]),
                    deps=[tdep, kv["t_scales"]])
                for j in range(1, 4):
                    t_cj = vec(lambda e, cin=cin, cout=cout, w=w, j=j: e.scalar_tensor_tensor(
                        cout[:], cin[:, j:TW + j], w[:, j:j + 1], cout[:],
                        op0=ALU.mult, op1=ALU.add), deps=[t_cj])
                convs.append(t_cj)
            t_cq0, t_cq1, t_ckv = convs
            kv["qscr_war"] = list(convs)
            t_sq0 = sca(lambda e: e.activation(qb0[:], qc0[:], ACT.Square),
                        deps=[t_cq0])
            t_sq1 = sca(lambda e: e.activation(qb1[:], qc1[:], ACT.Square),
                        deps=[t_cq1])
            t_sqk = sca(lambda e: e.activation(kvb[0:64, :], kvc[0:64, :], ACT.Square),
                        deps=[t_ckv])
            d0 = bk.acquire(0)
            t_ssq0 = mm(lambda e: e.matmul(PF[0][0:2, 0:TW], blk2_b[:, :], qb0[:, :],
                                           start=True, stop=True),
                        deps=[t_sq0] + d0, inc=True)
            d1 = bk.acquire(1)
            t_ssq1 = mm(lambda e: e.matmul(PF[1][0:2, 0:TW], blk2_b[:, :], qb1[:, :],
                                           start=True, stop=True),
                        deps=[t_sq1] + d1, inc=True)
            d4 = bk.acquire(4)
            t_ssqk = mm(lambda e: e.matmul(PF[4][0:1, 0:TW], onescol_b[0:64, :],
                                           kvb[0:64, :], start=True, stop=True),
                        deps=[t_sqk] + d4, inc=True)
            t_l0 = sca(lambda e: e.activation(rsq01[:], PF[0][0:2, 0:TW], ACT.Ln,
                                              bias=1e-12), deps=[t_ssq0])
            bk.read(0, t_l0)
            t_r0 = sca(lambda e: e.activation(rsq01[:], rsq01[:], ACT.Exp, scale=-0.5),
                       deps=[t_l0])
            t_l1 = sca(lambda e: e.activation(rsq23[:], PF[1][0:2, 0:TW], ACT.Ln,
                                              bias=1e-12), deps=[t_ssq1])
            bk.read(1, t_l1)
            t_r1 = sca(lambda e: e.activation(rsq23[:], rsq23[:], ACT.Exp, scale=-0.5),
                       deps=[t_l1])
            t_lk = sca(lambda e: e.activation(kv_rhs[0:1, 0:TW], PF[4][0:1, 0:TW],
                                              ACT.Ln, bias=1e-12), deps=[t_ssqk])
            bk.read(4, t_lk)
            t_rk = sca(lambda e: e.activation(kv_rhs[0:1, 0:TW], kv_rhs[0:1, 0:TW],
                                              ACT.Exp, scale=-0.5), deps=[t_lk])
            qbt = []
            for (t_seed, rr, sel, src, dst, t_scr) in (
                    (t_r0, rsq01, colsel_q, qc0, qb0, t_sq0),
                    (t_r1, rsq23, colsel_q, qc1, qb1, t_sq1),
                    (t_rk, kv_rhs, colsel_kv, kvc, kvb, t_sqk)):
                d5 = bk.acquire(5)
                t_bq = mm(lambda e, sel=sel, rr=rr: e.matmul(
                    PF[5][:, 0:TW], sel[:, :], rr[:, :], start=True, stop=True),
                    deps=[t_seed] + d5, inc=True)
                t_qb = vec(lambda e, src=src, dst=dst: e.tensor_mul(
                    dst[:], src[:], PF[5][:, 0:TW]), deps=[t_bq, t_scr])
                bk.read(5, t_qb)
                qbt.append(t_qb)
            t_qb0, t_qb1, t_kvb = qbt
            t_o0 = dma("sync", lambda e: e.dma_start(out=qodd0[:, :], in_=qb0[64:128, :]),
                       deps=[t_qb0])
            t_o1 = dma("sync", lambda e: e.dma_start(out=qodd1[:, :], in_=qb1[64:128, :]),
                       deps=[t_qb1])
            kv["t_qb"] = (t_qb0, t_qb1, t_kvb, t_o0, t_o1)
            kv["xbf_war"][par].append(ctx.tok("T"))
            if i == 0 and taps:
                tap("cp1", lambda: constpool[:, :], [t_nrc])
                tap("nrc", lambda: nrcol[:, :], [t_nrc])
                tap("nbb", lambda: negbb_rep[:, :], [kv["t_reps"]])
                tap("screp", lambda: screp[:, :], [kv["t_reps"]])
                tap("qb0", lambda: qb0[:, :], [t_qb0])
                tap("kvb", lambda: kvb[:, :], [t_kvb])
                tap("xraw", lambda: qc0[:, :], [t_cq0])
                tap("rr", lambda: rrep[:, :], [t_rrep])
                tap("gate", lambda: gate_tm[:, :],
                    [kv["t_beta3"], kv["t_g3"], ctx.tok("S")])
                tap("bg", lambda: beta_c[:, :], [kv["t_beta3"]],
                    dst=lambda: tap_d["bg"].ap()[:, 0:4])
                tap("bg", lambda: g_c[:, :], [kv["t_g3"]],
                    dst=lambda: tap_d["bg"].ap()[:, 4:8])
            if SCHED == "seq":
                for cc in range(4):
                    drive(gen_indep(i, cc))
                    drive(gen_chain(i, cc))
            elif SCHED == "pair":
                drive(gen_indep(i, 0), gen_indep(i, 1))
                drive(gen_chain(i, 0))
                drive(gen_indep(i, 2))
                drive(gen_chain(i, 1))
                drive(gen_indep(i, 3))
                drive(gen_chain(i, 2))
                drive(gen_chain(i, 3))
            else:
                drive(gen_indep(i, 0), gen_indep(i, 1))
                drive(gen_chain(i, 0), gen_indep(i, 2))
                drive(gen_chain(i, 1), gen_indep(i, 3))
                drive(gen_chain(i, 2))
                drive(gen_chain(i, 3))
            if i == 0 and taps:
                tap("Sf", lambda: S_f[:, :], [kv["t_Sf"]])
                tap("Of", lambda: CB[1]["O_f"][:, :],
                    [kv["t_of0"], kv["t_of1"], kv["t_of2"], kv["t_of3"]])
                tap("ogfm", lambda: ogfm[:, :], list(kv["og_done"]))
            quarter, qoff = (TW * i) // TOWN, (TW * i) % TOWN
            ogdeps = kv["og_done"]
            kv["og_done"] = []
            for r in range(8):
                bnk = 1 if r % 2 == 0 else 2
                half = r % 2
                dd = bk.acquire(bnk)
                for j in range(2):
                    mm(lambda e, r=r, j=j, bnk=bnk: e.matmul(
                        PF[bnk][:, 0:TW],
                        wpr_b[:, 1024 * j + 128 * r:1024 * j + 128 * (r + 1)],
                        ogfm[:, TW * j:TW * (j + 1)], start=(j == 0), stop=(j == 1)),
                        deps=(dd + ogdeps + [kv["t_wload"]]) if j == 0 else (),
                        inc=(j == 1))
                t_pm = ctx.tok("T")
                t_pe = vec(lambda e, bnk=bnk, half=half: e.tensor_copy(
                    pystage[:, TW * half:TW * (half + 1)], PF[bnk][:, 0:TW]),
                    deps=[t_pm, kv["py_war"][half]])
                bk.read(bnk, t_pe)
                t_pd = dma("scalar", lambda e, r=r, quarter=quarter, qoff=qoff, half=half:
                           e.dma_start(
                               out=py_in.ap()[quarter, 128 * r:128 * (r + 1),
                                              qoff:qoff + TW],
                               in_=pystage[:, TW * half:TW * (half + 1)]),
                           deps=[t_pe])
                kv["py_war"][half] = t_pd
                kv["py_done"].append(t_pd)
            kv[f"tsnap{par}"] = dict(ctx.counts)

        def drive(*gens):
            gens = [g for g in gens if g is not None]
            while gens:
                nxt = []
                for g in gens:
                    try:
                        next(g)
                        nxt.append(g)
                    except StopIteration:
                        pass
                gens = nxt

        def gen_indep(i, cc):
            mm, vec, sca = ctx.mm, ctx.vec, ctx.sca
            tb = kv["tb"]
            qb0, qb1, kvb = tb["qb0"], tb["qb1"], tb["kvb"]
            qodd0, qodd1 = tb["qodd0"], tb["qodd1"]
            gate_tm, beta_c, g_c = tb["gate_tm"], tb["beta_c"], tb["g_c"]
            b = CB[cc]
            BA, BB, BC = (0, 1, 2) if cc % 2 == 0 else (3, 4, 5)
            gpi = cc % 2
            sl = slice(128 * cc, 128 * (cc + 1))
            t_qb0, t_qb1, t_kvb, t_o0, t_o1 = kv["t_qb"]
            t_beta = kv[f"t_beta{cc}"]
            t_g = kv[f"t_g{cc}"]
            d0 = bk.acquire(BA)
            t_kk = mm(lambda e: e.matmul(PF[BA][:, 0:128], kvb[0:64, sl], kvb[0:64, sl],
                                         start=True, stop=True),
                      deps=[t_kvb] + d0, inc=True)
            t_kkb = sca(lambda e: e.activation(b["KKb"][:], PF[BA][:, 0:128], ACT.Copy,
                                               scale=beta_c[:, cc:cc + 1]),
                        deps=[t_kk, t_beta])
            bk.read(BA, t_kkb)
            yield
            d1 = bk.acquire(BB)
            mm(lambda e: e.matmul(PF[BB][:, 0:1], triu1_f[:, :], g_c[:, cc:cc + 1],
                                  start=True, stop=True), deps=[t_g] + d1)
            mm(lambda e: e.matmul(PF[BB][:, 64:65], tril0_f[:, :], g_c[:, cc:cc + 1],
                                  start=True, stop=True))
            t_cs = mm(lambda e: e.matmul(PF[BB][0:1, 128:256], g_c[:, cc:cc + 1],
                                         triu1_f[:, :], start=True, stop=True), inc=True)
            t_eb = sca(lambda e: e.activation(b["ebcol"][:], PF[BB][:, 0:1], ACT.Exp),
                       deps=[t_cs])
            t_dl = sca(lambda e: e.activation(b["dlcol"][:], PF[BB][:, 64:65], ACT.Exp),
                       deps=[t_cs])
            t_bc = sca(lambda e: e.copy(b["bcol"][:], PF[BB][:, 0:1]), deps=[t_cs])
            t_nbc = sca(lambda e: e.mul(b["nbcol"][:], PF[BB][:, 0:1], -1.0), deps=[t_cs])
            t_brr = sca(lambda e: e.copy(b["brow"][:], PF[BB][0:1, 128:256]), deps=[t_cs])
            bk.read(BB, t_brr)
            bk.read(BB, t_nbc)
            yield
            t_ebl = vec(lambda e: e.tensor_mul(b["eblcol"][:], b["ebcol"][:], b["dlcol"][:]),
                        deps=[t_eb, t_dl])
            t_bb = vec(lambda e: e.tensor_mul(b["bebe"][:], beta_c[:, cc:cc + 1],
                                              b["ebcol"][:]), deps=[t_eb, t_beta])
            d2 = bk.acquire(BC)
            t_df = mm(lambda e: e.matmul(PF[BC][:, 0:128], ones_f[:, :], b["brow"][:, :],
                                         start=True, stop=True),
                      deps=[t_brr] + d2, inc=True)
            t_dts = sca(lambda e: e.activation(b["dect"][:], PF[BC][:, 0:128], ACT.Exp,
                                               scale=-1.0, bias=b["bcol"][:, :]),
                        deps=[t_df, t_bc])
            t_dst = sca(lambda e: e.activation(b["dmst"][:], PF[BC][:, 0:128], ACT.Exp,
                                               scale=1.0, bias=b["nbcol"][:, :]),
                        deps=[t_df, t_nbc])
            bk.read(BC, t_dts)
            bk.read(BC, t_dst)
            yield
            t_dm = vec(lambda e: e.tensor_mul(b["dmst"][:], b["dmst"][:], triu1_f[:]),
                       deps=[t_dst])
            t_m1 = vec(lambda e: e.tensor_mul(b["tmpf"][:], b["dect"][:], b["KKb"][:]),
                       deps=[t_dts, t_kkb])
            t_m2 = vec(lambda e: e.tensor_mul(b["M_b"][:], b["tmpf"][:], tril0_f[:]),
                       deps=[t_m1])
            yield
            dg0 = bk.acquire(6 + gpi)
            t_tm = mm(lambda e: e.transpose(PG[gpi][:, 0:128], b["M_b"][:, :], idb[:, :]),
                      deps=[t_m2] + dg0, inc=True)
            t_b0 = vec(lambda e: e.tensor_copy(b["B0_b"][:], PG[gpi][:, 0:128]),
                       deps=[t_tm])
            bk.read(6 + gpi, t_b0)
            t_Y = vec(lambda e: e.tensor_sub(b["Y_b"][:], idb[:], b["B0_b"][:]),
                      deps=[t_b0])
            yield
            dg1 = bk.acquire(6 + gpi)
            t_tk = mm(lambda e: e.transpose(PG[gpi][:, 0:128], kvb[:, sl], idb[:, :]),
                      deps=[t_kvb] + dg1, inc=True)
            t_r1 = vec(lambda e: e.tensor_scalar_mul(b["RHS2"][:, 0:64], PG[gpi][:, 0:64],
                                                     b["bebe"][:, :]), deps=[t_tk, t_bb])
            t_r2 = vec(lambda e: e.tensor_scalar_mul(b["RHS2"][:, 64:128],
                                                     PG[gpi][:, 64:128],
                                                     beta_c[:, cc:cc + 1]),
                       deps=[t_tk, t_beta])
            t_kh = vec(lambda e: e.tensor_scalar_mul(b["khat"][:], PG[gpi][:, 0:64],
                                                     b["dlcol"][:, :]), deps=[t_tk, t_dl])
            bk.read(6 + gpi, t_r1)
            bk.read(6 + gpi, t_r2)
            bk.read(6 + gpi, t_kh)
            yield
            Acur, Bcur = b["M_b"][:, 0:128], b["B0_b"][:, 0:128]
            t_A = [t_m2, t_b0]
            for kk in range(1, 7):
                ABn = (b["AB0"], b["AB1"])[kk % 2]
                d0 = bk.acquire(BA)
                t_am = mm(lambda e, Bcur=Bcur, Acur=Acur: e.matmul(
                    PF[BA][:, 0:128], Bcur, Acur, start=True, stop=True),
                    deps=t_A + d0, inc=(kk == 6))
                if kk < 6:
                    t_am = mm(lambda e, Acur=Acur, Bcur=Bcur: e.matmul(
                        PF[BA][:, 128:256], Acur, Bcur, start=True, stop=True),
                        inc=True)
                    t_An = vec(lambda e, ABn=ABn: e.tensor_copy(ABn[:, 0:256],
                                                               PF[BA][:, 0:256]),
                               deps=[t_am])
                else:
                    t_An = vec(lambda e, ABn=ABn: e.tensor_copy(ABn[:, 0:128],
                                                               PF[BA][:, 0:128]),
                               deps=[t_am])
                bk.read(BA, t_An)
                yield
                d2 = bk.acquire(BC)
                t_ym = mm(lambda e, ABn=ABn: e.matmul(
                    PF[BC][:, 0:128], ABn[:, 0:128], b["Y_b"][:, :],
                    start=True, stop=True),
                    deps=[t_An, t_Y] + d2, inc=True)
                t_Y = vec(lambda e: e.tensor_add(b["Y_b"][:], b["Y_b"][:],
                                                 PF[BC][:, 0:128]),
                          deps=[t_ym])
                bk.read(BC, t_Y)
                Acur, Bcur, t_A = ABn[:, 0:128], ABn[:, 128:256], [t_An]
                yield
            d0 = bk.acquire(BA)
            t_u2m = mm(lambda e: e.matmul(PF[BA][:, 0:128], b["Y_b"][:, :], b["RHS2"][:, :],
                                          start=True, stop=True),
                       deps=[t_Y, t_r1, t_r2] + d0, inc=True)
            t_u2f = vec(lambda e: e.tensor_copy(b["U2f"][:], PF[BA][:, 0:128]),
                        deps=[t_u2m])
            bk.read(BA, t_u2f)
            t_u2fin = t_u2f
            yield
            d1 = bk.acquire(BB)
            t_u2t = mm(lambda e: e.matmul(PF[BB][:, 0:128], b["RHS2"][:, :], b["Y_b"][:, :],
                                          start=True, stop=True), deps=d1, inc=True)
            t_wt = vec(lambda e: e.tensor_copy(b["WtU"][0:64, :], PF[BB][0:64, 0:128]),
                       deps=[t_u2t])
            bk.read(BB, t_wt)
            yield
            qsrc = ((qb0, t_qb0), (qodd0, t_o0), (qb1, t_qb1), (qodd1, t_o1))
            d0 = bk.acquire(BC)
            for j in range(4):
                qt, t_qs = qsrc[j]
                mm(lambda e, qt=qt, j=j: e.matmul(
                    PF[BC][:, 128 * j:128 * (j + 1)], kvb[0:64, sl], qt[0:64, sl],
                    start=True, stop=True),
                    deps=([t_qs, t_kvb] + d0) if j == 0 else [t_qs], inc=(j == 3))
            t_kq = ctx.tok("T")
            yield
            t_am = vec(lambda e: e.tensor_tensor(
                out=b["amsk"][:, 0:512].rearrange("p (r c) -> p r c", r=4),
                in0=PF[BC][:, 0:512].rearrange("p (r c) -> p r c", r=4),
                in1=b["dmst"][:].rearrange("p (x c) -> p x c", x=1).broadcast_to(
                    [128, 4, 128]),
                op=ALU.mult), deps=[t_kq, t_dm])
            bk.read(BC, t_am)
            for j in range(4):
                kv[f"t_amsk{cc}_{j}"] = t_am
            yield
            t_grw = vec(lambda e, cc=cc: e.tensor_mul(
                b["grw"][:], gate_tm[:, 256 * cc:256 * (cc + 1)], rmsw_rep[:]),
                deps=[kv["t_rmsw"], kv[f"t_beta{cc}"]])
            kv[f"t_ind{cc}"] = dict(t_wt=t_wt, t_u2fin=t_u2fin, t_kh=t_kh,
                                    t_eb=t_eb, t_ebl=t_ebl, t_grw=t_grw,
                                    t_dm=t_dm)

        def gen_chain(i, cc):
            mm, vec, sca = ctx.mm, ctx.vec, ctx.sca
            tb = kv["tb"]
            qb0, qb1, kvb = tb["qb0"], tb["qb1"], tb["kvb"]
            qodd0, qodd1 = tb["qodd0"], tb["qodd1"]
            gate_tm, beta_c, g_c = tb["gate_tm"], tb["beta_c"], tb["g_c"]
            ogfm = tb["ogfm"]
            b = CB[cc]
            BA, BB = (3, 4) if cc % 2 == 0 else (0, 1)
            gpi = (cc + 1) % 2
            ind = kv[f"t_ind{cc}"]
            sl = slice(128 * cc, 128 * (cc + 1))
            t_qb0, t_qb1, t_kvb, t_o0, t_o1 = kv["t_qb"]
            d0 = bk.acquire(BA)
            t_ws = mm(lambda e: e.matmul(PF[BA][:, 0:64], b["WtU"][0:64, :],
                                         S_b[0:64, 0:64], start=True, stop=True),
                      deps=[ind["t_wt"], kv["t_Sb"]] + d0, inc=True)
            t_ub = vec(lambda e: e.scalar_tensor_tensor(
                b["U_b"][:], PF[BA][:, 0:64], -1.0, b["U2f"][:, 64:128],
                op0=ALU.mult, op1=ALU.add), deps=[t_ws, ind["t_u2fin"]])
            bk.read(BA, t_ub)
            yield
            qsrc = ((qb0, t_qb0), (qodd0, t_o0), (qb1, t_qb1), (qodd1, t_o1))
            d4 = bk.acquire(BB)
            d5 = bk.acquire(BA)
            for j in range(4):
                qt, t_qs = qsrc[j]
                mm(lambda e, j=j: e.matmul(
                    PF[BB][:, 64 * j:64 * (j + 1)], b["amsk"][:, 128 * j:128 * (j + 1)],
                    b["U_b"][:, :], start=True, stop=True),
                    deps=([kv[f"t_amsk{cc}_0"], t_ub] + d4) if j == 0 else (),
                    inc=(j == 3))
                if j == 0:
                    t_o1m = None
                mm(lambda e, qt=qt, j=j: e.matmul(
                    PF[BA][:, 64 * j:64 * (j + 1)], qt[0:64, sl], S_b[0:64, 0:64],
                    start=True, stop=True),
                    deps=([kv["t_Sb"]] + d5) if j == 0 else (), inc=(j == 3))
                if j == 1:
                    yield
            t_omm = ctx.tok("T")
            yield
            t_ebs = sca(lambda e: e.activation(b["ebtmp"][:, 0:256], PF[BA][:, 0:256],
                                               ACT.Copy, scale=b["ebcol"][:, :]),
                        deps=[t_omm, ind["t_eb"]])
            t_of = vec(lambda e: e.tensor_add(b["O_f"][:, 0:256], PF[BB][:, 0:256],
                                              b["ebtmp"][:, 0:256]),
                       deps=[t_omm, t_ebs])
            bk.read(BA, t_ebs)
            bk.read(BB, t_of)
            for j in range(4):
                kv[f"t_of{j}"] = t_of
            yield
            d0 = bk.acquire(BA)
            t_ku = mm(lambda e: e.matmul(PF[BA][0:64, 0:64], b["khat"][:, 0:64],
                                         b["U_b"][:, :], start=True, stop=True),
                      deps=[ind["t_kh"], t_ub] + d0, inc=True)
            t_Sf = vec(lambda e: e.scalar_tensor_tensor(
                S_f[:], S_f[:], b["eblcol"][0:64, :], PF[BA][0:64, 0:64],
                op0=ALU.mult, op1=ALU.add), deps=[t_ku, ind["t_ebl"], kv["t_Sf"]])
            bk.read(BA, t_Sf)
            t_Sb = vec(lambda e: e.tensor_copy(S_b[:], S_f[:]), deps=[t_Sf])
            kv["t_Sf"] = t_Sf
            kv["t_Sb"] = t_Sb
            yield
            t_osq = sca(lambda e: e.activation(b["Osq"][:], b["O_f"][:], ACT.Square),
                        deps=[kv["t_of0"]])
            t_red = vec(lambda e: e.reduce_sum(
                b["rms_in"][:, 0:4].rearrange("p (r x) -> p r x", x=1),
                b["Osq"][:, 0:256].rearrange("p (r c) -> p r c", r=4), AX.X),
                deps=[t_osq])
            t_rl = sca(lambda e: e.activation(b["rms4"][:], b["rms_in"][:], ACT.Ln,
                                              scale=1.0 / 64, bias=1e-5), deps=[t_red])
            t_rr = sca(lambda e: e.activation(b["rms4"][:], b["rms4"][:], ACT.Exp,
                                              scale=-0.5), deps=[t_rl])
            yield
            t_og1 = vec(lambda e: e.tensor_tensor(
                out=b["ebtmp"][:, 0:256].rearrange("p (r c) -> p r c", r=4),
                in0=b["O_f"][:, 0:256].rearrange("p (r c) -> p r c", r=4),
                in1=b["rms4"][:, 0:4].rearrange("p (r x) -> p r x", x=1).broadcast_to(
                    [128, 4, 64]),
                op=ALU.mult), deps=[t_rr, kv["t_of0"]])
            t_og = vec(lambda e: e.tensor_mul(b["og_bf"][:, 0:256],
                                              b["ebtmp"][:, 0:256], b["grw"][:, 0:256]),
                       deps=[t_og1, ind["t_grw"]])
            yield
            for half in range(2):
                dg = bk.acquire(6 + gpi)
                t_t = mm(lambda e, half=half: e.transpose(
                    PG[gpi][:, 0:128], b["og_bf"][:, 128 * half:128 * (half + 1)],
                    idb[:, :]), deps=[t_og] + dg, inc=True)
                t_e = vec(lambda e, half=half, cc=cc: e.tensor_copy(
                    ogfm[:, 512 * half + 128 * cc:512 * half + 128 * (cc + 1)],
                    PG[gpi][:, 0:128]), deps=[t_t])
                bk.read(6 + gpi, t_e)
                kv["og_done"].append(t_e)
                yield

        def phase15():
            mm, vec, sca, dma = ctx.mm, ctx.vec, ctx.sca, ctx.dma
            for ii in range(TOWN // TW):
                par = ii % 2
                ctx.guard()
                war = kv["xq_war"][par]
                kv["xq_war"][par] = []
                for r in range(8):
                    dma("sync", lambda e, r=r, ii=ii, par=par: e.dma_start(
                        out=xq[par][:, TW * r:TW * (r + 1)],
                        in_=x_own.ap()[128 * r:128 * (r + 1), TW * ii:TW * (ii + 1)]),
                        deps=war)
                t_xl = ctx.tok("DQS")
                t_cast, t_m, t_r = emit_ln(par, t_xl)
                t_mr = vec(lambda e: e.tensor_mul(mr_row[:], m_row[:], rstd_row[:]),
                           deps=[t_r])
                d5 = bk.acquire(5)
                t_rr = mm(lambda e: e.matmul(PF[5][:, 0:TW], ones_f[:, :],
                                             rstd_row[:, :], start=True, stop=True),
                          deps=[t_r] + d5, inc=True)
                t_rrep = vec(lambda e: e.tensor_copy(rrep[:], PF[5][:, 0:TW]),
                             deps=[t_rr])
                bk.read(5, t_rrep)
                d5 = bk.acquire(5)
                t_mm2 = mm(lambda e: e.matmul(PF[5][:, 0:TW], ones_f[:, :],
                                              mr_row[:, :], start=True, stop=True),
                           deps=[t_mr] + d5, inc=True)
                t_mrep = vec(lambda e: e.tensor_copy(mrrep[:], PF[5][:, 0:TW]),
                             deps=[t_mm2])
                bk.read(5, t_mrep)
                for r in range(8):
                    t_mx = vec(lambda e, r=r, par=par: e.tensor_mul(
                        qc0[:], xq[par][:, TW * r:TW * (r + 1)], rrep[:]),
                        deps=[t_rrep])
                    t_x = vec(lambda e, r=r, ii=ii: e.tensor_sub(
                        xn_own[:, TOWN * r + TW * ii:TOWN * r + TW * (ii + 1)],
                        qc0[:], mrrep[:]), deps=[t_mrep, t_mx])
                    kv["xq_war"][par].append(t_x)
            kv["t_xn"] = ctx.tok("V")

        def phase2():
            if use_cc:
                kv["t_cc"] = ctx.op("gpsimd", lambda e: e.collective_compute(
                    "ReduceScatter", ALU.add,
                    replica_groups=[[0, 1, 2, 3], [4, 5, 6, 7]],
                    ins=[py_in.ap().opt()], outs=[py_out.ap().opt()]),
                    deps=kv["py_done"], inc="CC", inc_n=1)
            else:
                kv["t_cc"] = ctx.op("gpsimd", lambda e: e.dma_start(
                    out=py_out.ap()[:, :], in_=py_in.ap()[0, :, :]),
                    deps=kv["py_done"], inc="CC", inc_n=1)

        def tap(name, src_ap_fn, deps, dst=None):
            if not taps or name not in tap_d:
                return None
            return ctx.dma("sync", lambda e: e.dma_start(
                out=tap_d[name].ap() if dst is None else dst(),
                in_=src_ap_fn()), deps=deps)

        def emit_ffn_pass(p):
            mm, vec, sca, dma = ctx.mm, ctx.vec, ctx.sca, ctx.dma
            war = kv.setdefault("fw_war", [[], []])
            scr_war = kv.setdefault("fscr_war", [None, None, None, None])
            l1war0 = kv.setdefault("l1_war", [[], [], [], []])
            if p == 1:
                for wi in range(2):
                    for slot in range(2):
                        l1war0[2 * wi + slot] = l1war0[2 * wi + slot] + war[wi]
            l1war = kv.setdefault("l1_war", [[], [], [], []])
            for ob in range(32):
                slot = ob % 2
                co = 1024 * slot
                wt = [None, None]
                for wi, wdr in enumerate((w1t, w2t)):
                    wt[wi] = dma("gpsimd", lambda e, wdr=wdr, ob=ob, wi=wi, co=co:
                                 e.dma_start(
                                     out=Wst[wi][:, co:co + 1024].rearrange(
                                         "p (s c) -> p s c", s=8),
                                     in_=wdr.ap()[:, 128 * ob:128 * (ob + 1)].rearrange(
                                         "(s p) c -> p s c", p=128)),
                                 deps=l1war[2 * wi + slot] + [kv["t_cc"]])
                    l1war[2 * wi + slot] = []
                nb = 2 * NTG
                dd = []
                for b in range(nb):
                    dd += bk.acquire(b)
                for wi in range(2):
                    for tg in range(NTG):
                        b = wi * NTG + tg
                        for k in range(8):
                            last = (k == 7 and wi == 1 and tg == NTG - 1)
                            mm(lambda e, k=k, wi=wi, tg=tg, b=b, p=p, co=co: e.matmul(
                                PF[b][:, 0:PTW],
                                Wst[wi][:, co + 128 * k:co + 128 * (k + 1)],
                                xn_own[:, TOWN * k + PT * p + PTW * tg:
                                       TOWN * k + PT * p + PTW * (tg + 1)],
                                start=(k == 0), stop=(k == 7)),
                                deps=(dd + [wt[0], wt[1], kv["t_xn"]]) if (k == 0 and wi == 0 and tg == 0) else (),
                                inc=last)
                t_l1 = ctx.tok("T")
                l1war[2 * 0 + slot].append(t_l1)
                l1war[2 * 1 + slot].append(t_l1)
                war[0].append(t_l1) if ob == 31 else None
                war[1].append(t_l1) if ob == 31 else None
                for tg in range(NTG):
                    t_e1 = sca(lambda e, ob=ob, tg=tg: e.activation(
                        ft1[:], PF[tg][:, 0:PTW], ACT.Silu,
                        scale=ffn_scales[:, ob:ob + 1]),
                        deps=[t_l1, kv["t_scales"], scr_war[0]])
                    t_e2 = sca(lambda e, ob=ob, tg=tg: e.activation(
                        ft2[:], PF[NTG + tg][:, 0:PTW], ACT.Copy,
                        scale=ffn_scales[:, 32 + ob:33 + ob]),
                        deps=[scr_war[1]])
                    t_h = vec(lambda e, ob=ob, tg=tg: e.tensor_mul(
                        ffnA[:, PT * ob + PTW * tg:PT * ob + PTW * (tg + 1)],
                        ft1[:], ft2[:]), deps=[t_e1, t_e2])
                    bk.read(tg, t_e1)
                    bk.read(NTG + tg, t_e2)
                    scr_war[0] = t_h
                    scr_war[1] = t_h
            t_src = ctx.tok("V")
            kv["tapA1"] = tap("A1", lambda: ffnA[:, :], [t_src]) if p == 0 else None
            xs, osg = fxs, fos
            for lay, (wdr, src, dst, scoff, nob) in enumerate(
                    ((h1t, ffnA, ffnB, 64, 32), (h2t, ffnB, ffnA, 96, 32),
                     (w3t, ffnA, None, 128, 8))):
                nxt = None
                for ob in range(nob):
                    par = ob % 2
                    t_wl = dma("gpsimd", lambda e, wdr=wdr, ob=ob, par=par: e.dma_start(
                        out=Wst[par][:, :].rearrange("p (s c) -> p s c", s=32),
                        in_=wdr.ap()[:, 128 * ob:128 * (ob + 1)].rearrange(
                            "(s p) c -> p s c", p=128)), deps=war[par] + [kv["t_cc"]])
                    war[par] = []
                    if lay == 2:
                        t_xl = dma("sync", lambda e, ob=ob, p=p: e.dma_start(
                            out=xs[:, :], in_=x_own.ap()[128 * ob:128 * (ob + 1),
                                                         PT * p:PT * (p + 1)]),
                            deps=[kv.get("t_l4adds")])
                        t_yl0 = dma("sync", lambda e, ob=ob, p=p: e.dma_start(
                            out=fys_b[:, :], in_=py_out.ap()[128 * ob:128 * (ob + 1),
                                                             PT * p:PT * (p + 1)]),
                            deps=[kv["t_cc"], kv.get("t_fysc")])
                        t_yl = vec(lambda e: e.tensor_copy(fys[:, :], fys_b[:, :]),
                                   deps=[t_yl0, kv.get("t_l4adds")])
                        kv["t_fysc"] = t_yl
                    dd = []
                    for tg in range(NTG):
                        dd += bk.acquire(tg)
                    t_srcdep = t_src if lay != 2 else kv["l3_done"]
                    for tg in range(NTG):
                        for k in range(32):
                            last = (k == 31 and tg == NTG - 1)
                            mm(lambda e, k=k, tg=tg, par=par, src=src: e.matmul(
                                PF[tg][:, 0:PTW], Wst[par][:, 128 * k:128 * (k + 1)],
                                src[:, PT * k + PTW * tg:PT * k + PTW * (tg + 1)],
                                start=(k == 0), stop=(k == 31)),
                                deps=(dd + [t_wl, t_srcdep]) if (k == 0 and tg == 0) else (),
                                inc=last)
                    t_mmk = ctx.tok("T")
                    war[par].append(t_mmk)
                    for tg in range(NTG):
                        if lay != 2:
                            scr = (ft1, ft2)[tg % 2]
                            t_gx = sca(lambda e, ob=ob, tg=tg, scoff=scoff, scr=scr:
                                       e.activation(scr[:], PF[tg][:, 0:PTW], ACT.Gelu,
                                                    scale=ffn_scales[:, scoff + ob:scoff + ob + 1]),
                                       deps=[t_mmk, scr_war[2 + tg % 2]])
                            t_cx = vec(lambda e, ob=ob, tg=tg, dst=dst, scr=scr: e.tensor_copy(
                                dst[:, PT * ob + PTW * tg:PT * ob + PTW * (tg + 1)], scr[:]),
                                deps=[t_gx, kv["tapA1"] if lay == 1 else None])
                            bk.read(tg, t_gx)
                            scr_war[2 + tg % 2] = t_cx
                            nxt = t_cx
                        else:
                            oso = PT * (ob % 2)
                            fw = kv.setdefault("fos_war", [None, None])
                            t_y = sca(lambda e, ob=ob, tg=tg, oso=oso: e.activation(
                                osg[:, oso + PTW * tg:oso + PTW * (tg + 1)],
                                PF[tg][:, 0:PTW], ACT.Copy,
                                scale=ffn_scales[:, 128 + ob:129 + ob]),
                                deps=[t_mmk, fw[ob % 2]])
                            bk.read(tg, t_y)
                            t_a1 = vec(lambda e, tg=tg, oso=oso: e.tensor_add(
                                osg[:, oso + PTW * tg:oso + PTW * (tg + 1)],
                                osg[:, oso + PTW * tg:oso + PTW * (tg + 1)],
                                xs[:, PTW * tg:PTW * (tg + 1)]), deps=[t_y, t_xl])
                            t_a2 = vec(lambda e, tg=tg, oso=oso: e.tensor_add(
                                osg[:, oso + PTW * tg:oso + PTW * (tg + 1)],
                                osg[:, oso + PTW * tg:oso + PTW * (tg + 1)],
                                fys[:, PTW * tg:PTW * (tg + 1)]), deps=[t_a1, t_yl])
                    if lay == 2:
                        oso = PT * (ob % 2)
                        kv["t_l4adds"] = ctx.tok("V")
                        kv["fos_war"][ob % 2] = dma(
                            "scalar", lambda e, ob=ob, p=p, oso=oso: e.dma_start(
                                out=out_fm.ap()[128 * ob:128 * (ob + 1),
                                                PT * p:PT * (p + 1)],
                                in_=osg[:, oso:oso + PT]), deps=[kv["t_l4adds"]])
                        kv["t_l4dma"] = kv["fos_war"][ob % 2]
                if lay == 0:
                    if p == 0:
                        tap("B2", lambda: ffnB[:, :], [nxt])
                    t_src = nxt
                elif lay == 1:
                    if p == 0:
                        tap("A3", lambda: ffnA[:, :], [nxt])
                    kv["l3_done"] = nxt
            kv["out_done"] = ctx.tok("DQA")

        def emit_all():
            setup()
            if stop_after == "setup":
                pass
            else:
                phase15()
                war = kv["xq_war"][0]
                kv["xq_war"][0] = []
                for r in range(8):
                    ctx.dma("sync", lambda e, r=r: e.dma_start(
                        out=xq[0][:, TW * r:TW * (r + 1)],
                        in_=x_fm.ap()[128 * r:128 * (r + 1), 0:TW]), deps=war)
                kv["t_x0"] = ctx.tok("DQS")
                n_tiles = 1 if stop_after == "tile0" else NT
                for i in range(n_tiles):
                    emit_tile(i)
                if stop_after not in ("tile0", "p1"):
                    tap("pyin", lambda: py_in.ap(), kv["py_done"])
                    tap("xn", lambda: xn_own[:, :], [kv["t_xn"]])
                    if stop_after != "p15":
                        if stop_after != "baronly":
                            phase2()
                        else:
                            kv["t_cc"] = kv["t_consts"]
                        tap("py", lambda: py_out.ap(), [kv["t_cc"]])
                        if stop_after != "cc":
                            ctx.barrier()
                            if stop_after not in ("bar", "baronly"):
                                emit_ffn_pass(0)
                                if stop_after != "ffn0":
                                    emit_ffn_pass(1)
            if stop_after is not None:
                kv["out_done"] = ctx.dma("scalar", lambda e: e.dma_start(
                    out=out_fm.ap()[0:128, 0:TW], in_=pystage[:, 0:TW]),
                    deps=kv.get("py_done", []))
            for eng in ENGS:
                ctx.op(eng, lambda e: None, [kv["out_done"]], None)

        with nc.Block() as block:
            for engname in ENGS:
                def run_pass(engobj, engname=engname):
                    ctx.begin(engname, engobj)
                    bk.war = {}
                    kv.clear()
                    emit_all()
                getattr(block, engname)(run_pass)
    return nc


def _core_inputs(inp, core):
    b, g = core // 4, core % 4
    x = np.asarray(inp["L0_attn_x"], np.float32)
    T = x.shape[1]
    TOWN = T // 4
    d = {}
    xt = np.ascontiguousarray(x[b].T)
    d["x_fm"] = xt
    d["x_own"] = np.ascontiguousarray(xt[:, TOWN * g:TOWN * (g + 1)])
    d["wq_t"] = np.ascontiguousarray(
        np.asarray(inp["L0_attn_Wq"], np.float32).T[:, 256 * g:256 * (g + 1)])
    wk = np.asarray(inp["L0_attn_Wk"], np.float32).T[:, 64 * g:64 * (g + 1)]
    wv = np.asarray(inp["L0_attn_Wv"], np.float32).T[:, 64 * g:64 * (g + 1)]
    d["wkv_t"] = np.ascontiguousarray(np.concatenate([wk, wv], axis=1))
    wg = np.asarray(inp["L0_attn_Wgate"], np.float32).T[:, 256 * g:256 * (g + 1)]
    wb = np.asarray(inp["L0_attn_Wbeta"], np.float32).T[:, g:g + 1]
    wgk = np.asarray(inp["L0_attn_Wgk"], np.float32).T[:, g:g + 1]
    d["wgbg_t"] = np.ascontiguousarray(np.concatenate([wg, wb, wgk], axis=1))
    d["cwq"] = np.ascontiguousarray(
        np.asarray(inp["L0_attn_cwq"], np.float32)[256 * g:256 * (g + 1)])
    cwk = np.asarray(inp["L0_attn_cwk"], np.float32)[64 * g:64 * (g + 1)]
    cwv = np.asarray(inp["L0_attn_cwv"], np.float32)[64 * g:64 * (g + 1)]
    d["cwkv"] = np.ascontiguousarray(np.concatenate([cwk, cwv], axis=0))
    d["scal"] = np.array([[inp["L0_attn_b_beta"][g], inp["L0_attn_A_log"][g],
                           inp["L0_attn_dt_bias"][g]]], np.float32)
    d["rmsw_row"] = np.ascontiguousarray(
        np.tile(np.asarray(inp["L0_attn_rms_w"], np.float32), 4)[None, :])
    d["wpr_t"] = np.ascontiguousarray(
        np.asarray(inp["L0_attn_Wproj"], np.float32).T[256 * g:256 * (g + 1), :])
    return d


def _shared_inputs(inp):
    d = {}
    d["w1t"] = np.ascontiguousarray(np.asarray(inp["L0_ffn_w1_w"], np.float32).T)
    d["w2t"] = np.ascontiguousarray(np.asarray(inp["L0_ffn_w2_w"], np.float32).T)
    d["h1t"] = np.ascontiguousarray(np.asarray(inp["L0_ffn_h1_w"], np.float32).T)
    d["h2t"] = np.ascontiguousarray(np.asarray(inp["L0_ffn_h2_w"], np.float32).T)
    d["w3t"] = np.ascontiguousarray(np.asarray(inp["L0_ffn_w3_w"], np.float32).T)
    d["sc1"] = np.ascontiguousarray(
        np.asarray(inp["L0_ffn_w1_s"], np.float32).reshape(32, 128).T)
    d["sc2"] = np.ascontiguousarray(
        np.asarray(inp["L0_ffn_w2_s"], np.float32).reshape(32, 128).T)
    d["sch1"] = np.ascontiguousarray(
        np.asarray(inp["L0_ffn_h1_s"], np.float32).reshape(32, 128).T)
    d["sch2"] = np.ascontiguousarray(
        np.asarray(inp["L0_ffn_h2_s"], np.float32).reshape(32, 128).T)
    d["sc3"] = np.ascontiguousarray(
        np.asarray(inp["L0_ffn_w3_s"], np.float32).reshape(8, 128).T)
    d["idn_b"] = np.eye(128).astype(BF)
    d["triu1"] = np.triu(np.ones((128, 128), np.float32))
    d["tril0"] = np.tril(np.ones((128, 128), np.float32), -1)
    csq = np.zeros((2, 128), np.float32)
    csq[0, 0:64] = 0.125
    csq[1, 64:128] = 0.125
    d["csq"] = csq
    cskv = np.zeros((2, 128), np.float32)
    cskv[0, 0:64] = 1.0
    cskv[1, 64:128] = 1.0
    d["cskv"] = cskv
    d["onesrow"] = np.ones((1, TW), np.float32)
    return d


def kernel(**inputs):
    inp = {k: np.asarray(v) for k, v in inputs.items()}
    T = inp["L0_attn_x"].shape[1]
    TOWN = T // 4
    nc = build_program(T=T)
    shared = _shared_inputs(inp)
    in_maps = []
    for core in range(8):
        d = _core_inputs(inp, core)
        d.update(shared)
        in_maps.append(d)
    res = run_bass_kernel_spmd(nc, in_maps, core_ids=list(range(8)))
    out = np.empty((2, T, 1024), np.float32)
    for core in range(8):
        b, g = core // 4, core % 4
        out[b, TOWN * g:TOWN * (g + 1), :] = res.results[core]["out_fm"].T
    return out


# version 15
# speedup vs baseline: 1.0350x; 1.0255x over previous
import contextlib

import numpy as np
import ml_dtypes

import concourse.bass as bass
import concourse.mybir as mybir
from concourse.bass_utils import run_bass_kernel_spmd

F32 = mybir.dt.float32
BF16 = mybir.dt.bfloat16
ALU = mybir.AluOpType
ACT = mybir.ActivationFunctionType
AX = mybir.AxisListType
BF = ml_dtypes.bfloat16

C = 1024
TW = 512

ENGS = ("sync", "tensor", "vector", "scalar", "gpsimd")
OWN_SEM = {"tensor": "T", "vector": "V", "scalar": "S", "gpsimd": "G", "sync": "Y"}


class Ctx:
    def __init__(self, nc, sems):
        self.nc = nc
        self.sems = sems
        self.cur = None
        self.e = None
        self.counts = {}
        self.waited = {}
        self.base_dep = None

    def begin(self, name, engobj):
        self.cur = name
        self.e = engobj
        self.counts = {k: 0 for k in self.sems}
        self.waited = {k: -1 for k in self.sems}
        self.base_dep = None

    def op(self, eng, fn, deps=(), inc=None, inc_n=1):
        tok = None
        if inc is not None:
            self.counts[inc] += inc_n
            tok = (inc, self.counts[inc])
        if self.cur == eng:
            own = OWN_SEM.get(eng)
            deps = list(deps)
            if self.base_dep is not None:
                deps.append(self.base_dep)
            nw = 0
            for dep in deps:
                if dep is None:
                    continue
                s, v = dep
                if v > self.waited[s]:
                    if nw and nw % 2 == 0:
                        self.e.nop(nofuse=True)
                    self.e.wait_ge(self.sems[s], v)
                    self.waited[s] = v
                    nw += 1
            inst = fn(self.e)
            if inc is not None and inst is not None:
                inst.then_inc(self.sems[inc], inc_n)
        return tok

    def mm(self, fn, deps=(), inc=False):
        return self.op("tensor", fn, deps, "T" if inc else None)

    def vec(self, fn, deps=(), inc=True):
        return self.op("vector", fn, deps, "V" if inc else None)

    def sca(self, fn, deps=(), inc=True):
        return self.op("scalar", fn, deps, "S" if inc else None)

    def dma(self, queue, fn, deps=()):
        q = {"sync": "DQS", "scalar": "DQA", "gpsimd": "DQG"}[queue]
        return self.op(queue, fn, deps, q, 16)

    def tok(self, sem):
        return (sem, self.counts[sem])

    def guard(self, engs=("tensor", "vector", "scalar", "sync"),
              sems=("T", "V", "S", "DQS", "DQA")):
        snap = dict(self.counts)
        for eng in engs:
            deps = [(s, snap[s]) for s in sems if s != OWN_SEM.get(eng)]
            self.op(eng, lambda e: None, deps, None)

    def barrier(self):
        self.guard(engs=ENGS, sems=("T", "V", "S", "DQS", "DQA", "DQG"))


class Banks:
    def __init__(self):
        self.war = {}

    def acquire(self, b):
        d = self.war.get(b, [])
        self.war[b] = []
        return d

    def read(self, b, tok):
        if tok is not None:
            self.war.setdefault(b, []).append(tok)


SCHED = "full"


def build_program(T=8192, use_cc=True, stop_after=None, taps=False):
    nc = bass.Bass("TRN2", target_bir_lowering=False,
                   detect_race_conditions=False)
    NT = T // TW
    TOWN = T // 4
    PT = TOWN // 2
    PTW = min(512, PT)
    NTG = PT // PTW

    x_fm = nc.dram_tensor("x_fm", [C, T], F32, kind="ExternalInput")
    x_own = nc.dram_tensor("x_own", [C, TOWN], F32, kind="ExternalInput")
    wq_t = nc.dram_tensor("wq_t", [C, 256], F32, kind="ExternalInput")
    wkv_t = nc.dram_tensor("wkv_t", [C, 128], F32, kind="ExternalInput")
    wgbg_t = nc.dram_tensor("wgbg_t", [C, 258], F32, kind="ExternalInput")
    cwq_d = nc.dram_tensor("cwq", [256, 4], F32, kind="ExternalInput")
    cwkv_d = nc.dram_tensor("cwkv", [128, 4], F32, kind="ExternalInput")
    scal_d = nc.dram_tensor("scal", [1, 3], F32, kind="ExternalInput")
    rmsw_d = nc.dram_tensor("rmsw_row", [1, 256], F32, kind="ExternalInput")
    wpr_t = nc.dram_tensor("wpr_t", [256, C], F32, kind="ExternalInput")
    w1t = nc.dram_tensor("w1t", [C, 4096], F32, kind="ExternalInput")
    w2t = nc.dram_tensor("w2t", [C, 4096], F32, kind="ExternalInput")
    h1t = nc.dram_tensor("h1t", [4096, 4096], F32, kind="ExternalInput")
    h2t = nc.dram_tensor("h2t", [4096, 4096], F32, kind="ExternalInput")
    w3t = nc.dram_tensor("w3t", [4096, C], F32, kind="ExternalInput")
    sc_d = {}
    for nm, w in (("sc1", 32), ("sc2", 32), ("sch1", 32), ("sch2", 32), ("sc3", 8)):
        sc_d[nm] = nc.dram_tensor(nm, [128, w], F32, kind="ExternalInput")
    idn_d = nc.dram_tensor("idn_b", [128, 128], BF16, kind="ExternalInput")
    triu1_d = nc.dram_tensor("triu1", [128, 128], F32, kind="ExternalInput")
    tril0_d = nc.dram_tensor("tril0", [128, 128], F32, kind="ExternalInput")
    csq_d = nc.dram_tensor("csq", [2, 128], F32, kind="ExternalInput")
    cskv_d = nc.dram_tensor("cskv", [2, 128], F32, kind="ExternalInput")
    onesrow_d = nc.dram_tensor("onesrow", [1, TW], F32, kind="ExternalInput")

    out_fm = nc.dram_tensor("out_fm", [C, TOWN], F32, kind="ExternalOutput")
    py_in = nc.dram_tensor("py_in", [4, C, TOWN], BF16, kind="Internal")
    py_out = nc.dram_tensor("py_out", [C, TOWN], BF16, kind="Internal")
    PT_ = TOWN // 2
    tap_d = {}
    if isinstance(taps, (tuple, list, set)):
        shapes = {"A1": ([128, 32 * PT_], BF16), "B2": ([128, 32 * PT_], BF16),
                  "A3": ([128, 32 * PT_], BF16), "xn": ([128, 8 * TOWN], BF16),
                  "py": ([C, TOWN], BF16)}
        for nm in taps:
            tap_d[nm] = nc.dram_tensor("tap_" + nm, list(shapes[nm][0]),
                                       shapes[nm][1], kind="ExternalOutput")
    elif taps == "min":
        for nm in ("A1", "B2", "A3"):
            tap_d[nm] = nc.dram_tensor("tap_" + nm, [128, 32 * PT_], BF16,
                                       kind="ExternalOutput")
        tap_d["xn"] = nc.dram_tensor("tap_xn", [128, 8 * TOWN], BF16,
                                     kind="ExternalOutput")
        tap_d["py"] = nc.dram_tensor("tap_py", [C, TOWN], BF16,
                                     kind="ExternalOutput")
    elif taps:
        tap_d["xn"] = nc.dram_tensor("tap_xn", [128, 8 * TOWN], BF16,
                                     kind="ExternalOutput")
        tap_d["A1"] = nc.dram_tensor("tap_A1", [128, 32 * PT_], BF16,
                                     kind="ExternalOutput")
        tap_d["B2"] = nc.dram_tensor("tap_B2", [128, 32 * PT_], BF16,
                                     kind="ExternalOutput")
        tap_d["A3"] = nc.dram_tensor("tap_A3", [128, 32 * PT_], BF16,
                                     kind="ExternalOutput")
        tap_d["py"] = nc.dram_tensor("tap_py", [C, TOWN], BF16,
                                     kind="ExternalOutput")
        tap_d["pyin"] = nc.dram_tensor("tap_pyin", [4, C, TOWN], BF16,
                                       kind="ExternalOutput")
        for nm, shape, dt_ in (("qb0", [128, TW], BF16), ("kvb", [128, TW], BF16),
                               ("gate", [128, 1024], BF16), ("bg", [128, 8], F32),
                               ("Sf", [64, 64], F32), ("Of", [128, 256], F32),
                               ("ogfm", [128, 1024], BF16), ("xraw", [128, TW], F32),
                               ("rr", [128, TW], F32), ("nrc", [128, 4], F32),
                               ("nbb", [128, 1], F32), ("screp", [128, 3], F32),
                               ("cp0", [128, 8], F32), ("cp1", [128, 8], F32)):
            tap_d[nm] = nc.dram_tensor("tap_" + nm, shape, dt_,
                                       kind="ExternalOutput")

    st = contextlib.ExitStack()
    with st:
        sems = {}
        for s in ("T", "V", "S", "G", "Y", "DQS", "DQA", "DQG", "CC"):
            sems[s] = st.enter_context(nc.semaphore(s))
        ctx = Ctx(nc, sems)
        bk = Banks()

        def sb(stk, name, shape, dtype):
            return stk.enter_context(nc.sbuf_tensor(name, list(shape), dtype))

        PF = [st.enter_context(nc.psum_tensor(f"pf{i}", [128, 512], F32))
              for i in range(6)]
        PG = [st.enter_context(nc.psum_tensor(f"pg{i}", [128, 256], BF16))
              for i in range(2)]

        constpool = sb(st, "constpool", (128, 8), F32)
        CONSTS = {1e-5: 0, 1e-12: 1, 1.0: 2, -1.0: 3, -0.5: 4,
                  1.0 / C: 5, 1.0 / 64: 6, 0.5: 7}
        for v, c in CONSTS.items():
            nc.const_aps.register(F32, v, constpool[:, c:c + 1])
        ffn_scales = sb(st, "ffn_scales", (128, 136), F32)
        xn_own = sb(st, "xn_own", (128, 8 * TOWN), BF16)
        S_f = sb(st, "S_f", (64, 64), F32)
        S_b = sb(st, "S_b", (64, 64), BF16)

        p1 = contextlib.ExitStack()
        idb = sb(p1, "idb", (128, 128), BF16)
        triu1_f = sb(p1, "triu1_f", (128, 128), F32)
        tril0_f = sb(p1, "tril0_f", (128, 128), F32)
        ones_f = sb(p1, "ones_f", (1, 128), F32)
        onescol_b = sb(p1, "onescol_b", (128, 1), BF16)
        blk2_b = sb(p1, "blk2_b", (128, 2), BF16)
        colsel_q = sb(p1, "colsel_q", (2, 128), F32)
        colsel_kv = sb(p1, "colsel_kv", (2, 128), F32)
        kv_rhs = sb(p1, "kv_rhs", (2, TW), F32)
        rmsw_rep = sb(p1, "rmsw_rep", (128, 256), F32)
        srow = sb(p1, "srow", (1, 3), F32)
        screp = sb(p1, "screp", (128, 3), F32)
        negbb_rep = sb(p1, "negbb_rep", (128, 1), F32)
        negexpA = sb(p1, "negexpA", (128, 1), F32)
        wqt_b = sb(p1, "wqt_b", (128, 2048), BF16)
        wkvt_b = sb(p1, "wkvt_b", (128, 1024), BF16)
        wgbg_b = sb(p1, "wgbg_b", (128, 8 * 258), BF16)
        wpr_b = sb(p1, "wpr_b", (128, 2048), BF16)
        cwq0 = sb(p1, "cwq0", (128, 4), F32)
        cwq1 = sb(p1, "cwq1", (128, 4), F32)
        cwkv_s = sb(p1, "cwkv_s", (128, 4), F32)
        wsum_fm = sb(p1, "wsum_fm", (1, 384), BF16)
        wsum_gbg = sb(p1, "wsum_gbg", (1, 258), BF16)
        xq = [sb(p1, f"xq{p}", (128, 8 * TW), F32) for p in range(2)]
        xbf = [sb(p1, f"xbf{p}", (128, 8 * TW), BF16) for p in range(2)]
        xsqb = sb(p1, "xsqb", (128, 8 * TW), BF16)
        m_row = sb(p1, "m_row", (1, TW), F32)
        var_row = sb(p1, "var_row", (1, TW), F32)
        rstd_row = sb(p1, "rstd_row", (1, TW), F32)
        mr_row = sb(p1, "mr_row", (1, TW), F32)
        negm_b = sb(p1, "negm_b", (1, TW), BF16)
        rrep = sb(p1, "rrep", (128, TW), F32)
        mrrep = sb(p1, "mrrep", (128, TW), F32)
        qraw0 = sb(p1, "qraw0", (128, TW + 3), F32)
        qraw1 = sb(p1, "qraw1", (128, TW + 3), F32)
        kvraw = sb(p1, "kvraw", (128, TW + 3), F32)
        qc0 = sb(p1, "qc0", (128, TW), F32)
        qc1 = sb(p1, "qc1", (128, TW), F32)
        kvc = sb(p1, "kvc", (128, TW), F32)
        rsq01 = sb(p1, "rsq01", (2, TW), F32)
        rsq23 = sb(p1, "rsq23", (2, TW), F32)
        TB = []
        for ti in range(2):
            tb = {}
            for nm, shape, dt_ in (
                    ("qb0", (128, TW), BF16), ("qb1", (128, TW), BF16),
                    ("kvb", (128, TW), BF16), ("qodd0", (64, TW), BF16),
                    ("qodd1", (64, TW), BF16), ("gate_tm", (128, 1024), BF16),
                    ("beta_c", (128, 4), F32), ("g_c", (128, 4), F32),
                    ("rcol", (128, 4), F32), ("nrcol", (128, 4), F32),
                    ("ogfm", (128, 1024), BF16)):
                tb[nm] = sb(p1, f"{nm}_t{ti}", shape, dt_)
            TB.append(tb)
        CB = []
        for pi in range(4):
            cb = {}
            for nm, shape, dt_ in (
                    ("KKb", (128, 128), F32), ("dect", (128, 128), F32),
                    ("dmst", (128, 128), F32), ("tmpf", (128, 128), F32),
                    ("M_b", (128, 128), BF16), ("B0_b", (128, 128), BF16),
                    ("Y_b", (128, 128), BF16),
                    ("AB0", (128, 256), BF16), ("AB1", (128, 256), BF16),
                    ("RHS2", (128, 128), BF16), ("khat", (128, 64), BF16),
                    ("U2f", (128, 128), F32), ("amsk", (128, 512), BF16),
                    ("WtU", (128, 128), BF16),
                    ("U_b", (128, 64), BF16), ("O_f", (128, 256), F32),
                    ("Osq", (128, 256), F32), ("rms_in", (128, 4), F32),
                    ("rms4", (128, 4), F32), ("grw", (128, 256), F32),
                    ("og_bf", (128, 256), BF16), ("ebcol", (128, 1), F32),
                    ("dlcol", (128, 1), F32), ("eblcol", (128, 1), F32),
                    ("bcol", (128, 1), F32), ("nbcol", (128, 1), F32),
                    ("bebe", (128, 1), F32), ("brow", (1, 128), F32),
                    ("ebtmp", (128, 256), F32)):
                cb[nm] = sb(p1, f"{nm}{pi}", shape, dt_)
            CB.append(cb)
        pystage = sb(p1, "pystage", (128, 1024), BF16)
        p1.close()

        ffnA = sb(st, "ffnA", (128, 32 * PT), BF16)
        ffnB = sb(st, "ffnB", (128, 32 * PT), BF16)
        Wst = [sb(st, f"fw{p}", (128, 4096), BF16) for p in range(2)]
        ft1 = sb(st, "ffnt1", (128, PTW), F32)
        ft2 = sb(st, "ffnt2", (128, PTW), F32)
        fxs = sb(st, "xstage", (128, PT), F32)
        fys_b = sb(st, "ystage_b", (128, PT), BF16)
        fys = sb(st, "ystage", (128, PT), F32)
        fos = sb(st, "ostage", (128, 2 * PT), F32)

        kv = {}

        def setup():
            dma, mm, vec, sca = ctx.dma, ctx.mm, ctx.vec, ctx.sca
            dma("sync", lambda e: e.dma_start(out=idb[:], in_=idn_d.ap()))
            dma("sync", lambda e: e.dma_start(out=triu1_f[:], in_=triu1_d.ap()))
            dma("sync", lambda e: e.dma_start(out=tril0_f[:], in_=tril0_d.ap()))
            dma("sync", lambda e: e.dma_start(out=srow[:], in_=scal_d.ap()))
            dma("sync", lambda e: e.dma_start(out=cwq0[:], in_=cwq_d.ap()[0:128, :]))
            dma("sync", lambda e: e.dma_start(out=cwq1[:], in_=cwq_d.ap()[128:256, :]))
            dma("sync", lambda e: e.dma_start(out=cwkv_s[:], in_=cwkv_d.ap()))
            dma("sync", lambda e: e.dma_start(out=rmsw_rep[0:1, 0:256], in_=rmsw_d.ap()))
            for j, (nm, c0, w) in enumerate([("sc1", 0, 32), ("sc2", 32, 32),
                                             ("sch1", 64, 32), ("sch2", 96, 32),
                                             ("sc3", 128, 8)]):
                dma("sync", lambda e, nm=nm, c0=c0, w=w: e.dma_start(
                    out=ffn_scales[:, c0:c0 + w], in_=sc_d[nm].ap()))
            dma("sync", lambda e: e.dma_start(out=colsel_q[:], in_=csq_d.ap()))
            dma("sync", lambda e: e.dma_start(out=colsel_kv[:], in_=cskv_d.ap()))
            dma("sync", lambda e: e.dma_start(out=kv_rhs[1:2, 0:TW], in_=onesrow_d.ap()))
            t_cl = ctx.tok("DQS")
            kv["t_scales"] = t_cl
            for r in range(8):
                dma("gpsimd", lambda e, r=r: e.dma_start(
                    out=wqt_b[:, 256 * r:256 * (r + 1)],
                    in_=wq_t.ap()[128 * r:128 * (r + 1), :]))
                dma("gpsimd", lambda e, r=r: e.dma_start(
                    out=wkvt_b[:, 128 * r:128 * (r + 1)],
                    in_=wkv_t.ap()[128 * r:128 * (r + 1), :]))
                dma("gpsimd", lambda e, r=r: e.dma_start(
                    out=wgbg_b[:, 258 * r:258 * (r + 1)],
                    in_=wgbg_t.ap()[128 * r:128 * (r + 1), :]))
            for r in range(2):
                dma("gpsimd", lambda e, r=r: e.dma_start(
                    out=wpr_b[:, 1024 * r:1024 * (r + 1)],
                    in_=wpr_t.ap()[128 * r:128 * (r + 1), :]))
            kv["t_wload"] = ctx.tok("DQG")
            for f in (
                lambda e: e.memset(ones_f[:], 1.0),
                lambda e: e.memset(onescol_b[:], 1.0),
                lambda e: e.memset(blk2_b[:], 0.0),
                lambda e: e.memset(blk2_b[0:64, 0:1], 1.0),
                lambda e: e.memset(blk2_b[64:128, 1:2], 1.0),
                lambda e: e.memset(S_f[:], 0.0),
                lambda e: e.memset(S_b[:], 0.0),
                lambda e: e.memset(qraw0[:, 0:3], 0.0),
                lambda e: e.memset(qraw1[:, 0:3], 0.0),
                lambda e: e.memset(kvraw[:, 0:3], 0.0),
            ):
                vec(f, inc=False)
            for v, c in CONSTS.items():
                vec(lambda e, v=v, c=c: e.memset(constpool[:, c:c + 1], v), inc=False)
            if taps:
                tcp = vec(lambda e: e.tensor_copy(rrep[:, 0:8], constpool[:, :]))
                tap("cp0", lambda: rrep[:, 0:8], [tcp])
            vec(lambda e: e.memset(TB[0]["rcol"][:], 0.0), inc=False)
            t_ms = vec(lambda e: e.memset(TB[1]["rcol"][:], 0.0))
            kv["t_consts"] = t_ms
            kv["t_Sb"] = t_ms
            kv["t_Sf"] = t_ms
            ctx.base_dep = t_ms
            t_mm1 = mm(lambda e: e.matmul(PF[1][:, 0:3], ones_f[:, :], srow[:, :],
                                          start=True, stop=True),
                       deps=[t_cl, t_ms] + bk.acquire(1), inc=True)
            t_ev = vec(lambda e: e.tensor_copy(screp[:], PF[1][:, 0:3]), deps=[t_mm1])
            bk.read(1, t_ev)
            sca(lambda e: e.mul(negbb_rep[:], screp[:, 0:1], -1.0), deps=[t_ev],
                inc=False)
            t_e1 = sca(lambda e: e.activation(negexpA[:], screp[:, 1:2], ACT.Exp),
                       deps=[t_ev])
            t_e2 = vec(lambda e: e.tensor_scalar_mul(negexpA[:], negexpA[:], -1.0),
                       deps=[t_e1])
            kv["t_reps"] = t_e2
            t_mm2 = mm(lambda e: e.matmul(PF[2][:, 0:256], ones_f[:, :],
                                          rmsw_rep[0:1, 0:256], start=True, stop=True),
                       deps=[t_cl, t_ms] + bk.acquire(2), inc=True)
            t_rw = vec(lambda e: e.tensor_copy(rmsw_rep[:], PF[2][:, 0:256]),
                       deps=[t_mm2])
            bk.read(2, t_rw)
            kv["t_rmsw"] = t_rw
            d0 = bk.acquire(0)
            for r in range(8):
                mm(lambda e, r=r: e.matmul(PF[0][0:1, 0:256], onescol_b[:, :],
                                           wqt_b[:, 256 * r:256 * (r + 1)],
                                           start=(r == 0), stop=(r == 7)),
                   deps=([kv["t_wload"], t_ms] + d0) if r == 0 else ())
            for r in range(8):
                mm(lambda e, r=r: e.matmul(PF[0][0:1, 256:384], onescol_b[:, :],
                                           wkvt_b[:, 128 * r:128 * (r + 1)],
                                           start=(r == 0), stop=(r == 7)),
                   inc=(r == 7))
            t_q = ctx.tok("T")
            d3 = bk.acquire(3)
            for r in range(8):
                mm(lambda e, r=r: e.matmul(PF[3][0:1, 0:258], onescol_b[:, :],
                                           wgbg_b[:, 258 * r:258 * (r + 1)],
                                           start=(r == 0), stop=(r == 7)),
                   deps=d3 if r == 0 else (), inc=(r == 7))
            t_g = ctx.tok("T")
            t_w1 = vec(lambda e: e.tensor_copy(wsum_fm[:], PF[0][0:1, 0:384]),
                       deps=[t_q])
            bk.read(0, t_w1)
            t_w2 = vec(lambda e: e.tensor_copy(wsum_gbg[:], PF[3][0:1, 0:258]),
                       deps=[t_g])
            bk.read(3, t_w2)
            kv["t_wsum"] = t_w2
            kv["t_wsumf"] = t_w1
            kv["xq_war"] = [[], []]
            kv["xbf_war"] = [[], []]
            kv["py_done"] = []
            kv["og_done"] = []
            kv["py_war"] = [None, None]
            kv["qscr_war"] = []

        def emit_ln(par, t_xload, t_xn_dep=None):
            mm, vec, sca = ctx.mm, ctx.vec, ctx.sca
            t_sq = sca(lambda e: e.activation(xsqb[:], xq[par][:], ACT.Square),
                       deps=[t_xload])
            t_cast = sca(lambda e: e.copy(xbf[par][:], xq[par][:]),
                         deps=[t_xload] + kv["xbf_war"][par])
            kv["xbf_war"][par] = []
            kv["xq_war"][par] += [t_cast, t_sq]
            d0 = bk.acquire(0)
            d1 = bk.acquire(1)
            for r in range(8):
                mm(lambda e, r=r, par=par: e.matmul(
                    PF[0][0:1, 0:TW], onescol_b[:, :], xbf[par][:, TW * r:TW * (r + 1)],
                    start=(r == 0), stop=(r == 7)),
                    deps=([t_cast, kv["t_consts"]] + d0) if r == 0 else (),
                    inc=(r == 7))
            t_s1 = ctx.tok("T")
            for r in range(8):
                mm(lambda e, r=r: e.matmul(
                    PF[1][0:1, 0:TW], onescol_b[:, :], xsqb[:, TW * r:TW * (r + 1)],
                    start=(r == 0), stop=(r == 7)),
                    deps=([t_sq] + d1) if r == 0 else (), inc=(r == 7))
            t_s2 = ctx.tok("T")
            t_m = sca(lambda e: e.activation(m_row[:], PF[0][0:1, 0:TW], ACT.Copy,
                                             scale=1.0 / C), deps=[t_s1])
            bk.read(0, t_m)
            t_v0 = sca(lambda e: e.activation(var_row[:], PF[1][0:1, 0:TW], ACT.Copy,
                                              scale=1.0 / C), deps=[t_s2])
            bk.read(1, t_v0)
            t_msq = vec(lambda e: e.tensor_mul(mr_row[:], m_row[:], m_row[:]),
                        deps=[t_m])
            t_var = vec(lambda e: e.tensor_sub(var_row[:], var_row[:], mr_row[:]),
                        deps=[t_v0, t_msq])
            t_ln = sca(lambda e: e.activation(rstd_row[:], var_row[:], ACT.Ln,
                                              bias=1e-5), deps=[t_var])
            t_r = sca(lambda e: e.activation(rstd_row[:], rstd_row[:], ACT.Exp,
                                             scale=-0.5), deps=[t_ln])
            return t_cast, t_m, t_r

        def gen_front(i):
            mm, vec, sca, dma = ctx.mm, ctx.vec, ctx.sca, ctx.dma
            par = i % 2
            tb = TB[par]
            kv["tb"] = tb
            qb0, qb1, kvb = tb["qb0"], tb["qb1"], tb["kvb"]
            qodd0, qodd1 = tb["qodd0"], tb["qodd1"]
            gate_tm, beta_c, g_c = tb["gate_tm"], tb["beta_c"], tb["g_c"]
            rcol, nrcol, ogfm = tb["rcol"], tb["nrcol"], tb["ogfm"]
            snap = kv.get(f"tsnap{par}")
            if snap is not None:
                for eng in ("tensor", "vector", "scalar", "sync"):
                    deps = [(sm, snap[sm]) for sm in ("T", "V", "S", "DQS", "DQA")]
                    ctx.op(eng, lambda e: None, deps, None)
            t_xload = kv[f"t_x{i}"]
            if i + 1 < NT:
                npar = (i + 1) % 2
                war = kv["xq_war"][npar]
                kv["xq_war"][npar] = []
                for r in range(8):
                    dma("sync", lambda e, r=r, i=i, npar=npar: e.dma_start(
                        out=xq[npar][:, TW * r:TW * (r + 1)],
                        in_=x_fm.ap()[128 * r:128 * (r + 1), TW * (i + 1):TW * (i + 2)]),
                        deps=war)
                kv[f"t_x{i+1}"] = ctx.tok("DQS")
            t_cast, t_m, t_r = emit_ln(par, t_xload)
            yield
            t_nm = sca(lambda e: e.activation(negm_b[:], m_row[:], ACT.Copy,
                                              scale=-1.0), deps=[t_m])
            d5 = bk.acquire(5)
            t_rrmm = mm(lambda e: e.matmul(PF[5][:, 0:TW], ones_f[:, :], rstd_row[:, :],
                                           start=True, stop=True),
                        deps=[t_r] + d5, inc=True)
            t_rrep = vec(lambda e: e.tensor_copy(rrep[:], PF[5][:, 0:TW]),
                         deps=[t_rrmm])
            bk.read(5, t_rrep)
            d3 = bk.acquire(5)
            for cc in range(4):
                mm(lambda e, cc=cc: e.matmul(PF[5][:, cc:cc + 1],
                                             rstd_row[0:1, 128 * cc:128 * (cc + 1)],
                                             ones_f[0:1, 0:1], start=True, stop=True),
                   deps=([t_r] + d3) if cc == 0 else (), inc=(cc == 3))
            t_rcmm = ctx.tok("T")
            t_rc = vec(lambda e: e.tensor_copy(rcol[:], PF[5][:, 0:4]), deps=[t_rcmm])
            bk.read(5, t_rc)
            t_nrc = vec(lambda e: e.tensor_scalar_mul(nrcol[:], rcol[:], -1.0),
                        deps=[t_rc])
            yield
            proj = []
            for (bnk, coff) in ((0, 0), (1, 128), (2, None)):
                dd = bk.acquire(bnk)
                for r in range(8):
                    if coff is None:
                        lhs = lambda e, r=r: wkvt_b[:, 128 * r:128 * (r + 1)]
                    else:
                        lhs = lambda e, r=r, coff=coff: wqt_b[:, 256 * r + coff:256 * r + coff + 128]
                    mm(lambda e, r=r, par=par, bnk=bnk, lhs=lhs: e.matmul(
                        PF[bnk][:, 0:TW], lhs(e), xbf[par][:, TW * r:TW * (r + 1)],
                        start=(r == 0), stop=False),
                        deps=(dd + [t_cast, kv["t_wsumf"], kv["t_wload"]]) if r == 0 else ())
                woff = {0: (0, 128), 1: (128, 256), 2: (256, 384)}[bnk]
                mm(lambda e, bnk=bnk, woff=woff: e.matmul(
                    PF[bnk][:, 0:TW], wsum_fm[0:1, woff[0]:woff[1]], negm_b[:, :],
                    start=False, stop=True), deps=[t_nm], inc=True)
                proj.append(ctx.tok("T"))
            t_q0p, t_q1p, t_kvp = proj
            evs = []
            for (buf, bnk, tq) in ((qraw0, 0, t_q0p), (qraw1, 1, t_q1p), (kvraw, 2, t_kvp)):
                if i > 0:
                    vec(lambda e, buf=buf: e.tensor_copy(buf[:, 0:3], buf[:, TW:TW + 3]),
                        deps=kv["qscr_war"], inc=False)
                t_e = vec(lambda e, buf=buf, bnk=bnk: e.tensor_mul(
                    buf[:, 3:TW + 3], PF[bnk][:, 0:TW], rrep[:]), deps=[tq, t_rrep])
                bk.read(bnk, t_e)
                evs.append(t_e)
            t_evq0, t_evq1, t_evkv = evs
            kv["qscr_war"] = []
            yield
            gtoks = []
            for cc in range(4):
                bnk = (5, 0, 1, 2)[cc]
                dd = bk.acquire(bnk)
                for r in range(8):
                    mm(lambda e, r=r, cc=cc, par=par, bnk=bnk: e.matmul(
                        PF[bnk][:, 0:258],
                        xbf[par][:, TW * r + 128 * cc:TW * r + 128 * cc + 128],
                        wgbg_b[:, 258 * r:258 * (r + 1)], start=(r == 0), stop=False),
                        deps=(dd + [kv["t_wsum"]]) if r == 0 else ())
                mm(lambda e, cc=cc, bnk=bnk: e.matmul(
                    PF[bnk][:, 0:258], negm_b[0:1, 128 * cc:128 * (cc + 1)],
                    wsum_gbg[:, :], start=False, stop=True), deps=[t_nm], inc=True)
                gtoks.append((cc, bnk, ctx.tok("T")))
            for (cc, bnk, t_gc) in gtoks:
                t_si = sca(lambda e, cc=cc, bnk=bnk: e.activation(
                    gate_tm[:, 256 * cc:256 * (cc + 1)], PF[bnk][:, 0:256], ACT.Silu,
                    scale=rcol[:, cc:cc + 1]), deps=[t_gc, t_rc])
                bk.read(bnk, t_si)
            for (cc, bnk, t_gc) in gtoks:
                t_be = sca(lambda e, cc=cc, bnk=bnk: e.activation(
                    beta_c[:, cc:cc + 1], PF[bnk][:, 256:257], ACT.Exp,
                    scale=nrcol[:, cc:cc + 1], bias=negbb_rep[:, :]),
                    deps=[t_gc, t_nrc, kv["t_reps"]])
                t_b2 = vec(lambda e, cc=cc: e.tensor_scalar_add(
                    beta_c[:, cc:cc + 1], beta_c[:, cc:cc + 1], 1.0), deps=[t_be])
                t_b3 = vec(lambda e, cc=cc: e.reciprocal(
                    beta_c[:, cc:cc + 1], beta_c[:, cc:cc + 1]), deps=[t_b2])
                t_ge = sca(lambda e, cc=cc, bnk=bnk: e.activation(
                    g_c[:, cc:cc + 1], PF[bnk][:, 257:258], ACT.Exp,
                    scale=rcol[:, cc:cc + 1], bias=screp[:, 2:3]), deps=[t_gc])
                t_g2 = sca(lambda e, cc=cc: e.activation(
                    g_c[:, cc:cc + 1], g_c[:, cc:cc + 1], ACT.Ln, bias=1.0),
                    deps=[t_ge])
                t_g3 = vec(lambda e, cc=cc: e.tensor_mul(
                    g_c[:, cc:cc + 1], g_c[:, cc:cc + 1], negexpA[:]), deps=[t_g2])
                bk.read(bnk, t_ge)
                kv[f"t_beta{cc}"] = t_b3
                kv[f"t_g{cc}"] = t_g3
            yield
            convs = []
            for (cin, cout, w, tdep) in ((qraw0, qc0, cwq0, t_evq0),
                                         (qraw1, qc1, cwq1, t_evq1),
                                         (kvraw, kvc, cwkv_s, t_evkv)):
                t_cj = vec(lambda e, cin=cin, cout=cout, w=w: e.tensor_scalar_mul(
                    cout[:], cin[:, 0:TW], w[:, 0:1]),
                    deps=[tdep, kv["t_scales"]])
                for j in range(1, 4):
                    t_cj = vec(lambda e, cin=cin, cout=cout, w=w, j=j: e.scalar_tensor_tensor(
                        cout[:], cin[:, j:TW + j], w[:, j:j + 1], cout[:],
                        op0=ALU.mult, op1=ALU.add), deps=[t_cj])
                convs.append(t_cj)
            t_cq0, t_cq1, t_ckv = convs
            kv["qscr_war"] = list(convs)
            yield
            t_sq0 = sca(lambda e: e.activation(qb0[:], qc0[:], ACT.Square),
                        deps=[t_cq0])
            t_sq1 = sca(lambda e: e.activation(qb1[:], qc1[:], ACT.Square),
                        deps=[t_cq1])
            t_sqk = sca(lambda e: e.activation(kvb[0:64, :], kvc[0:64, :], ACT.Square),
                        deps=[t_ckv])
            d0 = bk.acquire(0)
            t_ssq0 = mm(lambda e: e.matmul(PF[0][0:2, 0:TW], blk2_b[:, :], qb0[:, :],
                                           start=True, stop=True),
                        deps=[t_sq0] + d0, inc=True)
            d1 = bk.acquire(1)
            t_ssq1 = mm(lambda e: e.matmul(PF[1][0:2, 0:TW], blk2_b[:, :], qb1[:, :],
                                           start=True, stop=True),
                        deps=[t_sq1] + d1, inc=True)
            d4 = bk.acquire(2)
            t_ssqk = mm(lambda e: e.matmul(PF[2][0:1, 0:TW], onescol_b[0:64, :],
                                           kvb[0:64, :], start=True, stop=True),
                        deps=[t_sqk] + d4, inc=True)
            t_l0 = sca(lambda e: e.activation(rsq01[:], PF[0][0:2, 0:TW], ACT.Ln,
                                              bias=1e-12), deps=[t_ssq0])
            bk.read(0, t_l0)
            t_r0 = sca(lambda e: e.activation(rsq01[:], rsq01[:], ACT.Exp, scale=-0.5),
                       deps=[t_l0])
            t_l1 = sca(lambda e: e.activation(rsq23[:], PF[1][0:2, 0:TW], ACT.Ln,
                                              bias=1e-12), deps=[t_ssq1])
            bk.read(1, t_l1)
            t_r1 = sca(lambda e: e.activation(rsq23[:], rsq23[:], ACT.Exp, scale=-0.5),
                       deps=[t_l1])
            t_lk = sca(lambda e: e.activation(kv_rhs[0:1, 0:TW], PF[2][0:1, 0:TW],
                                              ACT.Ln, bias=1e-12), deps=[t_ssqk])
            bk.read(2, t_lk)
            t_rk = sca(lambda e: e.activation(kv_rhs[0:1, 0:TW], kv_rhs[0:1, 0:TW],
                                              ACT.Exp, scale=-0.5), deps=[t_lk])
            qbt = []
            for (t_seed, rr, sel, src, dst, t_scr) in (
                    (t_r0, rsq01, colsel_q, qc0, qb0, t_sq0),
                    (t_r1, rsq23, colsel_q, qc1, qb1, t_sq1),
                    (t_rk, kv_rhs, colsel_kv, kvc, kvb, t_sqk)):
                d5 = bk.acquire(5)
                t_bq = mm(lambda e, sel=sel, rr=rr: e.matmul(
                    PF[5][:, 0:TW], sel[:, :], rr[:, :], start=True, stop=True),
                    deps=[t_seed] + d5, inc=True)
                t_qb = vec(lambda e, src=src, dst=dst: e.tensor_mul(
                    dst[:], src[:], PF[5][:, 0:TW]), deps=[t_bq, t_scr])
                bk.read(5, t_qb)
                qbt.append(t_qb)
            t_qb0, t_qb1, t_kvb = qbt
            t_o0 = dma("sync", lambda e: e.dma_start(out=qodd0[:, :], in_=qb0[64:128, :]),
                       deps=[t_qb0])
            t_o1 = dma("sync", lambda e: e.dma_start(out=qodd1[:, :], in_=qb1[64:128, :]),
                       deps=[t_qb1])
            kv["t_qb"] = (t_qb0, t_qb1, t_kvb, t_o0, t_o1)
            kv["xbf_war"][par].append(ctx.tok("T"))
            if i == 0 and taps:
                tap("cp1", lambda: constpool[:, :], [t_nrc])
                tap("nrc", lambda: nrcol[:, :], [t_nrc])
                tap("nbb", lambda: negbb_rep[:, :], [kv["t_reps"]])
                tap("screp", lambda: screp[:, :], [kv["t_reps"]])
                tap("qb0", lambda: qb0[:, :], [t_qb0])
                tap("kvb", lambda: kvb[:, :], [t_kvb])
                tap("xraw", lambda: qc0[:, :], [t_cq0])
                tap("rr", lambda: rrep[:, :], [t_rrep])
                tap("gate", lambda: gate_tm[:, :],
                    [kv["t_beta3"], kv["t_g3"], ctx.tok("S")])
                tap("bg", lambda: beta_c[:, :], [kv["t_beta3"]],
                    dst=lambda: tap_d["bg"].ap()[:, 0:4])
                tap("bg", lambda: g_c[:, :], [kv["t_g3"]],
                    dst=lambda: tap_d["bg"].ap()[:, 4:8])
        def emit_tile_rest(i):
            mm, vec, sca, dma = ctx.mm, ctx.vec, ctx.sca, ctx.dma
            par = i % 2
            tb = TB[par]
            ogfm = tb["ogfm"]
            if SCHED == "seq":
                for cc in range(4):
                    drive(gen_indep(i, cc, tb))
                    drive(gen_chain(i, cc, tb))
            elif SCHED == "pair":
                drive(gen_indep(i, 0, tb), gen_indep(i, 1, tb))
                drive(gen_chain(i, 0, tb))
                drive(gen_indep(i, 2, tb))
                drive(gen_chain(i, 1, tb))
                drive(gen_indep(i, 3, tb))
                drive(gen_chain(i, 2, tb))
                drive(gen_chain(i, 3, tb))
            else:
                drive(gen_indep(i, 0, tb), gen_indep(i, 1, tb))
                drive(gen_chain(i, 0, tb), gen_indep(i, 2, tb))
                drive(gen_chain(i, 1, tb), gen_indep(i, 3, tb))
                fnext = gen_front(i + 1) if i + 1 < NT else None
                fnext = drive2(gen_chain(i, 2, tb), fnext)
                fnext = drive2(gen_chain(i, 3, tb), fnext)
                if fnext is not None:
                    drive(fnext)
            if i == 0 and taps:
                tap("Sf", lambda: S_f[:, :], [kv["t_Sf"]])
                tap("Of", lambda: CB[1]["O_f"][:, :], [kv["t_of0"]])
                tap("ogfm", lambda: ogfm[:, :], list(kv["og_done"]))
            quarter, qoff = (TW * i) // TOWN, (TW * i) % TOWN
            ogdeps = kv["og_done"]
            kv["og_done"] = []
            for r in range(8):
                bnk = 1 if r % 2 == 0 else 2
                half = r % 2
                dd = bk.acquire(bnk)
                for j in range(2):
                    mm(lambda e, r=r, j=j, bnk=bnk: e.matmul(
                        PF[bnk][:, 0:TW],
                        wpr_b[:, 1024 * j + 128 * r:1024 * j + 128 * (r + 1)],
                        ogfm[:, TW * j:TW * (j + 1)], start=(j == 0), stop=(j == 1)),
                        deps=(dd + ogdeps + [kv["t_wload"]]) if j == 0 else (),
                        inc=(j == 1))
                t_pm = ctx.tok("T")
                t_pe = vec(lambda e, bnk=bnk, half=half: e.tensor_copy(
                    pystage[:, TW * half:TW * (half + 1)], PF[bnk][:, 0:TW]),
                    deps=[t_pm, kv["py_war"][half]])
                bk.read(bnk, t_pe)
                t_pd = dma("scalar", lambda e, r=r, quarter=quarter, qoff=qoff, half=half:
                           e.dma_start(
                               out=py_in.ap()[quarter, 128 * r:128 * (r + 1),
                                              qoff:qoff + TW],
                               in_=pystage[:, TW * half:TW * (half + 1)]),
                           deps=[t_pe])
                kv["py_war"][half] = t_pd
                kv["py_done"].append(t_pd)
            kv[f"tsnap{par}"] = dict(ctx.counts)

        def drive2(primary, filler):
            while True:
                try:
                    next(primary)
                except StopIteration:
                    break
                if filler is not None:
                    try:
                        next(filler)
                    except StopIteration:
                        filler = None
            return filler

        def drive(*gens):
            gens = [g for g in gens if g is not None]
            while gens:
                nxt = []
                for g in gens:
                    try:
                        next(g)
                        nxt.append(g)
                    except StopIteration:
                        pass
                gens = nxt

        def gen_indep(i, cc, tb):
            mm, vec, sca = ctx.mm, ctx.vec, ctx.sca
            qb0, qb1, kvb = tb["qb0"], tb["qb1"], tb["kvb"]
            qodd0, qodd1 = tb["qodd0"], tb["qodd1"]
            gate_tm, beta_c, g_c = tb["gate_tm"], tb["beta_c"], tb["g_c"]
            b = CB[cc]
            BA, BB = (0, 1) if cc % 2 == 0 else (2, 5)
            BC = BB
            gpi = cc % 2
            sl = slice(128 * cc, 128 * (cc + 1))
            t_qb0, t_qb1, t_kvb, t_o0, t_o1 = kv["t_qb"]
            t_beta = kv[f"t_beta{cc}"]
            t_g = kv[f"t_g{cc}"]
            d0 = bk.acquire(BA)
            t_kk = mm(lambda e: e.matmul(PF[BA][:, 0:128], kvb[0:64, sl], kvb[0:64, sl],
                                         start=True, stop=True),
                      deps=[t_kvb] + d0, inc=True)
            t_kkb = sca(lambda e: e.activation(b["KKb"][:], PF[BA][:, 0:128], ACT.Copy,
                                               scale=beta_c[:, cc:cc + 1]),
                        deps=[t_kk, t_beta])
            bk.read(BA, t_kkb)
            yield
            d1 = bk.acquire(BB)
            mm(lambda e: e.matmul(PF[BB][:, 0:1], triu1_f[:, :], g_c[:, cc:cc + 1],
                                  start=True, stop=True), deps=[t_g] + d1)
            mm(lambda e: e.matmul(PF[BB][:, 64:65], tril0_f[:, :], g_c[:, cc:cc + 1],
                                  start=True, stop=True))
            t_cs = mm(lambda e: e.matmul(PF[BB][0:1, 128:256], g_c[:, cc:cc + 1],
                                         triu1_f[:, :], start=True, stop=True), inc=True)
            t_eb = sca(lambda e: e.activation(b["ebcol"][:], PF[BB][:, 0:1], ACT.Exp),
                       deps=[t_cs])
            t_dl = sca(lambda e: e.activation(b["dlcol"][:], PF[BB][:, 64:65], ACT.Exp),
                       deps=[t_cs])
            t_bc = sca(lambda e: e.copy(b["bcol"][:], PF[BB][:, 0:1]), deps=[t_cs])
            t_nbc = sca(lambda e: e.mul(b["nbcol"][:], PF[BB][:, 0:1], -1.0), deps=[t_cs])
            t_brr = sca(lambda e: e.copy(b["brow"][:], PF[BB][0:1, 128:256]), deps=[t_cs])
            bk.read(BB, t_brr)
            bk.read(BB, t_nbc)
            yield
            t_ebl = vec(lambda e: e.tensor_mul(b["eblcol"][:], b["ebcol"][:], b["dlcol"][:]),
                        deps=[t_eb, t_dl])
            t_bb = vec(lambda e: e.tensor_mul(b["bebe"][:], beta_c[:, cc:cc + 1],
                                              b["ebcol"][:]), deps=[t_eb, t_beta])
            d2 = bk.acquire(BC)
            t_df = mm(lambda e: e.matmul(PF[BC][:, 0:128], ones_f[:, :], b["brow"][:, :],
                                         start=True, stop=True),
                      deps=[t_brr] + d2, inc=True)
            t_dts = sca(lambda e: e.activation(b["dect"][:], PF[BC][:, 0:128], ACT.Exp,
                                               scale=-1.0, bias=b["bcol"][:, :]),
                        deps=[t_df, t_bc])
            t_dst = sca(lambda e: e.activation(b["dmst"][:], PF[BC][:, 0:128], ACT.Exp,
                                               scale=1.0, bias=b["nbcol"][:, :]),
                        deps=[t_df, t_nbc])
            bk.read(BC, t_dts)
            bk.read(BC, t_dst)
            yield
            t_dm = vec(lambda e: e.tensor_mul(b["dmst"][:], b["dmst"][:], triu1_f[:]),
                       deps=[t_dst])
            t_m1 = vec(lambda e: e.tensor_mul(b["tmpf"][:], b["dect"][:], b["KKb"][:]),
                       deps=[t_dts, t_kkb])
            t_m2 = vec(lambda e: e.tensor_mul(b["M_b"][:], b["tmpf"][:], tril0_f[:]),
                       deps=[t_m1])
            yield
            dg0 = bk.acquire(6 + gpi)
            t_tm = mm(lambda e: e.transpose(PG[gpi][:, 0:128], b["M_b"][:, :], idb[:, :]),
                      deps=[t_m2] + dg0, inc=True)
            t_b0 = vec(lambda e: e.tensor_copy(b["B0_b"][:], PG[gpi][:, 0:128]),
                       deps=[t_tm])
            bk.read(6 + gpi, t_b0)
            t_Y = vec(lambda e: e.tensor_sub(b["Y_b"][:], idb[:], b["B0_b"][:]),
                      deps=[t_b0])
            yield
            dg1 = bk.acquire(6 + gpi)
            t_tk = mm(lambda e: e.transpose(PG[gpi][:, 0:128], kvb[:, sl], idb[:, :]),
                      deps=[t_kvb] + dg1, inc=True)
            t_r1 = vec(lambda e: e.tensor_scalar_mul(b["RHS2"][:, 0:64], PG[gpi][:, 0:64],
                                                     b["bebe"][:, :]), deps=[t_tk, t_bb])
            t_r2 = vec(lambda e: e.tensor_scalar_mul(b["RHS2"][:, 64:128],
                                                     PG[gpi][:, 64:128],
                                                     beta_c[:, cc:cc + 1]),
                       deps=[t_tk, t_beta])
            t_kh = vec(lambda e: e.tensor_scalar_mul(b["khat"][:], PG[gpi][:, 0:64],
                                                     b["dlcol"][:, :]), deps=[t_tk, t_dl])
            bk.read(6 + gpi, t_r1)
            bk.read(6 + gpi, t_r2)
            bk.read(6 + gpi, t_kh)
            yield
            Acur, Bcur = b["M_b"][:, 0:128], b["B0_b"][:, 0:128]
            t_A = [t_m2, t_b0]
            for kk in range(1, 7):
                ABn = (b["AB0"], b["AB1"])[kk % 2]
                d0 = bk.acquire(BA)
                t_am = mm(lambda e, Bcur=Bcur, Acur=Acur: e.matmul(
                    PF[BA][:, 0:128], Bcur, Acur, start=True, stop=True),
                    deps=t_A + d0, inc=(kk == 6))
                if kk < 6:
                    t_am = mm(lambda e, Acur=Acur, Bcur=Bcur: e.matmul(
                        PF[BA][:, 128:256], Acur, Bcur, start=True, stop=True),
                        inc=True)
                    t_An = vec(lambda e, ABn=ABn: e.tensor_copy(ABn[:, 0:256],
                                                               PF[BA][:, 0:256]),
                               deps=[t_am])
                else:
                    t_An = vec(lambda e, ABn=ABn: e.tensor_copy(ABn[:, 0:128],
                                                               PF[BA][:, 0:128]),
                               deps=[t_am])
                bk.read(BA, t_An)
                yield
                d2 = bk.acquire(BC)
                t_ym = mm(lambda e, ABn=ABn: e.matmul(
                    PF[BC][:, 0:128], ABn[:, 0:128], b["Y_b"][:, :],
                    start=True, stop=True),
                    deps=[t_An, t_Y] + d2, inc=True)
                t_Y = vec(lambda e: e.tensor_add(b["Y_b"][:], b["Y_b"][:],
                                                 PF[BC][:, 0:128]),
                          deps=[t_ym])
                bk.read(BC, t_Y)
                Acur, Bcur, t_A = ABn[:, 0:128], ABn[:, 128:256], [t_An]
                yield
            d0 = bk.acquire(BA)
            t_u2m = mm(lambda e: e.matmul(PF[BA][:, 0:128], b["Y_b"][:, :], b["RHS2"][:, :],
                                          start=True, stop=True),
                       deps=[t_Y, t_r1, t_r2] + d0, inc=True)
            t_u2f = vec(lambda e: e.tensor_copy(b["U2f"][:], PF[BA][:, 0:128]),
                        deps=[t_u2m])
            bk.read(BA, t_u2f)
            t_u2fin = t_u2f
            yield
            d1 = bk.acquire(BB)
            t_u2t = mm(lambda e: e.matmul(PF[BB][:, 0:128], b["RHS2"][:, :], b["Y_b"][:, :],
                                          start=True, stop=True), deps=d1, inc=True)
            t_wt = vec(lambda e: e.tensor_copy(b["WtU"][0:64, :], PF[BB][0:64, 0:128]),
                       deps=[t_u2t])
            bk.read(BB, t_wt)
            yield
            qsrc = ((qb0, t_qb0), (qodd0, t_o0), (qb1, t_qb1), (qodd1, t_o1))
            d0 = bk.acquire(BC)
            for j in range(4):
                qt, t_qs = qsrc[j]
                mm(lambda e, qt=qt, j=j: e.matmul(
                    PF[BC][:, 128 * j:128 * (j + 1)], kvb[0:64, sl], qt[0:64, sl],
                    start=True, stop=True),
                    deps=([t_qs, t_kvb] + d0) if j == 0 else [t_qs], inc=(j == 3))
            t_kq = ctx.tok("T")
            yield
            t_am = vec(lambda e: e.tensor_tensor(
                out=b["amsk"][:, 0:512].rearrange("p (r c) -> p r c", r=4),
                in0=PF[BC][:, 0:512].rearrange("p (r c) -> p r c", r=4),
                in1=b["dmst"][:].rearrange("p (x c) -> p x c", x=1).broadcast_to(
                    [128, 4, 128]),
                op=ALU.mult), deps=[t_kq, t_dm])
            bk.read(BC, t_am)
            for j in range(4):
                kv[f"t_amsk{cc}_{j}"] = t_am
            yield
            t_grw = vec(lambda e, cc=cc: e.tensor_mul(
                b["grw"][:], gate_tm[:, 256 * cc:256 * (cc + 1)], rmsw_rep[:]),
                deps=[kv["t_rmsw"], kv[f"t_beta{cc}"]])
            kv[f"t_ind{cc}"] = dict(t_wt=t_wt, t_u2fin=t_u2fin, t_kh=t_kh,
                                    t_eb=t_eb, t_ebl=t_ebl, t_grw=t_grw,
                                    t_dm=t_dm)

        def gen_chain(i, cc, tb):
            mm, vec, sca = ctx.mm, ctx.vec, ctx.sca
            qb0, qb1, kvb = tb["qb0"], tb["qb1"], tb["kvb"]
            qodd0, qodd1 = tb["qodd0"], tb["qodd1"]
            gate_tm, beta_c, g_c = tb["gate_tm"], tb["beta_c"], tb["g_c"]
            ogfm = tb["ogfm"]
            b = CB[cc]
            BA, BB = 3, 4
            gpi = (cc + 1) % 2
            ind = kv[f"t_ind{cc}"]
            sl = slice(128 * cc, 128 * (cc + 1))
            t_qb0, t_qb1, t_kvb, t_o0, t_o1 = kv["t_qb"]
            d0 = bk.acquire(BA)
            t_ws = mm(lambda e: e.matmul(PF[BA][:, 0:64], b["WtU"][0:64, :],
                                         S_b[0:64, 0:64], start=True, stop=True),
                      deps=[ind["t_wt"], kv["t_Sb"]] + d0, inc=True)
            t_ub = vec(lambda e: e.scalar_tensor_tensor(
                b["U_b"][:], PF[BA][:, 0:64], -1.0, b["U2f"][:, 64:128],
                op0=ALU.mult, op1=ALU.add), deps=[t_ws, ind["t_u2fin"]])
            bk.read(BA, t_ub)
            yield
            qsrc = ((qb0, t_qb0), (qodd0, t_o0), (qb1, t_qb1), (qodd1, t_o1))
            d4 = bk.acquire(BB)
            d5 = bk.acquire(BA)
            for j in range(4):
                qt, t_qs = qsrc[j]
                mm(lambda e, j=j: e.matmul(
                    PF[BB][:, 64 * j:64 * (j + 1)], b["amsk"][:, 128 * j:128 * (j + 1)],
                    b["U_b"][:, :], start=True, stop=True),
                    deps=([kv[f"t_amsk{cc}_0"], t_ub] + d4) if j == 0 else (),
                    inc=(j == 3))
                if j == 0:
                    t_o1m = None
                mm(lambda e, qt=qt, j=j: e.matmul(
                    PF[BA][:, 64 * j:64 * (j + 1)], qt[0:64, sl], S_b[0:64, 0:64],
                    start=True, stop=True),
                    deps=([kv["t_Sb"]] + d5) if j == 0 else (), inc=(j == 3))
                if j == 1:
                    yield
            t_omm = ctx.tok("T")
            yield
            t_ebs = sca(lambda e: e.activation(b["ebtmp"][:, 0:256], PF[BA][:, 0:256],
                                               ACT.Copy, scale=b["ebcol"][:, :]),
                        deps=[t_omm, ind["t_eb"]])
            t_of = vec(lambda e: e.tensor_add(b["O_f"][:, 0:256], PF[BB][:, 0:256],
                                              b["ebtmp"][:, 0:256]),
                       deps=[t_omm, t_ebs])
            bk.read(BA, t_ebs)
            bk.read(BB, t_of)
            for j in range(4):
                kv[f"t_of{j}"] = t_of
            yield
            d0 = bk.acquire(BA)
            t_ku = mm(lambda e: e.matmul(PF[BA][0:64, 0:64], b["khat"][:, 0:64],
                                         b["U_b"][:, :], start=True, stop=True),
                      deps=[ind["t_kh"], t_ub] + d0, inc=True)
            t_Sf = vec(lambda e: e.scalar_tensor_tensor(
                S_f[:], S_f[:], b["eblcol"][0:64, :], PF[BA][0:64, 0:64],
                op0=ALU.mult, op1=ALU.add), deps=[t_ku, ind["t_ebl"], kv["t_Sf"]])
            bk.read(BA, t_Sf)
            t_Sb = vec(lambda e: e.tensor_copy(S_b[:], S_f[:]), deps=[t_Sf])
            kv["t_Sf"] = t_Sf
            kv["t_Sb"] = t_Sb
            yield
            t_osq = sca(lambda e: e.activation(b["Osq"][:], b["O_f"][:], ACT.Square),
                        deps=[kv["t_of0"]])
            t_red = vec(lambda e: e.reduce_sum(
                b["rms_in"][:, 0:4].rearrange("p (r x) -> p r x", x=1),
                b["Osq"][:, 0:256].rearrange("p (r c) -> p r c", r=4), AX.X),
                deps=[t_osq])
            t_rl = sca(lambda e: e.activation(b["rms4"][:], b["rms_in"][:], ACT.Ln,
                                              scale=1.0 / 64, bias=1e-5), deps=[t_red])
            t_rr = sca(lambda e: e.activation(b["rms4"][:], b["rms4"][:], ACT.Exp,
                                              scale=-0.5), deps=[t_rl])
            yield
            t_og1 = vec(lambda e: e.tensor_tensor(
                out=b["ebtmp"][:, 0:256].rearrange("p (r c) -> p r c", r=4),
                in0=b["O_f"][:, 0:256].rearrange("p (r c) -> p r c", r=4),
                in1=b["rms4"][:, 0:4].rearrange("p (r x) -> p r x", x=1).broadcast_to(
                    [128, 4, 64]),
                op=ALU.mult), deps=[t_rr, kv["t_of0"]])
            t_og = vec(lambda e: e.tensor_mul(b["og_bf"][:, 0:256],
                                              b["ebtmp"][:, 0:256], b["grw"][:, 0:256]),
                       deps=[t_og1, ind["t_grw"]])
            yield
            for half in range(2):
                dg = bk.acquire(6 + gpi)
                t_t = mm(lambda e, half=half: e.transpose(
                    PG[gpi][:, 0:128], b["og_bf"][:, 128 * half:128 * (half + 1)],
                    idb[:, :]), deps=[t_og] + dg, inc=True)
                t_e = vec(lambda e, half=half, cc=cc: e.tensor_copy(
                    ogfm[:, 512 * half + 128 * cc:512 * half + 128 * (cc + 1)],
                    PG[gpi][:, 0:128]), deps=[t_t])
                bk.read(6 + gpi, t_e)
                kv["og_done"].append(t_e)
                yield

        def phase15():
            mm, vec, sca, dma = ctx.mm, ctx.vec, ctx.sca, ctx.dma
            for ii in range(TOWN // TW):
                par = ii % 2
                ctx.guard()
                war = kv["xq_war"][par]
                kv["xq_war"][par] = []
                for r in range(8):
                    dma("sync", lambda e, r=r, ii=ii, par=par: e.dma_start(
                        out=xq[par][:, TW * r:TW * (r + 1)],
                        in_=x_own.ap()[128 * r:128 * (r + 1), TW * ii:TW * (ii + 1)]),
                        deps=war)
                t_xl = ctx.tok("DQS")
                t_cast, t_m, t_r = emit_ln(par, t_xl)
                t_mr = vec(lambda e: e.tensor_mul(mr_row[:], m_row[:], rstd_row[:]),
                           deps=[t_r])
                d5 = bk.acquire(5)
                t_rr = mm(lambda e: e.matmul(PF[5][:, 0:TW], ones_f[:, :],
                                             rstd_row[:, :], start=True, stop=True),
                          deps=[t_r] + d5, inc=True)
                t_rrep = vec(lambda e: e.tensor_copy(rrep[:], PF[5][:, 0:TW]),
                             deps=[t_rr])
                bk.read(5, t_rrep)
                d5 = bk.acquire(5)
                t_mm2 = mm(lambda e: e.matmul(PF[5][:, 0:TW], ones_f[:, :],
                                              mr_row[:, :], start=True, stop=True),
                           deps=[t_mr] + d5, inc=True)
                t_mrep = vec(lambda e: e.tensor_copy(mrrep[:], PF[5][:, 0:TW]),
                             deps=[t_mm2])
                bk.read(5, t_mrep)
                for r in range(8):
                    t_mx = vec(lambda e, r=r, par=par: e.tensor_mul(
                        qc0[:], xq[par][:, TW * r:TW * (r + 1)], rrep[:]),
                        deps=[t_rrep])
                    t_x = vec(lambda e, r=r, ii=ii: e.tensor_sub(
                        xn_own[:, TOWN * r + TW * ii:TOWN * r + TW * (ii + 1)],
                        qc0[:], mrrep[:]), deps=[t_mrep, t_mx])
                    kv["xq_war"][par].append(t_x)
            kv["t_xn"] = ctx.tok("V")

        def phase2():
            if use_cc:
                kv["t_cc"] = ctx.op("gpsimd", lambda e: e.collective_compute(
                    "ReduceScatter", ALU.add,
                    replica_groups=[[0, 1, 2, 3], [4, 5, 6, 7]],
                    ins=[py_in.ap().opt()], outs=[py_out.ap().opt()]),
                    deps=kv["py_done"], inc="CC", inc_n=1)
            else:
                kv["t_cc"] = ctx.op("gpsimd", lambda e: e.dma_start(
                    out=py_out.ap()[:, :], in_=py_in.ap()[0, :, :]),
                    deps=kv["py_done"], inc="CC", inc_n=1)

        def tap(name, src_ap_fn, deps, dst=None):
            if not taps or name not in tap_d:
                return None
            return ctx.dma("sync", lambda e: e.dma_start(
                out=tap_d[name].ap() if dst is None else dst(),
                in_=src_ap_fn()), deps=deps)

        def emit_ffn_pass(p):
            mm, vec, sca, dma = ctx.mm, ctx.vec, ctx.sca, ctx.dma
            war = kv.setdefault("fw_war", [[], []])
            scr_war = kv.setdefault("fscr_war", [None, None, None, None])
            l1war0 = kv.setdefault("l1_war", [[], [], [], []])
            if p == 1:
                for wi in range(2):
                    for slot in range(2):
                        l1war0[2 * wi + slot] = l1war0[2 * wi + slot] + war[wi]
            l1war = kv.setdefault("l1_war", [[], [], [], []])
            for ob in range(32):
                slot = ob % 2
                co = 1024 * slot
                wt = [None, None]
                for wi, wdr in enumerate((w1t, w2t)):
                    wt[wi] = dma("gpsimd", lambda e, wdr=wdr, ob=ob, wi=wi, co=co:
                                 e.dma_start(
                                     out=Wst[wi][:, co:co + 1024].rearrange(
                                         "p (s c) -> p s c", s=8),
                                     in_=wdr.ap()[:, 128 * ob:128 * (ob + 1)].rearrange(
                                         "(s p) c -> p s c", p=128)),
                                 deps=l1war[2 * wi + slot] + [kv["t_cc"]])
                    l1war[2 * wi + slot] = []
                nb = 2 * NTG
                dd = []
                for b in range(nb):
                    dd += bk.acquire(b)
                for wi in range(2):
                    for tg in range(NTG):
                        b = wi * NTG + tg
                        for k in range(8):
                            last = (k == 7 and wi == 1 and tg == NTG - 1)
                            mm(lambda e, k=k, wi=wi, tg=tg, b=b, p=p, co=co: e.matmul(
                                PF[b][:, 0:PTW],
                                Wst[wi][:, co + 128 * k:co + 128 * (k + 1)],
                                xn_own[:, TOWN * k + PT * p + PTW * tg:
                                       TOWN * k + PT * p + PTW * (tg + 1)],
                                start=(k == 0), stop=(k == 7)),
                                deps=(dd + [wt[0], wt[1], kv["t_xn"]]) if (k == 0 and wi == 0 and tg == 0) else (),
                                inc=last)
                t_l1 = ctx.tok("T")
                l1war[2 * 0 + slot].append(t_l1)
                l1war[2 * 1 + slot].append(t_l1)
                war[0].append(t_l1) if ob == 31 else None
                war[1].append(t_l1) if ob == 31 else None
                for tg in range(NTG):
                    t_e1 = sca(lambda e, ob=ob, tg=tg: e.activation(
                        ft1[:], PF[tg][:, 0:PTW], ACT.Silu,
                        scale=ffn_scales[:, ob:ob + 1]),
                        deps=[t_l1, kv["t_scales"], scr_war[0]])
                    t_e2 = sca(lambda e, ob=ob, tg=tg: e.activation(
                        ft2[:], PF[NTG + tg][:, 0:PTW], ACT.Copy,
                        scale=ffn_scales[:, 32 + ob:33 + ob]),
                        deps=[scr_war[1]])
                    t_h = vec(lambda e, ob=ob, tg=tg: e.tensor_mul(
                        ffnA[:, PT * ob + PTW * tg:PT * ob + PTW * (tg + 1)],
                        ft1[:], ft2[:]), deps=[t_e1, t_e2])
                    bk.read(tg, t_e1)
                    bk.read(NTG + tg, t_e2)
                    scr_war[0] = t_h
                    scr_war[1] = t_h
            t_src = ctx.tok("V")
            kv["tapA1"] = tap("A1", lambda: ffnA[:, :], [t_src]) if p == 0 else None
            xs, osg = fxs, fos
            for lay, (wdr, src, dst, scoff, nob) in enumerate(
                    ((h1t, ffnA, ffnB, 64, 32), (h2t, ffnB, ffnA, 96, 32),
                     (w3t, ffnA, None, 128, 8))):
                nxt = None
                for ob in range(nob):
                    par = ob % 2
                    t_wl = dma("gpsimd", lambda e, wdr=wdr, ob=ob, par=par: e.dma_start(
                        out=Wst[par][:, :].rearrange("p (s c) -> p s c", s=32),
                        in_=wdr.ap()[:, 128 * ob:128 * (ob + 1)].rearrange(
                            "(s p) c -> p s c", p=128)), deps=war[par] + [kv["t_cc"]])
                    war[par] = []
                    if lay == 2:
                        t_xl = dma("sync", lambda e, ob=ob, p=p: e.dma_start(
                            out=xs[:, :], in_=x_own.ap()[128 * ob:128 * (ob + 1),
                                                         PT * p:PT * (p + 1)]),
                            deps=[kv.get("t_l4adds")])
                        t_yl0 = dma("sync", lambda e, ob=ob, p=p: e.dma_start(
                            out=fys_b[:, :], in_=py_out.ap()[128 * ob:128 * (ob + 1),
                                                             PT * p:PT * (p + 1)]),
                            deps=[kv["t_cc"], kv.get("t_fysc")])
                        t_yl = vec(lambda e: e.tensor_copy(fys[:, :], fys_b[:, :]),
                                   deps=[t_yl0, kv.get("t_l4adds")])
                        kv["t_fysc"] = t_yl
                    dd = []
                    for tg in range(NTG):
                        dd += bk.acquire(tg)
                    t_srcdep = t_src if lay != 2 else kv["l3_done"]
                    for tg in range(NTG):
                        for k in range(32):
                            last = (k == 31 and tg == NTG - 1)
                            mm(lambda e, k=k, tg=tg, par=par, src=src: e.matmul(
                                PF[tg][:, 0:PTW], Wst[par][:, 128 * k:128 * (k + 1)],
                                src[:, PT * k + PTW * tg:PT * k + PTW * (tg + 1)],
                                start=(k == 0), stop=(k == 31)),
                                deps=(dd + [t_wl, t_srcdep]) if (k == 0 and tg == 0) else (),
                                inc=last)
                    t_mmk = ctx.tok("T")
                    war[par].append(t_mmk)
                    for tg in range(NTG):
                        if lay != 2:
                            scr = (ft1, ft2)[tg % 2]
                            t_gx = sca(lambda e, ob=ob, tg=tg, scoff=scoff, scr=scr:
                                       e.activation(scr[:], PF[tg][:, 0:PTW], ACT.Gelu,
                                                    scale=ffn_scales[:, scoff + ob:scoff + ob + 1]),
                                       deps=[t_mmk, scr_war[2 + tg % 2]])
                            t_cx = vec(lambda e, ob=ob, tg=tg, dst=dst, scr=scr: e.tensor_copy(
                                dst[:, PT * ob + PTW * tg:PT * ob + PTW * (tg + 1)], scr[:]),
                                deps=[t_gx, kv["tapA1"] if lay == 1 else None])
                            bk.read(tg, t_gx)
                            scr_war[2 + tg % 2] = t_cx
                            nxt = t_cx
                        else:
                            oso = PT * (ob % 2)
                            fw = kv.setdefault("fos_war", [None, None])
                            t_y = sca(lambda e, ob=ob, tg=tg, oso=oso: e.activation(
                                osg[:, oso + PTW * tg:oso + PTW * (tg + 1)],
                                PF[tg][:, 0:PTW], ACT.Copy,
                                scale=ffn_scales[:, 128 + ob:129 + ob]),
                                deps=[t_mmk, fw[ob % 2]])
                            bk.read(tg, t_y)
                            t_a1 = vec(lambda e, tg=tg, oso=oso: e.tensor_add(
                                osg[:, oso + PTW * tg:oso + PTW * (tg + 1)],
                                osg[:, oso + PTW * tg:oso + PTW * (tg + 1)],
                                xs[:, PTW * tg:PTW * (tg + 1)]), deps=[t_y, t_xl])
                            t_a2 = vec(lambda e, tg=tg, oso=oso: e.tensor_add(
                                osg[:, oso + PTW * tg:oso + PTW * (tg + 1)],
                                osg[:, oso + PTW * tg:oso + PTW * (tg + 1)],
                                fys[:, PTW * tg:PTW * (tg + 1)]), deps=[t_a1, t_yl])
                    if lay == 2:
                        oso = PT * (ob % 2)
                        kv["t_l4adds"] = ctx.tok("V")
                        kv["fos_war"][ob % 2] = dma(
                            "scalar", lambda e, ob=ob, p=p, oso=oso: e.dma_start(
                                out=out_fm.ap()[128 * ob:128 * (ob + 1),
                                                PT * p:PT * (p + 1)],
                                in_=osg[:, oso:oso + PT]), deps=[kv["t_l4adds"]])
                        kv["t_l4dma"] = kv["fos_war"][ob % 2]
                if lay == 0:
                    if p == 0:
                        tap("B2", lambda: ffnB[:, :], [nxt])
                    t_src = nxt
                elif lay == 1:
                    if p == 0:
                        tap("A3", lambda: ffnA[:, :], [nxt])
                    kv["l3_done"] = nxt
            kv["out_done"] = ctx.tok("DQA")

        def emit_all():
            setup()
            if stop_after == "setup":
                pass
            else:
                phase15()
                war = kv["xq_war"][0]
                kv["xq_war"][0] = []
                for r in range(8):
                    ctx.dma("sync", lambda e, r=r: e.dma_start(
                        out=xq[0][:, TW * r:TW * (r + 1)],
                        in_=x_fm.ap()[128 * r:128 * (r + 1), 0:TW]), deps=war)
                kv["t_x0"] = ctx.tok("DQS")
                n_tiles = 1 if stop_after == "tile0" else NT
                drive(gen_front(0))
                for i in range(n_tiles):
                    emit_tile_rest(i)
                if stop_after not in ("tile0", "p1"):
                    tap("pyin", lambda: py_in.ap(), kv["py_done"])
                    tap("xn", lambda: xn_own[:, :], [kv["t_xn"]])
                    if stop_after != "p15":
                        if stop_after != "baronly":
                            phase2()
                        else:
                            kv["t_cc"] = kv["t_consts"]
                        tap("py", lambda: py_out.ap(), [kv["t_cc"]])
                        if stop_after != "cc":
                            ctx.barrier()
                            if stop_after not in ("bar", "baronly"):
                                emit_ffn_pass(0)
                                if stop_after != "ffn0":
                                    emit_ffn_pass(1)
            if stop_after is not None:
                kv["out_done"] = ctx.dma("scalar", lambda e: e.dma_start(
                    out=out_fm.ap()[0:128, 0:TW], in_=pystage[:, 0:TW]),
                    deps=kv.get("py_done", []))
            for eng in ENGS:
                ctx.op(eng, lambda e: None, [kv["out_done"]], None)

        with nc.Block() as block:
            for engname in ENGS:
                def run_pass(engobj, engname=engname):
                    ctx.begin(engname, engobj)
                    bk.war = {}
                    kv.clear()
                    emit_all()
                getattr(block, engname)(run_pass)
    return nc


def _core_inputs(inp, core):
    b, g = core // 4, core % 4
    x = np.asarray(inp["L0_attn_x"], np.float32)
    T = x.shape[1]
    TOWN = T // 4
    d = {}
    xt = np.ascontiguousarray(x[b].T)
    d["x_fm"] = xt
    d["x_own"] = np.ascontiguousarray(xt[:, TOWN * g:TOWN * (g + 1)])
    d["wq_t"] = np.ascontiguousarray(
        np.asarray(inp["L0_attn_Wq"], np.float32).T[:, 256 * g:256 * (g + 1)])
    wk = np.asarray(inp["L0_attn_Wk"], np.float32).T[:, 64 * g:64 * (g + 1)]
    wv = np.asarray(inp["L0_attn_Wv"], np.float32).T[:, 64 * g:64 * (g + 1)]
    d["wkv_t"] = np.ascontiguousarray(np.concatenate([wk, wv], axis=1))
    wg = np.asarray(inp["L0_attn_Wgate"], np.float32).T[:, 256 * g:256 * (g + 1)]
    wb = np.asarray(inp["L0_attn_Wbeta"], np.float32).T[:, g:g + 1]
    wgk = np.asarray(inp["L0_attn_Wgk"], np.float32).T[:, g:g + 1]
    d["wgbg_t"] = np.ascontiguousarray(np.concatenate([wg, wb, wgk], axis=1))
    d["cwq"] = np.ascontiguousarray(
        np.asarray(inp["L0_attn_cwq"], np.float32)[256 * g:256 * (g + 1)])
    cwk = np.asarray(inp["L0_attn_cwk"], np.float32)[64 * g:64 * (g + 1)]
    cwv = np.asarray(inp["L0_attn_cwv"], np.float32)[64 * g:64 * (g + 1)]
    d["cwkv"] = np.ascontiguousarray(np.concatenate([cwk, cwv], axis=0))
    d["scal"] = np.array([[inp["L0_attn_b_beta"][g], inp["L0_attn_A_log"][g],
                           inp["L0_attn_dt_bias"][g]]], np.float32)
    d["rmsw_row"] = np.ascontiguousarray(
        np.tile(np.asarray(inp["L0_attn_rms_w"], np.float32), 4)[None, :])
    d["wpr_t"] = np.ascontiguousarray(
        np.asarray(inp["L0_attn_Wproj"], np.float32).T[256 * g:256 * (g + 1), :])
    return d


def _shared_inputs(inp):
    d = {}
    d["w1t"] = np.ascontiguousarray(np.asarray(inp["L0_ffn_w1_w"], np.float32).T)
    d["w2t"] = np.ascontiguousarray(np.asarray(inp["L0_ffn_w2_w"], np.float32).T)
    d["h1t"] = np.ascontiguousarray(np.asarray(inp["L0_ffn_h1_w"], np.float32).T)
    d["h2t"] = np.ascontiguousarray(np.asarray(inp["L0_ffn_h2_w"], np.float32).T)
    d["w3t"] = np.ascontiguousarray(np.asarray(inp["L0_ffn_w3_w"], np.float32).T)
    d["sc1"] = np.ascontiguousarray(
        np.asarray(inp["L0_ffn_w1_s"], np.float32).reshape(32, 128).T)
    d["sc2"] = np.ascontiguousarray(
        np.asarray(inp["L0_ffn_w2_s"], np.float32).reshape(32, 128).T)
    d["sch1"] = np.ascontiguousarray(
        np.asarray(inp["L0_ffn_h1_s"], np.float32).reshape(32, 128).T)
    d["sch2"] = np.ascontiguousarray(
        np.asarray(inp["L0_ffn_h2_s"], np.float32).reshape(32, 128).T)
    d["sc3"] = np.ascontiguousarray(
        np.asarray(inp["L0_ffn_w3_s"], np.float32).reshape(8, 128).T)
    d["idn_b"] = np.eye(128).astype(BF)
    d["triu1"] = np.triu(np.ones((128, 128), np.float32))
    d["tril0"] = np.tril(np.ones((128, 128), np.float32), -1)
    csq = np.zeros((2, 128), np.float32)
    csq[0, 0:64] = 0.125
    csq[1, 64:128] = 0.125
    d["csq"] = csq
    cskv = np.zeros((2, 128), np.float32)
    cskv[0, 0:64] = 1.0
    cskv[1, 64:128] = 1.0
    d["cskv"] = cskv
    d["onesrow"] = np.ones((1, TW), np.float32)
    return d


def kernel(**inputs):
    inp = {k: np.asarray(v) for k, v in inputs.items()}
    T = inp["L0_attn_x"].shape[1]
    TOWN = T // 4
    nc = build_program(T=T)
    shared = _shared_inputs(inp)
    in_maps = []
    for core in range(8):
        d = _core_inputs(inp, core)
        d.update(shared)
        in_maps.append(d)
    res = run_bass_kernel_spmd(nc, in_maps, core_ids=list(range(8)))
    out = np.empty((2, T, 1024), np.float32)
    for core in range(8):
        b, g = core // 4, core % 4
        out[b, TOWN * g:TOWN * (g + 1), :] = res.results[core]["out_fm"].T
    return out


# version 16
# speedup vs baseline: 1.0467x; 1.0113x over previous
import contextlib

import numpy as np
import ml_dtypes

import concourse.bass as bass
import concourse.mybir as mybir
from concourse.bass_utils import run_bass_kernel_spmd

F32 = mybir.dt.float32
BF16 = mybir.dt.bfloat16
ALU = mybir.AluOpType
ACT = mybir.ActivationFunctionType
AX = mybir.AxisListType
BF = ml_dtypes.bfloat16

C = 1024
TW = 512

ENGS = ("sync", "tensor", "vector", "scalar", "gpsimd")
OWN_SEM = {"tensor": "T", "vector": "V", "scalar": "S", "gpsimd": "G", "sync": "Y"}


class Ctx:
    def __init__(self, nc, sems):
        self.nc = nc
        self.sems = sems
        self.cur = None
        self.e = None
        self.counts = {}
        self.waited = {}
        self.base_dep = None

    def begin(self, name, engobj):
        self.cur = name
        self.e = engobj
        self.counts = {k: 0 for k in self.sems}
        self.waited = {k: -1 for k in self.sems}
        self.base_dep = None

    def op(self, eng, fn, deps=(), inc=None, inc_n=1):
        tok = None
        if inc is not None:
            self.counts[inc] += inc_n
            tok = (inc, self.counts[inc])
        if self.cur == eng:
            own = OWN_SEM.get(eng)
            deps = list(deps)
            if self.base_dep is not None:
                deps.append(self.base_dep)
            nw = 0
            for dep in deps:
                if dep is None:
                    continue
                s, v = dep
                if v > self.waited[s]:
                    if nw and nw % 2 == 0:
                        self.e.nop(nofuse=True)
                    self.e.wait_ge(self.sems[s], v)
                    self.waited[s] = v
                    nw += 1
            inst = fn(self.e)
            if inc is not None and inst is not None:
                inst.then_inc(self.sems[inc], inc_n)
        return tok

    def mm(self, fn, deps=(), inc=False):
        return self.op("tensor", fn, deps, "T" if inc else None)

    def vec(self, fn, deps=(), inc=True):
        return self.op("vector", fn, deps, "V" if inc else None)

    def sca(self, fn, deps=(), inc=True):
        return self.op("scalar", fn, deps, "S" if inc else None)

    def dma(self, queue, fn, deps=()):
        q = {"sync": "DQS", "scalar": "DQA", "gpsimd": "DQG"}[queue]
        return self.op(queue, fn, deps, q, 16)

    def tok(self, sem):
        return (sem, self.counts[sem])

    def guard(self, engs=("tensor", "vector", "scalar", "sync"),
              sems=("T", "V", "S", "DQS", "DQA")):
        snap = dict(self.counts)
        for eng in engs:
            deps = [(s, snap[s]) for s in sems if s != OWN_SEM.get(eng)]
            self.op(eng, lambda e: None, deps, None)

    def barrier(self):
        self.guard(engs=ENGS, sems=("T", "V", "S", "DQS", "DQA", "DQG"))


class Banks:
    def __init__(self):
        self.war = {}

    def acquire(self, b):
        d = self.war.get(b, [])
        self.war[b] = []
        return d

    def read(self, b, tok):
        if tok is not None:
            self.war.setdefault(b, []).append(tok)


SCHED = "full"


def build_program(T=8192, use_cc=True, stop_after=None, taps=False):
    nc = bass.Bass("TRN2", target_bir_lowering=False,
                   detect_race_conditions=False)
    NT = T // TW
    TOWN = T // 4
    PT = TOWN // 2
    PTW = min(512, PT)
    NTG = PT // PTW

    x_fm = nc.dram_tensor("x_fm", [C, T], F32, kind="ExternalInput")
    x_own = nc.dram_tensor("x_own", [C, TOWN], F32, kind="ExternalInput")
    wq_t = nc.dram_tensor("wq_t", [C, 256], F32, kind="ExternalInput")
    wkv_t = nc.dram_tensor("wkv_t", [C, 128], F32, kind="ExternalInput")
    wgbg_t = nc.dram_tensor("wgbg_t", [C, 258], F32, kind="ExternalInput")
    cwq_d = nc.dram_tensor("cwq", [256, 4], F32, kind="ExternalInput")
    cwkv_d = nc.dram_tensor("cwkv", [128, 4], F32, kind="ExternalInput")
    scal_d = nc.dram_tensor("scal", [1, 3], F32, kind="ExternalInput")
    rmsw_d = nc.dram_tensor("rmsw_row", [1, 256], F32, kind="ExternalInput")
    wpr_t = nc.dram_tensor("wpr_t", [256, C], F32, kind="ExternalInput")
    w1t = nc.dram_tensor("w1t", [C, 4096], F32, kind="ExternalInput")
    w2t = nc.dram_tensor("w2t", [C, 4096], F32, kind="ExternalInput")
    h1t = nc.dram_tensor("h1t", [4096, 4096], F32, kind="ExternalInput")
    h2t = nc.dram_tensor("h2t", [4096, 4096], F32, kind="ExternalInput")
    w3t = nc.dram_tensor("w3t", [4096, C], F32, kind="ExternalInput")
    sc_d = {}
    for nm, w in (("sc1", 32), ("sc2", 32), ("sch1", 32), ("sch2", 32), ("sc3", 8)):
        sc_d[nm] = nc.dram_tensor(nm, [128, w], F32, kind="ExternalInput")
    idn_d = nc.dram_tensor("idn_b", [128, 128], BF16, kind="ExternalInput")
    triu1_d = nc.dram_tensor("triu1", [128, 128], F32, kind="ExternalInput")
    tril0_d = nc.dram_tensor("tril0", [128, 128], F32, kind="ExternalInput")
    csq_d = nc.dram_tensor("csq", [2, 128], F32, kind="ExternalInput")
    cskv_d = nc.dram_tensor("cskv", [2, 128], F32, kind="ExternalInput")
    onesrow_d = nc.dram_tensor("onesrow", [1, TW], F32, kind="ExternalInput")

    out_fm = nc.dram_tensor("out_fm", [C, TOWN], F32, kind="ExternalOutput")
    py_in = nc.dram_tensor("py_in", [4, C, TOWN], BF16, kind="Internal")
    py_out = nc.dram_tensor("py_out", [C, TOWN], BF16, kind="Internal")
    PT_ = TOWN // 2
    tap_d = {}
    if isinstance(taps, (tuple, list, set)):
        shapes = {"A1": ([128, 32 * PT_], BF16), "B2": ([128, 32 * PT_], BF16),
                  "A3": ([128, 32 * PT_], BF16), "xn": ([128, 8 * TOWN], BF16),
                  "py": ([C, TOWN], BF16)}
        for nm in taps:
            tap_d[nm] = nc.dram_tensor("tap_" + nm, list(shapes[nm][0]),
                                       shapes[nm][1], kind="ExternalOutput")
    elif taps == "min":
        for nm in ("A1", "B2", "A3"):
            tap_d[nm] = nc.dram_tensor("tap_" + nm, [128, 32 * PT_], BF16,
                                       kind="ExternalOutput")
        tap_d["xn"] = nc.dram_tensor("tap_xn", [128, 8 * TOWN], BF16,
                                     kind="ExternalOutput")
        tap_d["py"] = nc.dram_tensor("tap_py", [C, TOWN], BF16,
                                     kind="ExternalOutput")
    elif taps:
        tap_d["xn"] = nc.dram_tensor("tap_xn", [128, 8 * TOWN], BF16,
                                     kind="ExternalOutput")
        tap_d["A1"] = nc.dram_tensor("tap_A1", [128, 32 * PT_], BF16,
                                     kind="ExternalOutput")
        tap_d["B2"] = nc.dram_tensor("tap_B2", [128, 32 * PT_], BF16,
                                     kind="ExternalOutput")
        tap_d["A3"] = nc.dram_tensor("tap_A3", [128, 32 * PT_], BF16,
                                     kind="ExternalOutput")
        tap_d["py"] = nc.dram_tensor("tap_py", [C, TOWN], BF16,
                                     kind="ExternalOutput")
        tap_d["pyin"] = nc.dram_tensor("tap_pyin", [4, C, TOWN], BF16,
                                       kind="ExternalOutput")
        for nm, shape, dt_ in (("qb0", [128, TW], BF16), ("kvb", [128, TW], BF16),
                               ("gate", [128, 1024], BF16), ("bg", [128, 8], F32),
                               ("Sf", [64, 64], F32), ("Of", [128, 256], F32),
                               ("ogfm", [128, 1024], BF16), ("xraw", [128, TW], F32),
                               ("rr", [128, TW], F32), ("nrc", [128, 4], F32),
                               ("nbb", [128, 1], F32), ("screp", [128, 3], F32),
                               ("cp0", [128, 8], F32), ("cp1", [128, 8], F32)):
            tap_d[nm] = nc.dram_tensor("tap_" + nm, shape, dt_,
                                       kind="ExternalOutput")

    st = contextlib.ExitStack()
    with st:
        sems = {}
        for s in ("T", "V", "S", "G", "Y", "DQS", "DQA", "DQG", "CC"):
            sems[s] = st.enter_context(nc.semaphore(s))
        ctx = Ctx(nc, sems)
        bk = Banks()

        def sb(stk, name, shape, dtype):
            return stk.enter_context(nc.sbuf_tensor(name, list(shape), dtype))

        PF = [st.enter_context(nc.psum_tensor(f"pf{i}", [128, 512], F32))
              for i in range(6)]
        PG = [st.enter_context(nc.psum_tensor(f"pg{i}", [128, 256], BF16))
              for i in range(2)]

        constpool = sb(st, "constpool", (128, 8), F32)
        CONSTS = {1e-5: 0, 1e-12: 1, 1.0: 2, -1.0: 3, -0.5: 4,
                  1.0 / C: 5, 1.0 / 64: 6, 0.5: 7}
        for v, c in CONSTS.items():
            nc.const_aps.register(F32, v, constpool[:, c:c + 1])
        ffn_scales = sb(st, "ffn_scales", (128, 136), F32)
        xn_own = sb(st, "xn_own", (128, 8 * TOWN), BF16)
        S_f = sb(st, "S_f", (64, 64), F32)
        S_b = sb(st, "S_b", (64, 64), BF16)

        p1 = contextlib.ExitStack()
        idb = sb(p1, "idb", (128, 128), BF16)
        triu1_f = sb(p1, "triu1_f", (128, 128), F32)
        tril0_f = sb(p1, "tril0_f", (128, 128), F32)
        ones_f = sb(p1, "ones_f", (1, 128), F32)
        onescol_b = sb(p1, "onescol_b", (128, 1), BF16)
        blk2_b = sb(p1, "blk2_b", (128, 2), BF16)
        colsel_q = sb(p1, "colsel_q", (2, 128), F32)
        colsel_kv = sb(p1, "colsel_kv", (2, 128), F32)
        kv_rhs = sb(p1, "kv_rhs", (2, TW), F32)
        rmsw_rep = sb(p1, "rmsw_rep", (128, 256), F32)
        srow = sb(p1, "srow", (1, 3), F32)
        screp = sb(p1, "screp", (128, 3), F32)
        negbb_rep = sb(p1, "negbb_rep", (128, 1), F32)
        negexpA = sb(p1, "negexpA", (128, 1), F32)
        wqt_b = sb(p1, "wqt_b", (128, 2048), BF16)
        wkvt_b = sb(p1, "wkvt_b", (128, 1024), BF16)
        wgbg_b = sb(p1, "wgbg_b", (128, 8 * 258), BF16)
        wpr_b = sb(p1, "wpr_b", (128, 2048), BF16)
        cwq0 = sb(p1, "cwq0", (128, 4), F32)
        cwq1 = sb(p1, "cwq1", (128, 4), F32)
        cwkv_s = sb(p1, "cwkv_s", (128, 4), F32)
        wsum_fm = sb(p1, "wsum_fm", (1, 384), BF16)
        wsum_gbg = sb(p1, "wsum_gbg", (1, 258), BF16)
        xq = [sb(p1, f"xq{p}", (128, 8 * TW), F32) for p in range(2)]
        xbf = [sb(p1, f"xbf{p}", (128, 8 * TW), BF16) for p in range(2)]
        xsqb = sb(p1, "xsqb", (128, 8 * TW), BF16)
        m_row = sb(p1, "m_row", (1, TW), F32)
        var_row = sb(p1, "var_row", (1, TW), F32)
        rstd_row = sb(p1, "rstd_row", (1, TW), F32)
        mr_row = sb(p1, "mr_row", (1, TW), F32)
        negm_b = sb(p1, "negm_b", (1, TW), BF16)
        rrep = sb(p1, "rrep", (128, TW), F32)
        mrrep = sb(p1, "mrrep", (128, TW), F32)
        qraw0 = sb(p1, "qraw0", (128, TW + 3), BF16)
        qraw1 = sb(p1, "qraw1", (128, TW + 3), BF16)
        kvraw = sb(p1, "kvraw", (128, TW + 3), BF16)
        qc0 = sb(p1, "qc0", (128, TW), BF16)
        qc1 = sb(p1, "qc1", (128, TW), BF16)
        kvc = sb(p1, "kvc", (128, TW), BF16)
        rsq01 = sb(p1, "rsq01", (2, TW), F32)
        rsq23 = sb(p1, "rsq23", (2, TW), F32)
        TB = []
        for ti in range(2):
            tb = {}
            for nm, shape, dt_ in (
                    ("qb0", (128, TW), BF16), ("qb1", (128, TW), BF16),
                    ("kvb", (128, TW), BF16), ("qodd0", (64, TW), BF16),
                    ("qodd1", (64, TW), BF16), ("gate_tm", (128, 1024), BF16),
                    ("beta_c", (128, 4), F32), ("g_c", (128, 4), F32),
                    ("rcol", (128, 4), F32), ("nrcol", (128, 4), F32),
                    ("ogfm", (128, 1024), BF16)):
                tb[nm] = sb(p1, f"{nm}_t{ti}", shape, dt_)
            TB.append(tb)
        CB = []
        for pi in range(4):
            cb = {}
            for nm, shape, dt_ in (
                    ("KKb", (128, 128), F32), ("dect", (128, 128), F32),
                    ("dmst", (128, 128), F32), ("tmpf", (128, 128), F32),
                    ("M_b", (128, 128), BF16), ("B0_b", (128, 128), BF16),
                    ("Y_b", (128, 128), BF16),
                    ("AB0", (128, 256), BF16), ("AB1", (128, 256), BF16),
                    ("RHS2", (128, 128), BF16), ("khat", (128, 64), BF16),
                    ("U2f", (128, 128), F32), ("amsk", (128, 512), BF16),
                    ("WtU", (128, 128), BF16),
                    ("U_b", (128, 64), BF16), ("O_f", (128, 256), F32),
                    ("Osq", (128, 256), F32), ("rms_in", (128, 4), F32),
                    ("rms4", (128, 4), F32), ("grw", (128, 256), F32),
                    ("og_bf", (128, 256), BF16), ("ebcol", (128, 1), F32),
                    ("dlcol", (128, 1), F32), ("eblcol", (128, 1), F32),
                    ("bcol", (128, 1), F32), ("nbcol", (128, 1), F32),
                    ("bebe", (128, 1), F32), ("brow", (1, 128), F32),
                    ("ebtmp", (128, 256), F32)):
                cb[nm] = sb(p1, f"{nm}{pi}", shape, dt_)
            CB.append(cb)
        pystage = sb(p1, "pystage", (128, 1024), BF16)
        p1.close()

        ffnA = sb(st, "ffnA", (128, 32 * PT), BF16)
        ffnB = sb(st, "ffnB", (128, 32 * PT), BF16)
        Wst = [sb(st, f"fw{p}", (128, 4096), BF16) for p in range(2)]
        ft1 = sb(st, "ffnt1", (128, PTW), F32)
        ft2 = sb(st, "ffnt2", (128, PTW), F32)
        fxs = sb(st, "xstage", (128, PT), F32)
        fys_b = sb(st, "ystage_b", (128, PT), BF16)
        fys = sb(st, "ystage", (128, PT), F32)
        fos = sb(st, "ostage", (128, 2 * PT), F32)

        kv = {}

        def setup():
            dma, mm, vec, sca = ctx.dma, ctx.mm, ctx.vec, ctx.sca
            dma("sync", lambda e: e.dma_start(out=idb[:], in_=idn_d.ap()))
            dma("sync", lambda e: e.dma_start(out=triu1_f[:], in_=triu1_d.ap()))
            dma("sync", lambda e: e.dma_start(out=tril0_f[:], in_=tril0_d.ap()))
            dma("sync", lambda e: e.dma_start(out=srow[:], in_=scal_d.ap()))
            dma("sync", lambda e: e.dma_start(out=cwq0[:], in_=cwq_d.ap()[0:128, :]))
            dma("sync", lambda e: e.dma_start(out=cwq1[:], in_=cwq_d.ap()[128:256, :]))
            dma("sync", lambda e: e.dma_start(out=cwkv_s[:], in_=cwkv_d.ap()))
            dma("sync", lambda e: e.dma_start(out=rmsw_rep[0:1, 0:256], in_=rmsw_d.ap()))
            for j, (nm, c0, w) in enumerate([("sc1", 0, 32), ("sc2", 32, 32),
                                             ("sch1", 64, 32), ("sch2", 96, 32),
                                             ("sc3", 128, 8)]):
                dma("sync", lambda e, nm=nm, c0=c0, w=w: e.dma_start(
                    out=ffn_scales[:, c0:c0 + w], in_=sc_d[nm].ap()))
            dma("sync", lambda e: e.dma_start(out=colsel_q[:], in_=csq_d.ap()))
            dma("sync", lambda e: e.dma_start(out=colsel_kv[:], in_=cskv_d.ap()))
            dma("sync", lambda e: e.dma_start(out=kv_rhs[1:2, 0:TW], in_=onesrow_d.ap()))
            t_cl = ctx.tok("DQS")
            kv["t_scales"] = t_cl
            for r in range(8):
                dma("gpsimd", lambda e, r=r: e.dma_start(
                    out=wqt_b[:, 256 * r:256 * (r + 1)],
                    in_=wq_t.ap()[128 * r:128 * (r + 1), :]))
                dma("gpsimd", lambda e, r=r: e.dma_start(
                    out=wkvt_b[:, 128 * r:128 * (r + 1)],
                    in_=wkv_t.ap()[128 * r:128 * (r + 1), :]))
                dma("gpsimd", lambda e, r=r: e.dma_start(
                    out=wgbg_b[:, 258 * r:258 * (r + 1)],
                    in_=wgbg_t.ap()[128 * r:128 * (r + 1), :]))
            for r in range(2):
                dma("gpsimd", lambda e, r=r: e.dma_start(
                    out=wpr_b[:, 1024 * r:1024 * (r + 1)],
                    in_=wpr_t.ap()[128 * r:128 * (r + 1), :]))
            kv["t_wload"] = ctx.tok("DQG")
            for f in (
                lambda e: e.memset(ones_f[:], 1.0),
                lambda e: e.memset(onescol_b[:], 1.0),
                lambda e: e.memset(blk2_b[:], 0.0),
                lambda e: e.memset(blk2_b[0:64, 0:1], 1.0),
                lambda e: e.memset(blk2_b[64:128, 1:2], 1.0),
                lambda e: e.memset(S_f[:], 0.0),
                lambda e: e.memset(S_b[:], 0.0),
                lambda e: e.memset(qraw0[:, 0:3], 0.0),
                lambda e: e.memset(qraw1[:, 0:3], 0.0),
                lambda e: e.memset(kvraw[:, 0:3], 0.0),
            ):
                vec(f, inc=False)
            for v, c in CONSTS.items():
                vec(lambda e, v=v, c=c: e.memset(constpool[:, c:c + 1], v), inc=False)
            if taps:
                tcp = vec(lambda e: e.tensor_copy(rrep[:, 0:8], constpool[:, :]))
                tap("cp0", lambda: rrep[:, 0:8], [tcp])
            vec(lambda e: e.memset(TB[0]["rcol"][:], 0.0), inc=False)
            t_ms = vec(lambda e: e.memset(TB[1]["rcol"][:], 0.0))
            kv["t_consts"] = t_ms
            kv["t_Sb"] = t_ms
            kv["t_Sf"] = t_ms
            ctx.base_dep = t_ms
            t_mm1 = mm(lambda e: e.matmul(PF[1][:, 0:3], ones_f[:, :], srow[:, :],
                                          start=True, stop=True),
                       deps=[t_cl, t_ms] + bk.acquire(1), inc=True)
            t_ev = vec(lambda e: e.tensor_copy(screp[:], PF[1][:, 0:3]), deps=[t_mm1])
            bk.read(1, t_ev)
            sca(lambda e: e.mul(negbb_rep[:], screp[:, 0:1], -1.0), deps=[t_ev],
                inc=False)
            t_e1 = sca(lambda e: e.activation(negexpA[:], screp[:, 1:2], ACT.Exp),
                       deps=[t_ev])
            t_e2 = vec(lambda e: e.tensor_scalar_mul(negexpA[:], negexpA[:], -1.0),
                       deps=[t_e1])
            kv["t_reps"] = t_e2
            t_mm2 = mm(lambda e: e.matmul(PF[2][:, 0:256], ones_f[:, :],
                                          rmsw_rep[0:1, 0:256], start=True, stop=True),
                       deps=[t_cl, t_ms] + bk.acquire(2), inc=True)
            t_rw = vec(lambda e: e.tensor_copy(rmsw_rep[:], PF[2][:, 0:256]),
                       deps=[t_mm2])
            bk.read(2, t_rw)
            kv["t_rmsw"] = t_rw
            d0 = bk.acquire(0)
            for r in range(8):
                mm(lambda e, r=r: e.matmul(PF[0][0:1, 0:256], onescol_b[:, :],
                                           wqt_b[:, 256 * r:256 * (r + 1)],
                                           start=(r == 0), stop=(r == 7)),
                   deps=([kv["t_wload"], t_ms] + d0) if r == 0 else ())
            for r in range(8):
                mm(lambda e, r=r: e.matmul(PF[0][0:1, 256:384], onescol_b[:, :],
                                           wkvt_b[:, 128 * r:128 * (r + 1)],
                                           start=(r == 0), stop=(r == 7)),
                   inc=(r == 7))
            t_q = ctx.tok("T")
            d3 = bk.acquire(3)
            for r in range(8):
                mm(lambda e, r=r: e.matmul(PF[3][0:1, 0:258], onescol_b[:, :],
                                           wgbg_b[:, 258 * r:258 * (r + 1)],
                                           start=(r == 0), stop=(r == 7)),
                   deps=d3 if r == 0 else (), inc=(r == 7))
            t_g = ctx.tok("T")
            t_w1 = vec(lambda e: e.tensor_copy(wsum_fm[:], PF[0][0:1, 0:384]),
                       deps=[t_q])
            bk.read(0, t_w1)
            t_w2 = vec(lambda e: e.tensor_copy(wsum_gbg[:], PF[3][0:1, 0:258]),
                       deps=[t_g])
            bk.read(3, t_w2)
            kv["t_wsum"] = t_w2
            kv["t_wsumf"] = t_w1
            kv["xq_war"] = [[], []]
            kv["xbf_war"] = [[], []]
            kv["py_done"] = []
            kv["og_done"] = []
            kv["py_war"] = [None, None]
            kv["qscr_war"] = []

        def emit_ln(par, t_xload, t_xn_dep=None):
            mm, vec, sca = ctx.mm, ctx.vec, ctx.sca
            t_sq = sca(lambda e: e.activation(xsqb[:], xq[par][:], ACT.Square),
                       deps=[t_xload])
            t_cast = sca(lambda e: e.copy(xbf[par][:], xq[par][:]),
                         deps=[t_xload] + kv["xbf_war"][par])
            kv["xbf_war"][par] = []
            kv["xq_war"][par] += [t_cast, t_sq]
            d0 = bk.acquire(0)
            d1 = bk.acquire(1)
            for r in range(8):
                mm(lambda e, r=r, par=par: e.matmul(
                    PF[0][0:1, 0:TW], onescol_b[:, :], xbf[par][:, TW * r:TW * (r + 1)],
                    start=(r == 0), stop=(r == 7)),
                    deps=([t_cast, kv["t_consts"]] + d0) if r == 0 else (),
                    inc=(r == 7))
            t_s1 = ctx.tok("T")
            for r in range(8):
                mm(lambda e, r=r: e.matmul(
                    PF[1][0:1, 0:TW], onescol_b[:, :], xsqb[:, TW * r:TW * (r + 1)],
                    start=(r == 0), stop=(r == 7)),
                    deps=([t_sq] + d1) if r == 0 else (), inc=(r == 7))
            t_s2 = ctx.tok("T")
            t_m = sca(lambda e: e.activation(m_row[:], PF[0][0:1, 0:TW], ACT.Copy,
                                             scale=1.0 / C), deps=[t_s1])
            bk.read(0, t_m)
            t_v0 = sca(lambda e: e.activation(var_row[:], PF[1][0:1, 0:TW], ACT.Copy,
                                              scale=1.0 / C), deps=[t_s2])
            bk.read(1, t_v0)
            t_msq = vec(lambda e: e.tensor_mul(mr_row[:], m_row[:], m_row[:]),
                        deps=[t_m])
            t_var = vec(lambda e: e.tensor_sub(var_row[:], var_row[:], mr_row[:]),
                        deps=[t_v0, t_msq])
            t_ln = sca(lambda e: e.activation(rstd_row[:], var_row[:], ACT.Ln,
                                              bias=1e-5), deps=[t_var])
            t_r = sca(lambda e: e.activation(rstd_row[:], rstd_row[:], ACT.Exp,
                                             scale=-0.5), deps=[t_ln])
            return t_cast, t_m, t_r

        def gen_front(i):
            mm, vec, sca, dma = ctx.mm, ctx.vec, ctx.sca, ctx.dma
            par = i % 2
            tb = TB[par]
            kv["tb"] = tb
            qb0, qb1, kvb = tb["qb0"], tb["qb1"], tb["kvb"]
            qodd0, qodd1 = tb["qodd0"], tb["qodd1"]
            gate_tm, beta_c, g_c = tb["gate_tm"], tb["beta_c"], tb["g_c"]
            rcol, nrcol, ogfm = tb["rcol"], tb["nrcol"], tb["ogfm"]
            snap = kv.get(f"tsnap{par}")
            if snap is not None:
                for eng in ("tensor", "vector", "scalar", "sync"):
                    deps = [(sm, snap[sm]) for sm in ("T", "V", "S", "DQS", "DQA")]
                    ctx.op(eng, lambda e: None, deps, None)
            t_xload = kv[f"t_x{i}"]
            if i + 1 < NT:
                npar = (i + 1) % 2
                war = kv["xq_war"][npar]
                kv["xq_war"][npar] = []
                for r in range(8):
                    dma("sync", lambda e, r=r, i=i, npar=npar: e.dma_start(
                        out=xq[npar][:, TW * r:TW * (r + 1)],
                        in_=x_fm.ap()[128 * r:128 * (r + 1), TW * (i + 1):TW * (i + 2)]),
                        deps=war)
                kv[f"t_x{i+1}"] = ctx.tok("DQS")
            t_cast, t_m, t_r = emit_ln(par, t_xload)
            yield
            t_nm = sca(lambda e: e.activation(negm_b[:], m_row[:], ACT.Copy,
                                              scale=-1.0), deps=[t_m])
            d5 = bk.acquire(5)
            t_rrmm = mm(lambda e: e.matmul(PF[5][:, 0:TW], ones_f[:, :], rstd_row[:, :],
                                           start=True, stop=True),
                        deps=[t_r] + d5, inc=True)
            t_rrep = vec(lambda e: e.tensor_copy(rrep[:], PF[5][:, 0:TW]),
                         deps=[t_rrmm])
            bk.read(5, t_rrep)
            d3 = bk.acquire(5)
            for cc in range(4):
                mm(lambda e, cc=cc: e.matmul(PF[5][:, cc:cc + 1],
                                             rstd_row[0:1, 128 * cc:128 * (cc + 1)],
                                             ones_f[0:1, 0:1], start=True, stop=True),
                   deps=([t_r] + d3) if cc == 0 else (), inc=(cc == 3))
            t_rcmm = ctx.tok("T")
            t_rc = vec(lambda e: e.tensor_copy(rcol[:], PF[5][:, 0:4]), deps=[t_rcmm])
            bk.read(5, t_rc)
            t_nrc = vec(lambda e: e.tensor_scalar_mul(nrcol[:], rcol[:], -1.0),
                        deps=[t_rc])
            yield
            proj = []
            for (bnk, coff) in ((0, 0), (1, 128), (2, None)):
                dd = bk.acquire(bnk)
                for r in range(8):
                    if coff is None:
                        lhs = lambda e, r=r: wkvt_b[:, 128 * r:128 * (r + 1)]
                    else:
                        lhs = lambda e, r=r, coff=coff: wqt_b[:, 256 * r + coff:256 * r + coff + 128]
                    mm(lambda e, r=r, par=par, bnk=bnk, lhs=lhs: e.matmul(
                        PF[bnk][:, 0:TW], lhs(e), xbf[par][:, TW * r:TW * (r + 1)],
                        start=(r == 0), stop=False),
                        deps=(dd + [t_cast, kv["t_wsumf"], kv["t_wload"]]) if r == 0 else ())
                woff = {0: (0, 128), 1: (128, 256), 2: (256, 384)}[bnk]
                mm(lambda e, bnk=bnk, woff=woff: e.matmul(
                    PF[bnk][:, 0:TW], wsum_fm[0:1, woff[0]:woff[1]], negm_b[:, :],
                    start=False, stop=True), deps=[t_nm], inc=True)
                proj.append(ctx.tok("T"))
            t_q0p, t_q1p, t_kvp = proj
            evs = []
            for (buf, bnk, tq) in ((qraw0, 0, t_q0p), (qraw1, 1, t_q1p), (kvraw, 2, t_kvp)):
                if i > 0:
                    vec(lambda e, buf=buf: e.tensor_copy(buf[:, 0:3], buf[:, TW:TW + 3]),
                        deps=kv["qscr_war"], inc=False)
                t_e = vec(lambda e, buf=buf, bnk=bnk: e.tensor_mul(
                    buf[:, 3:TW + 3], PF[bnk][:, 0:TW], rrep[:]), deps=[tq, t_rrep])
                bk.read(bnk, t_e)
                evs.append(t_e)
            t_evq0, t_evq1, t_evkv = evs
            kv["qscr_war"] = []
            yield
            gtoks = []
            for cc in range(4):
                bnk = (5, 0, 1, 2)[cc]
                dd = bk.acquire(bnk)
                for r in range(8):
                    mm(lambda e, r=r, cc=cc, par=par, bnk=bnk: e.matmul(
                        PF[bnk][:, 0:258],
                        xbf[par][:, TW * r + 128 * cc:TW * r + 128 * cc + 128],
                        wgbg_b[:, 258 * r:258 * (r + 1)], start=(r == 0), stop=False),
                        deps=(dd + [kv["t_wsum"]]) if r == 0 else ())
                mm(lambda e, cc=cc, bnk=bnk: e.matmul(
                    PF[bnk][:, 0:258], negm_b[0:1, 128 * cc:128 * (cc + 1)],
                    wsum_gbg[:, :], start=False, stop=True), deps=[t_nm], inc=True)
                gtoks.append((cc, bnk, ctx.tok("T")))
            for (cc, bnk, t_gc) in gtoks:
                t_si = sca(lambda e, cc=cc, bnk=bnk: e.activation(
                    gate_tm[:, 256 * cc:256 * (cc + 1)], PF[bnk][:, 0:256], ACT.Silu,
                    scale=rcol[:, cc:cc + 1]), deps=[t_gc, t_rc])
                bk.read(bnk, t_si)
            for (cc, bnk, t_gc) in gtoks:
                t_be = sca(lambda e, cc=cc, bnk=bnk: e.activation(
                    beta_c[:, cc:cc + 1], PF[bnk][:, 256:257], ACT.Exp,
                    scale=nrcol[:, cc:cc + 1], bias=negbb_rep[:, :]),
                    deps=[t_gc, t_nrc, kv["t_reps"]])
                t_b2 = vec(lambda e, cc=cc: e.tensor_scalar_add(
                    beta_c[:, cc:cc + 1], beta_c[:, cc:cc + 1], 1.0), deps=[t_be])
                t_b3 = vec(lambda e, cc=cc: e.reciprocal(
                    beta_c[:, cc:cc + 1], beta_c[:, cc:cc + 1]), deps=[t_b2])
                t_ge = sca(lambda e, cc=cc, bnk=bnk: e.activation(
                    g_c[:, cc:cc + 1], PF[bnk][:, 257:258], ACT.Exp,
                    scale=rcol[:, cc:cc + 1], bias=screp[:, 2:3]), deps=[t_gc])
                t_g2 = sca(lambda e, cc=cc: e.activation(
                    g_c[:, cc:cc + 1], g_c[:, cc:cc + 1], ACT.Ln, bias=1.0),
                    deps=[t_ge])
                t_g3 = vec(lambda e, cc=cc: e.tensor_mul(
                    g_c[:, cc:cc + 1], g_c[:, cc:cc + 1], negexpA[:]), deps=[t_g2])
                bk.read(bnk, t_ge)
                kv[f"t_beta{cc}"] = t_b3
                kv[f"t_g{cc}"] = t_g3
            yield
            convs = []
            for (cin, cout, w, tdep) in ((qraw0, qc0, cwq0, t_evq0),
                                         (qraw1, qc1, cwq1, t_evq1),
                                         (kvraw, kvc, cwkv_s, t_evkv)):
                t_cj = vec(lambda e, cin=cin, cout=cout, w=w: e.tensor_scalar_mul(
                    cout[:], cin[:, 0:TW], w[:, 0:1]),
                    deps=[tdep, kv["t_scales"]])
                for j in range(1, 4):
                    t_cj = vec(lambda e, cin=cin, cout=cout, w=w, j=j: e.scalar_tensor_tensor(
                        cout[:], cin[:, j:TW + j], w[:, j:j + 1], cout[:],
                        op0=ALU.mult, op1=ALU.add), deps=[t_cj])
                convs.append(t_cj)
            t_cq0, t_cq1, t_ckv = convs
            kv["qscr_war"] = list(convs)
            yield
            t_sq0 = sca(lambda e: e.activation(qb0[:], qc0[:], ACT.Square),
                        deps=[t_cq0])
            t_sq1 = sca(lambda e: e.activation(qb1[:], qc1[:], ACT.Square),
                        deps=[t_cq1])
            t_sqk = sca(lambda e: e.activation(kvb[0:64, :], kvc[0:64, :], ACT.Square),
                        deps=[t_ckv])
            d0 = bk.acquire(0)
            t_ssq0 = mm(lambda e: e.matmul(PF[0][0:2, 0:TW], blk2_b[:, :], qb0[:, :],
                                           start=True, stop=True),
                        deps=[t_sq0] + d0, inc=True)
            d1 = bk.acquire(1)
            t_ssq1 = mm(lambda e: e.matmul(PF[1][0:2, 0:TW], blk2_b[:, :], qb1[:, :],
                                           start=True, stop=True),
                        deps=[t_sq1] + d1, inc=True)
            d4 = bk.acquire(2)
            t_ssqk = mm(lambda e: e.matmul(PF[2][0:1, 0:TW], onescol_b[0:64, :],
                                           kvb[0:64, :], start=True, stop=True),
                        deps=[t_sqk] + d4, inc=True)
            t_l0 = sca(lambda e: e.activation(rsq01[:], PF[0][0:2, 0:TW], ACT.Ln,
                                              bias=1e-12), deps=[t_ssq0])
            bk.read(0, t_l0)
            t_r0 = sca(lambda e: e.activation(rsq01[:], rsq01[:], ACT.Exp, scale=-0.5),
                       deps=[t_l0])
            t_l1 = sca(lambda e: e.activation(rsq23[:], PF[1][0:2, 0:TW], ACT.Ln,
                                              bias=1e-12), deps=[t_ssq1])
            bk.read(1, t_l1)
            t_r1 = sca(lambda e: e.activation(rsq23[:], rsq23[:], ACT.Exp, scale=-0.5),
                       deps=[t_l1])
            t_lk = sca(lambda e: e.activation(kv_rhs[0:1, 0:TW], PF[2][0:1, 0:TW],
                                              ACT.Ln, bias=1e-12), deps=[t_ssqk])
            bk.read(2, t_lk)
            t_rk = sca(lambda e: e.activation(kv_rhs[0:1, 0:TW], kv_rhs[0:1, 0:TW],
                                              ACT.Exp, scale=-0.5), deps=[t_lk])
            qbt = []
            for (t_seed, rr, sel, src, dst, t_scr) in (
                    (t_r0, rsq01, colsel_q, qc0, qb0, t_sq0),
                    (t_r1, rsq23, colsel_q, qc1, qb1, t_sq1),
                    (t_rk, kv_rhs, colsel_kv, kvc, kvb, t_sqk)):
                d5 = bk.acquire(5)
                t_bq = mm(lambda e, sel=sel, rr=rr: e.matmul(
                    PF[5][:, 0:TW], sel[:, :], rr[:, :], start=True, stop=True),
                    deps=[t_seed] + d5, inc=True)
                t_qb = vec(lambda e, src=src, dst=dst: e.tensor_mul(
                    dst[:], src[:], PF[5][:, 0:TW]), deps=[t_bq, t_scr])
                bk.read(5, t_qb)
                qbt.append(t_qb)
            t_qb0, t_qb1, t_kvb = qbt
            t_o0 = dma("sync", lambda e: e.dma_start(out=qodd0[:, :], in_=qb0[64:128, :]),
                       deps=[t_qb0])
            t_o1 = dma("sync", lambda e: e.dma_start(out=qodd1[:, :], in_=qb1[64:128, :]),
                       deps=[t_qb1])
            kv["t_qb"] = (t_qb0, t_qb1, t_kvb, t_o0, t_o1)
            kv["xbf_war"][par].append(ctx.tok("T"))
            if i == 0 and taps:
                tap("cp1", lambda: constpool[:, :], [t_nrc])
                tap("nrc", lambda: nrcol[:, :], [t_nrc])
                tap("nbb", lambda: negbb_rep[:, :], [kv["t_reps"]])
                tap("screp", lambda: screp[:, :], [kv["t_reps"]])
                tap("qb0", lambda: qb0[:, :], [t_qb0])
                tap("kvb", lambda: kvb[:, :], [t_kvb])
                tap("xraw", lambda: qc0[:, :], [t_cq0])
                tap("rr", lambda: rrep[:, :], [t_rrep])
                tap("gate", lambda: gate_tm[:, :],
                    [kv["t_beta3"], kv["t_g3"], ctx.tok("S")])
                tap("bg", lambda: beta_c[:, :], [kv["t_beta3"]],
                    dst=lambda: tap_d["bg"].ap()[:, 0:4])
                tap("bg", lambda: g_c[:, :], [kv["t_g3"]],
                    dst=lambda: tap_d["bg"].ap()[:, 4:8])
        def emit_tile_rest(i):
            mm, vec, sca, dma = ctx.mm, ctx.vec, ctx.sca, ctx.dma
            par = i % 2
            tb = TB[par]
            ogfm = tb["ogfm"]
            if SCHED == "seq":
                for cc in range(4):
                    drive(gen_indep(i, cc, tb))
                    drive(gen_chain(i, cc, tb))
            elif SCHED == "pair":
                drive(gen_indep(i, 0, tb), gen_indep(i, 1, tb))
                drive(gen_chain(i, 0, tb))
                drive(gen_indep(i, 2, tb))
                drive(gen_chain(i, 1, tb))
                drive(gen_indep(i, 3, tb))
                drive(gen_chain(i, 2, tb))
                drive(gen_chain(i, 3, tb))
            else:
                drive(gen_indep(i, 0, tb), gen_indep(i, 1, tb))
                drive(gen_chain(i, 0, tb), gen_indep(i, 2, tb))
                drive(gen_chain(i, 1, tb), gen_indep(i, 3, tb))
                fnext = gen_front(i + 1) if i + 1 < NT else None
                fnext = drive2(gen_chain(i, 2, tb), fnext)
                fnext = drive2(gen_chain(i, 3, tb), fnext)
                if fnext is not None:
                    drive(fnext)
            if i == 0 and taps:
                tap("Sf", lambda: S_f[:, :], [kv["t_Sf"]])
                tap("Of", lambda: CB[1]["O_f"][:, :], [kv["t_of0"]])
                tap("ogfm", lambda: ogfm[:, :], list(kv["og_done"]))
            quarter, qoff = (TW * i) // TOWN, (TW * i) % TOWN
            ogdeps = kv["og_done"]
            kv["og_done"] = []
            for r in range(8):
                bnk = 1 if r % 2 == 0 else 2
                half = r % 2
                dd = bk.acquire(bnk)
                for j in range(2):
                    mm(lambda e, r=r, j=j, bnk=bnk: e.matmul(
                        PF[bnk][:, 0:TW],
                        wpr_b[:, 1024 * j + 128 * r:1024 * j + 128 * (r + 1)],
                        ogfm[:, TW * j:TW * (j + 1)], start=(j == 0), stop=(j == 1)),
                        deps=(dd + ogdeps + [kv["t_wload"]]) if j == 0 else (),
                        inc=(j == 1))
                t_pm = ctx.tok("T")
                t_pe = vec(lambda e, bnk=bnk, half=half: e.tensor_copy(
                    pystage[:, TW * half:TW * (half + 1)], PF[bnk][:, 0:TW]),
                    deps=[t_pm, kv["py_war"][half]])
                bk.read(bnk, t_pe)
                t_pd = dma("scalar", lambda e, r=r, quarter=quarter, qoff=qoff, half=half:
                           e.dma_start(
                               out=py_in.ap()[quarter, 128 * r:128 * (r + 1),
                                              qoff:qoff + TW],
                               in_=pystage[:, TW * half:TW * (half + 1)]),
                           deps=[t_pe])
                kv["py_war"][half] = t_pd
                kv["py_done"].append(t_pd)
            kv[f"tsnap{par}"] = dict(ctx.counts)

        def drive2(primary, filler):
            while True:
                try:
                    next(primary)
                except StopIteration:
                    break
                if filler is not None:
                    try:
                        next(filler)
                    except StopIteration:
                        filler = None
            return filler

        def drive(*gens):
            gens = [g for g in gens if g is not None]
            while gens:
                nxt = []
                for g in gens:
                    try:
                        next(g)
                        nxt.append(g)
                    except StopIteration:
                        pass
                gens = nxt

        def gen_indep(i, cc, tb):
            mm, vec, sca = ctx.mm, ctx.vec, ctx.sca
            qb0, qb1, kvb = tb["qb0"], tb["qb1"], tb["kvb"]
            qodd0, qodd1 = tb["qodd0"], tb["qodd1"]
            gate_tm, beta_c, g_c = tb["gate_tm"], tb["beta_c"], tb["g_c"]
            b = CB[cc]
            BA, BB = (0, 1) if cc % 2 == 0 else (2, 5)
            BC = BB
            gpi = cc % 2
            sl = slice(128 * cc, 128 * (cc + 1))
            t_qb0, t_qb1, t_kvb, t_o0, t_o1 = kv["t_qb"]
            t_beta = kv[f"t_beta{cc}"]
            t_g = kv[f"t_g{cc}"]
            d0 = bk.acquire(BA)
            t_kk = mm(lambda e: e.matmul(PF[BA][:, 0:128], kvb[0:64, sl], kvb[0:64, sl],
                                         start=True, stop=True),
                      deps=[t_kvb] + d0, inc=True)
            t_kkb = sca(lambda e: e.activation(b["KKb"][:], PF[BA][:, 0:128], ACT.Copy,
                                               scale=beta_c[:, cc:cc + 1]),
                        deps=[t_kk, t_beta])
            bk.read(BA, t_kkb)
            yield
            d1 = bk.acquire(BB)
            mm(lambda e: e.matmul(PF[BB][:, 0:1], triu1_f[:, :], g_c[:, cc:cc + 1],
                                  start=True, stop=True), deps=[t_g] + d1)
            mm(lambda e: e.matmul(PF[BB][:, 64:65], tril0_f[:, :], g_c[:, cc:cc + 1],
                                  start=True, stop=True))
            t_cs = mm(lambda e: e.matmul(PF[BB][0:1, 128:256], g_c[:, cc:cc + 1],
                                         triu1_f[:, :], start=True, stop=True), inc=True)
            t_eb = sca(lambda e: e.activation(b["ebcol"][:], PF[BB][:, 0:1], ACT.Exp),
                       deps=[t_cs])
            t_dl = sca(lambda e: e.activation(b["dlcol"][:], PF[BB][:, 64:65], ACT.Exp),
                       deps=[t_cs])
            t_bc = sca(lambda e: e.copy(b["bcol"][:], PF[BB][:, 0:1]), deps=[t_cs])
            t_nbc = sca(lambda e: e.mul(b["nbcol"][:], PF[BB][:, 0:1], -1.0), deps=[t_cs])
            t_brr = sca(lambda e: e.copy(b["brow"][:], PF[BB][0:1, 128:256]), deps=[t_cs])
            bk.read(BB, t_brr)
            bk.read(BB, t_nbc)
            yield
            t_ebl = vec(lambda e: e.tensor_mul(b["eblcol"][:], b["ebcol"][:], b["dlcol"][:]),
                        deps=[t_eb, t_dl])
            t_bb = vec(lambda e: e.tensor_mul(b["bebe"][:], beta_c[:, cc:cc + 1],
                                              b["ebcol"][:]), deps=[t_eb, t_beta])
            d2 = bk.acquire(BC)
            t_df = mm(lambda e: e.matmul(PF[BC][:, 0:128], ones_f[:, :], b["brow"][:, :],
                                         start=True, stop=True),
                      deps=[t_brr] + d2, inc=True)
            t_dts = sca(lambda e: e.activation(b["dect"][:], PF[BC][:, 0:128], ACT.Exp,
                                               scale=-1.0, bias=b["bcol"][:, :]),
                        deps=[t_df, t_bc])
            t_dst = sca(lambda e: e.activation(b["dmst"][:], PF[BC][:, 0:128], ACT.Exp,
                                               scale=1.0, bias=b["nbcol"][:, :]),
                        deps=[t_df, t_nbc])
            bk.read(BC, t_dts)
            bk.read(BC, t_dst)
            yield
            t_dm = vec(lambda e: e.tensor_mul(b["dmst"][:], b["dmst"][:], triu1_f[:]),
                       deps=[t_dst])
            t_m1 = vec(lambda e: e.tensor_mul(b["tmpf"][:], b["dect"][:], b["KKb"][:]),
                       deps=[t_dts, t_kkb])
            t_m2 = vec(lambda e: e.tensor_mul(b["M_b"][:], b["tmpf"][:], tril0_f[:]),
                       deps=[t_m1])
            yield
            dg0 = bk.acquire(6 + gpi)
            t_tm = mm(lambda e: e.transpose(PG[gpi][:, 0:128], b["M_b"][:, :], idb[:, :]),
                      deps=[t_m2] + dg0, inc=True)
            t_b0 = vec(lambda e: e.tensor_copy(b["B0_b"][:], PG[gpi][:, 0:128]),
                       deps=[t_tm])
            bk.read(6 + gpi, t_b0)
            t_Y = vec(lambda e: e.tensor_sub(b["Y_b"][:], idb[:], b["B0_b"][:]),
                      deps=[t_b0])
            yield
            dg1 = bk.acquire(6 + gpi)
            t_tk = mm(lambda e: e.transpose(PG[gpi][:, 0:128], kvb[:, sl], idb[:, :]),
                      deps=[t_kvb] + dg1, inc=True)
            t_r1 = vec(lambda e: e.tensor_scalar_mul(b["RHS2"][:, 0:64], PG[gpi][:, 0:64],
                                                     b["bebe"][:, :]), deps=[t_tk, t_bb])
            t_r2 = vec(lambda e: e.tensor_scalar_mul(b["RHS2"][:, 64:128],
                                                     PG[gpi][:, 64:128],
                                                     beta_c[:, cc:cc + 1]),
                       deps=[t_tk, t_beta])
            t_kh = vec(lambda e: e.tensor_scalar_mul(b["khat"][:], PG[gpi][:, 0:64],
                                                     b["dlcol"][:, :]), deps=[t_tk, t_dl])
            bk.read(6 + gpi, t_r1)
            bk.read(6 + gpi, t_r2)
            bk.read(6 + gpi, t_kh)
            yield
            Acur, Bcur = b["M_b"][:, 0:128], b["B0_b"][:, 0:128]
            t_A = [t_m2, t_b0]
            for kk in range(1, 7):
                ABn = (b["AB0"], b["AB1"])[kk % 2]
                d0 = bk.acquire(BA)
                t_am = mm(lambda e, Bcur=Bcur, Acur=Acur: e.matmul(
                    PF[BA][:, 0:128], Bcur, Acur, start=True, stop=True),
                    deps=t_A + d0, inc=(kk == 6))
                if kk < 6:
                    t_am = mm(lambda e, Acur=Acur, Bcur=Bcur: e.matmul(
                        PF[BA][:, 128:256], Acur, Bcur, start=True, stop=True),
                        inc=True)
                    t_An = vec(lambda e, ABn=ABn: e.tensor_copy(ABn[:, 0:256],
                                                               PF[BA][:, 0:256]),
                               deps=[t_am])
                else:
                    t_An = vec(lambda e, ABn=ABn: e.tensor_copy(ABn[:, 0:128],
                                                               PF[BA][:, 0:128]),
                               deps=[t_am])
                bk.read(BA, t_An)
                yield
                d2 = bk.acquire(BC)
                t_ym = mm(lambda e, ABn=ABn: e.matmul(
                    PF[BC][:, 0:128], ABn[:, 0:128], b["Y_b"][:, :],
                    start=True, stop=True),
                    deps=[t_An, t_Y] + d2, inc=True)
                t_Y = vec(lambda e: e.tensor_add(b["Y_b"][:], b["Y_b"][:],
                                                 PF[BC][:, 0:128]),
                          deps=[t_ym])
                bk.read(BC, t_Y)
                Acur, Bcur, t_A = ABn[:, 0:128], ABn[:, 128:256], [t_An]
                yield
            d0 = bk.acquire(BA)
            t_u2m = mm(lambda e: e.matmul(PF[BA][:, 0:128], b["Y_b"][:, :], b["RHS2"][:, :],
                                          start=True, stop=True),
                       deps=[t_Y, t_r1, t_r2] + d0, inc=True)
            t_u2f = vec(lambda e: e.tensor_copy(b["U2f"][:], PF[BA][:, 0:128]),
                        deps=[t_u2m])
            bk.read(BA, t_u2f)
            t_u2fin = t_u2f
            yield
            d1 = bk.acquire(BB)
            t_u2t = mm(lambda e: e.matmul(PF[BB][:, 0:128], b["RHS2"][:, :], b["Y_b"][:, :],
                                          start=True, stop=True), deps=d1, inc=True)
            t_wt = vec(lambda e: e.tensor_copy(b["WtU"][0:64, :], PF[BB][0:64, 0:128]),
                       deps=[t_u2t])
            bk.read(BB, t_wt)
            yield
            qsrc = ((qb0, t_qb0), (qodd0, t_o0), (qb1, t_qb1), (qodd1, t_o1))
            d0 = bk.acquire(BC)
            for j in range(4):
                qt, t_qs = qsrc[j]
                mm(lambda e, qt=qt, j=j: e.matmul(
                    PF[BC][:, 128 * j:128 * (j + 1)], kvb[0:64, sl], qt[0:64, sl],
                    start=True, stop=True),
                    deps=([t_qs, t_kvb] + d0) if j == 0 else [t_qs], inc=(j == 3))
            t_kq = ctx.tok("T")
            yield
            t_am = vec(lambda e: e.tensor_tensor(
                out=b["amsk"][:, 0:512].rearrange("p (r c) -> p r c", r=4),
                in0=PF[BC][:, 0:512].rearrange("p (r c) -> p r c", r=4),
                in1=b["dmst"][:].rearrange("p (x c) -> p x c", x=1).broadcast_to(
                    [128, 4, 128]),
                op=ALU.mult), deps=[t_kq, t_dm])
            bk.read(BC, t_am)
            for j in range(4):
                kv[f"t_amsk{cc}_{j}"] = t_am
            yield
            t_grw = vec(lambda e, cc=cc: e.tensor_mul(
                b["grw"][:], gate_tm[:, 256 * cc:256 * (cc + 1)], rmsw_rep[:]),
                deps=[kv["t_rmsw"], kv[f"t_beta{cc}"]])
            kv[f"t_ind{cc}"] = dict(t_wt=t_wt, t_u2fin=t_u2fin, t_kh=t_kh,
                                    t_eb=t_eb, t_ebl=t_ebl, t_grw=t_grw,
                                    t_dm=t_dm)

        def gen_chain(i, cc, tb):
            mm, vec, sca = ctx.mm, ctx.vec, ctx.sca
            qb0, qb1, kvb = tb["qb0"], tb["qb1"], tb["kvb"]
            qodd0, qodd1 = tb["qodd0"], tb["qodd1"]
            gate_tm, beta_c, g_c = tb["gate_tm"], tb["beta_c"], tb["g_c"]
            ogfm = tb["ogfm"]
            b = CB[cc]
            BA, BB = 3, 4
            gpi = (cc + 1) % 2
            ind = kv[f"t_ind{cc}"]
            sl = slice(128 * cc, 128 * (cc + 1))
            t_qb0, t_qb1, t_kvb, t_o0, t_o1 = kv["t_qb"]
            d0 = bk.acquire(BA)
            t_ws = mm(lambda e: e.matmul(PF[BA][:, 0:64], b["WtU"][0:64, :],
                                         S_b[0:64, 0:64], start=True, stop=True),
                      deps=[ind["t_wt"], kv["t_Sb"]] + d0, inc=True)
            t_ub = vec(lambda e: e.scalar_tensor_tensor(
                b["U_b"][:], PF[BA][:, 0:64], -1.0, b["U2f"][:, 64:128],
                op0=ALU.mult, op1=ALU.add), deps=[t_ws, ind["t_u2fin"]])
            bk.read(BA, t_ub)
            yield
            qsrc = ((qb0, t_qb0), (qodd0, t_o0), (qb1, t_qb1), (qodd1, t_o1))
            d4 = bk.acquire(BB)
            d5 = bk.acquire(BA)
            for j in range(4):
                qt, t_qs = qsrc[j]
                mm(lambda e, j=j: e.matmul(
                    PF[BB][:, 64 * j:64 * (j + 1)], b["amsk"][:, 128 * j:128 * (j + 1)],
                    b["U_b"][:, :], start=True, stop=True),
                    deps=([kv[f"t_amsk{cc}_0"], t_ub] + d4) if j == 0 else (),
                    inc=(j == 3))
                if j == 0:
                    t_o1m = None
                mm(lambda e, qt=qt, j=j: e.matmul(
                    PF[BA][:, 64 * j:64 * (j + 1)], qt[0:64, sl], S_b[0:64, 0:64],
                    start=True, stop=True),
                    deps=([kv["t_Sb"]] + d5) if j == 0 else (), inc=(j == 3))
                if j == 1:
                    yield
            t_omm = ctx.tok("T")
            yield
            t_ebs = sca(lambda e: e.activation(b["ebtmp"][:, 0:256], PF[BA][:, 0:256],
                                               ACT.Copy, scale=b["ebcol"][:, :]),
                        deps=[t_omm, ind["t_eb"]])
            t_of = vec(lambda e: e.tensor_add(b["O_f"][:, 0:256], PF[BB][:, 0:256],
                                              b["ebtmp"][:, 0:256]),
                       deps=[t_omm, t_ebs])
            bk.read(BA, t_ebs)
            bk.read(BB, t_of)
            for j in range(4):
                kv[f"t_of{j}"] = t_of
            yield
            d0 = bk.acquire(BA)
            t_ku = mm(lambda e: e.matmul(PF[BA][0:64, 0:64], b["khat"][:, 0:64],
                                         b["U_b"][:, :], start=True, stop=True),
                      deps=[ind["t_kh"], t_ub] + d0, inc=True)
            t_Sf = vec(lambda e: e.scalar_tensor_tensor(
                S_f[:], S_f[:], b["eblcol"][0:64, :], PF[BA][0:64, 0:64],
                op0=ALU.mult, op1=ALU.add), deps=[t_ku, ind["t_ebl"], kv["t_Sf"]])
            bk.read(BA, t_Sf)
            t_Sb = vec(lambda e: e.tensor_copy(S_b[:], S_f[:]), deps=[t_Sf])
            kv["t_Sf"] = t_Sf
            kv["t_Sb"] = t_Sb
            yield
            t_osq = sca(lambda e: e.activation(b["Osq"][:], b["O_f"][:], ACT.Square),
                        deps=[kv["t_of0"]])
            t_red = vec(lambda e: e.reduce_sum(
                b["rms_in"][:, 0:4].rearrange("p (r x) -> p r x", x=1),
                b["Osq"][:, 0:256].rearrange("p (r c) -> p r c", r=4), AX.X),
                deps=[t_osq])
            t_rl = sca(lambda e: e.activation(b["rms4"][:], b["rms_in"][:], ACT.Ln,
                                              scale=1.0 / 64, bias=1e-5), deps=[t_red])
            t_rr = sca(lambda e: e.activation(b["rms4"][:], b["rms4"][:], ACT.Exp,
                                              scale=-0.5), deps=[t_rl])
            yield
            t_og1 = vec(lambda e: e.tensor_tensor(
                out=b["ebtmp"][:, 0:256].rearrange("p (r c) -> p r c", r=4),
                in0=b["O_f"][:, 0:256].rearrange("p (r c) -> p r c", r=4),
                in1=b["rms4"][:, 0:4].rearrange("p (r x) -> p r x", x=1).broadcast_to(
                    [128, 4, 64]),
                op=ALU.mult), deps=[t_rr, kv["t_of0"]])
            t_og = vec(lambda e: e.tensor_mul(b["og_bf"][:, 0:256],
                                              b["ebtmp"][:, 0:256], b["grw"][:, 0:256]),
                       deps=[t_og1, ind["t_grw"]])
            yield
            for half in range(2):
                dg = bk.acquire(6 + gpi)
                t_t = mm(lambda e, half=half: e.transpose(
                    PG[gpi][:, 0:128], b["og_bf"][:, 128 * half:128 * (half + 1)],
                    idb[:, :]), deps=[t_og] + dg, inc=True)
                t_e = vec(lambda e, half=half, cc=cc: e.tensor_copy(
                    ogfm[:, 512 * half + 128 * cc:512 * half + 128 * (cc + 1)],
                    PG[gpi][:, 0:128]), deps=[t_t])
                bk.read(6 + gpi, t_e)
                kv["og_done"].append(t_e)
                yield

        def phase15():
            mm, vec, sca, dma = ctx.mm, ctx.vec, ctx.sca, ctx.dma
            for ii in range(TOWN // TW):
                par = ii % 2
                ctx.guard()
                war = kv["xq_war"][par]
                kv["xq_war"][par] = []
                for r in range(8):
                    dma("sync", lambda e, r=r, ii=ii, par=par: e.dma_start(
                        out=xq[par][:, TW * r:TW * (r + 1)],
                        in_=x_own.ap()[128 * r:128 * (r + 1), TW * ii:TW * (ii + 1)]),
                        deps=war)
                t_xl = ctx.tok("DQS")
                t_cast, t_m, t_r = emit_ln(par, t_xl)
                t_mr = vec(lambda e: e.tensor_mul(mr_row[:], m_row[:], rstd_row[:]),
                           deps=[t_r])
                d5 = bk.acquire(5)
                t_rr = mm(lambda e: e.matmul(PF[5][:, 0:TW], ones_f[:, :],
                                             rstd_row[:, :], start=True, stop=True),
                          deps=[t_r] + d5, inc=True)
                t_rrep = vec(lambda e: e.tensor_copy(rrep[:], PF[5][:, 0:TW]),
                             deps=[t_rr])
                bk.read(5, t_rrep)
                d5 = bk.acquire(5)
                t_mm2 = mm(lambda e: e.matmul(PF[5][:, 0:TW], ones_f[:, :],
                                              mr_row[:, :], start=True, stop=True),
                           deps=[t_mr] + d5, inc=True)
                t_mrep = vec(lambda e: e.tensor_copy(mrrep[:], PF[5][:, 0:TW]),
                             deps=[t_mm2])
                bk.read(5, t_mrep)
                for r in range(8):
                    t_mx = vec(lambda e, r=r, par=par: e.tensor_mul(
                        qc0[:], xq[par][:, TW * r:TW * (r + 1)], rrep[:]),
                        deps=[t_rrep])
                    t_x = vec(lambda e, r=r, ii=ii: e.tensor_sub(
                        xn_own[:, TOWN * r + TW * ii:TOWN * r + TW * (ii + 1)],
                        qc0[:], mrrep[:]), deps=[t_mrep, t_mx])
                    kv["xq_war"][par].append(t_x)
            kv["t_xn"] = ctx.tok("V")

        def phase2():
            if use_cc:
                kv["t_cc"] = ctx.op("gpsimd", lambda e: e.collective_compute(
                    "ReduceScatter", ALU.add,
                    replica_groups=[[0, 1, 2, 3], [4, 5, 6, 7]],
                    ins=[py_in.ap().opt()], outs=[py_out.ap().opt()]),
                    deps=kv["py_done"], inc="CC", inc_n=1)
            else:
                kv["t_cc"] = ctx.op("gpsimd", lambda e: e.dma_start(
                    out=py_out.ap()[:, :], in_=py_in.ap()[0, :, :]),
                    deps=kv["py_done"], inc="CC", inc_n=1)

        def tap(name, src_ap_fn, deps, dst=None):
            if not taps or name not in tap_d:
                return None
            return ctx.dma("sync", lambda e: e.dma_start(
                out=tap_d[name].ap() if dst is None else dst(),
                in_=src_ap_fn()), deps=deps)

        def emit_ffn_pass(p):
            mm, vec, sca, dma = ctx.mm, ctx.vec, ctx.sca, ctx.dma
            war = kv.setdefault("fw_war", [[], []])
            scr_war = kv.setdefault("fscr_war", [None, None, None, None])
            l1war0 = kv.setdefault("l1_war", [[], [], [], []])
            if p == 1:
                for wi in range(2):
                    for slot in range(2):
                        l1war0[2 * wi + slot] = l1war0[2 * wi + slot] + war[wi]
            l1war = kv.setdefault("l1_war", [[], [], [], []])
            for ob in range(32):
                slot = ob % 2
                co = 1024 * slot
                wt = [None, None]
                for wi, wdr in enumerate((w1t, w2t)):
                    wt[wi] = dma("gpsimd", lambda e, wdr=wdr, ob=ob, wi=wi, co=co:
                                 e.dma_start(
                                     out=Wst[wi][:, co:co + 1024].rearrange(
                                         "p (s c) -> p s c", s=8),
                                     in_=wdr.ap()[:, 128 * ob:128 * (ob + 1)].rearrange(
                                         "(s p) c -> p s c", p=128)),
                                 deps=l1war[2 * wi + slot] + [kv["t_cc"]])
                    l1war[2 * wi + slot] = []
                nb = 2 * NTG
                dd = []
                for b in range(nb):
                    dd += bk.acquire(b)
                for wi in range(2):
                    for tg in range(NTG):
                        b = wi * NTG + tg
                        for k in range(8):
                            last = (k == 7 and wi == 1 and tg == NTG - 1)
                            mm(lambda e, k=k, wi=wi, tg=tg, b=b, p=p, co=co: e.matmul(
                                PF[b][:, 0:PTW],
                                Wst[wi][:, co + 128 * k:co + 128 * (k + 1)],
                                xn_own[:, TOWN * k + PT * p + PTW * tg:
                                       TOWN * k + PT * p + PTW * (tg + 1)],
                                start=(k == 0), stop=(k == 7)),
                                deps=(dd + [wt[0], wt[1], kv["t_xn"]]) if (k == 0 and wi == 0 and tg == 0) else (),
                                inc=last)
                t_l1 = ctx.tok("T")
                l1war[2 * 0 + slot].append(t_l1)
                l1war[2 * 1 + slot].append(t_l1)
                war[0].append(t_l1) if ob == 31 else None
                war[1].append(t_l1) if ob == 31 else None
                for tg in range(NTG):
                    t_e1 = sca(lambda e, ob=ob, tg=tg: e.activation(
                        ft1[:], PF[tg][:, 0:PTW], ACT.Silu,
                        scale=ffn_scales[:, ob:ob + 1]),
                        deps=[t_l1, kv["t_scales"], scr_war[0]])
                    t_e2 = sca(lambda e, ob=ob, tg=tg: e.activation(
                        ft2[:], PF[NTG + tg][:, 0:PTW], ACT.Copy,
                        scale=ffn_scales[:, 32 + ob:33 + ob]),
                        deps=[scr_war[1]])
                    t_h = vec(lambda e, ob=ob, tg=tg: e.tensor_mul(
                        ffnA[:, PT * ob + PTW * tg:PT * ob + PTW * (tg + 1)],
                        ft1[:], ft2[:]), deps=[t_e1, t_e2])
                    bk.read(tg, t_e1)
                    bk.read(NTG + tg, t_e2)
                    scr_war[0] = t_h
                    scr_war[1] = t_h
            t_src = ctx.tok("V")
            kv["tapA1"] = tap("A1", lambda: ffnA[:, :], [t_src]) if p == 0 else None
            xs, osg = fxs, fos
            for lay, (wdr, src, dst, scoff, nob) in enumerate(
                    ((h1t, ffnA, ffnB, 64, 32), (h2t, ffnB, ffnA, 96, 32),
                     (w3t, ffnA, None, 128, 8))):
                nxt = None
                for ob in range(nob):
                    par = ob % 2
                    t_wl = dma("gpsimd", lambda e, wdr=wdr, ob=ob, par=par: e.dma_start(
                        out=Wst[par][:, :].rearrange("p (s c) -> p s c", s=32),
                        in_=wdr.ap()[:, 128 * ob:128 * (ob + 1)].rearrange(
                            "(s p) c -> p s c", p=128)), deps=war[par] + [kv["t_cc"]])
                    war[par] = []
                    if lay == 2:
                        t_xl = dma("sync", lambda e, ob=ob, p=p: e.dma_start(
                            out=xs[:, :], in_=x_own.ap()[128 * ob:128 * (ob + 1),
                                                         PT * p:PT * (p + 1)]),
                            deps=[kv.get("t_l4adds")])
                        t_yl0 = dma("sync", lambda e, ob=ob, p=p: e.dma_start(
                            out=fys_b[:, :], in_=py_out.ap()[128 * ob:128 * (ob + 1),
                                                             PT * p:PT * (p + 1)]),
                            deps=[kv["t_cc"], kv.get("t_fysc")])
                        t_yl = vec(lambda e: e.tensor_copy(fys[:, :], fys_b[:, :]),
                                   deps=[t_yl0, kv.get("t_l4adds")])
                        kv["t_fysc"] = t_yl
                    dd = []
                    for tg in range(NTG):
                        dd += bk.acquire(tg)
                    t_srcdep = t_src if lay != 2 else kv["l3_done"]
                    for tg in range(NTG):
                        for k in range(32):
                            last = (k == 31 and tg == NTG - 1)
                            mm(lambda e, k=k, tg=tg, par=par, src=src: e.matmul(
                                PF[tg][:, 0:PTW], Wst[par][:, 128 * k:128 * (k + 1)],
                                src[:, PT * k + PTW * tg:PT * k + PTW * (tg + 1)],
                                start=(k == 0), stop=(k == 31)),
                                deps=(dd + [t_wl, t_srcdep]) if (k == 0 and tg == 0) else (),
                                inc=last)
                    t_mmk = ctx.tok("T")
                    war[par].append(t_mmk)
                    for tg in range(NTG):
                        if lay != 2:
                            scr = (ft1, ft2)[tg % 2]
                            t_gx = sca(lambda e, ob=ob, tg=tg, scoff=scoff, scr=scr:
                                       e.activation(scr[:], PF[tg][:, 0:PTW], ACT.Gelu,
                                                    scale=ffn_scales[:, scoff + ob:scoff + ob + 1]),
                                       deps=[t_mmk, scr_war[2 + tg % 2]])
                            t_cx = vec(lambda e, ob=ob, tg=tg, dst=dst, scr=scr: e.tensor_copy(
                                dst[:, PT * ob + PTW * tg:PT * ob + PTW * (tg + 1)], scr[:]),
                                deps=[t_gx, kv["tapA1"] if lay == 1 else None])
                            bk.read(tg, t_gx)
                            scr_war[2 + tg % 2] = t_cx
                            nxt = t_cx
                        else:
                            oso = PT * (ob % 2)
                            fw = kv.setdefault("fos_war", [None, None])
                            t_y = sca(lambda e, ob=ob, tg=tg, oso=oso: e.activation(
                                osg[:, oso + PTW * tg:oso + PTW * (tg + 1)],
                                PF[tg][:, 0:PTW], ACT.Copy,
                                scale=ffn_scales[:, 128 + ob:129 + ob]),
                                deps=[t_mmk, fw[ob % 2]])
                            bk.read(tg, t_y)
                            t_a1 = vec(lambda e, tg=tg, oso=oso: e.tensor_add(
                                osg[:, oso + PTW * tg:oso + PTW * (tg + 1)],
                                osg[:, oso + PTW * tg:oso + PTW * (tg + 1)],
                                xs[:, PTW * tg:PTW * (tg + 1)]), deps=[t_y, t_xl])
                            t_a2 = vec(lambda e, tg=tg, oso=oso: e.tensor_add(
                                osg[:, oso + PTW * tg:oso + PTW * (tg + 1)],
                                osg[:, oso + PTW * tg:oso + PTW * (tg + 1)],
                                fys[:, PTW * tg:PTW * (tg + 1)]), deps=[t_a1, t_yl])
                    if lay == 2:
                        oso = PT * (ob % 2)
                        kv["t_l4adds"] = ctx.tok("V")
                        kv["fos_war"][ob % 2] = dma(
                            "scalar", lambda e, ob=ob, p=p, oso=oso: e.dma_start(
                                out=out_fm.ap()[128 * ob:128 * (ob + 1),
                                                PT * p:PT * (p + 1)],
                                in_=osg[:, oso:oso + PT]), deps=[kv["t_l4adds"]])
                        kv["t_l4dma"] = kv["fos_war"][ob % 2]
                if lay == 0:
                    if p == 0:
                        tap("B2", lambda: ffnB[:, :], [nxt])
                    t_src = nxt
                elif lay == 1:
                    if p == 0:
                        tap("A3", lambda: ffnA[:, :], [nxt])
                    kv["l3_done"] = nxt
            kv["out_done"] = ctx.tok("DQA")

        def emit_all():
            setup()
            if stop_after == "setup":
                pass
            else:
                phase15()
                war = kv["xq_war"][0]
                kv["xq_war"][0] = []
                for r in range(8):
                    ctx.dma("sync", lambda e, r=r: e.dma_start(
                        out=xq[0][:, TW * r:TW * (r + 1)],
                        in_=x_fm.ap()[128 * r:128 * (r + 1), 0:TW]), deps=war)
                kv["t_x0"] = ctx.tok("DQS")
                n_tiles = 1 if stop_after == "tile0" else NT
                drive(gen_front(0))
                for i in range(n_tiles):
                    emit_tile_rest(i)
                if stop_after not in ("tile0", "p1"):
                    tap("pyin", lambda: py_in.ap(), kv["py_done"])
                    tap("xn", lambda: xn_own[:, :], [kv["t_xn"]])
                    if stop_after != "p15":
                        if stop_after != "baronly":
                            phase2()
                        else:
                            kv["t_cc"] = kv["t_consts"]
                        tap("py", lambda: py_out.ap(), [kv["t_cc"]])
                        if stop_after != "cc":
                            ctx.barrier()
                            if stop_after not in ("bar", "baronly"):
                                emit_ffn_pass(0)
                                if stop_after != "ffn0":
                                    emit_ffn_pass(1)
            if stop_after is not None:
                kv["out_done"] = ctx.dma("scalar", lambda e: e.dma_start(
                    out=out_fm.ap()[0:128, 0:TW], in_=pystage[:, 0:TW]),
                    deps=kv.get("py_done", []))
            for eng in ENGS:
                ctx.op(eng, lambda e: None, [kv["out_done"]], None)

        with nc.Block() as block:
            for engname in ENGS:
                def run_pass(engobj, engname=engname):
                    ctx.begin(engname, engobj)
                    bk.war = {}
                    kv.clear()
                    emit_all()
                getattr(block, engname)(run_pass)
    return nc


def _core_inputs(inp, core):
    b, g = core // 4, core % 4
    x = np.asarray(inp["L0_attn_x"], np.float32)
    T = x.shape[1]
    TOWN = T // 4
    d = {}
    xt = np.ascontiguousarray(x[b].T)
    d["x_fm"] = xt
    d["x_own"] = np.ascontiguousarray(xt[:, TOWN * g:TOWN * (g + 1)])
    d["wq_t"] = np.ascontiguousarray(
        np.asarray(inp["L0_attn_Wq"], np.float32).T[:, 256 * g:256 * (g + 1)])
    wk = np.asarray(inp["L0_attn_Wk"], np.float32).T[:, 64 * g:64 * (g + 1)]
    wv = np.asarray(inp["L0_attn_Wv"], np.float32).T[:, 64 * g:64 * (g + 1)]
    d["wkv_t"] = np.ascontiguousarray(np.concatenate([wk, wv], axis=1))
    wg = np.asarray(inp["L0_attn_Wgate"], np.float32).T[:, 256 * g:256 * (g + 1)]
    wb = np.asarray(inp["L0_attn_Wbeta"], np.float32).T[:, g:g + 1]
    wgk = np.asarray(inp["L0_attn_Wgk"], np.float32).T[:, g:g + 1]
    d["wgbg_t"] = np.ascontiguousarray(np.concatenate([wg, wb, wgk], axis=1))
    d["cwq"] = np.ascontiguousarray(
        np.asarray(inp["L0_attn_cwq"], np.float32)[256 * g:256 * (g + 1)])
    cwk = np.asarray(inp["L0_attn_cwk"], np.float32)[64 * g:64 * (g + 1)]
    cwv = np.asarray(inp["L0_attn_cwv"], np.float32)[64 * g:64 * (g + 1)]
    d["cwkv"] = np.ascontiguousarray(np.concatenate([cwk, cwv], axis=0))
    d["scal"] = np.array([[inp["L0_attn_b_beta"][g], inp["L0_attn_A_log"][g],
                           inp["L0_attn_dt_bias"][g]]], np.float32)
    d["rmsw_row"] = np.ascontiguousarray(
        np.tile(np.asarray(inp["L0_attn_rms_w"], np.float32), 4)[None, :])
    d["wpr_t"] = np.ascontiguousarray(
        np.asarray(inp["L0_attn_Wproj"], np.float32).T[256 * g:256 * (g + 1), :])
    return d


def _shared_inputs(inp):
    d = {}
    d["w1t"] = np.ascontiguousarray(np.asarray(inp["L0_ffn_w1_w"], np.float32).T)
    d["w2t"] = np.ascontiguousarray(np.asarray(inp["L0_ffn_w2_w"], np.float32).T)
    d["h1t"] = np.ascontiguousarray(np.asarray(inp["L0_ffn_h1_w"], np.float32).T)
    d["h2t"] = np.ascontiguousarray(np.asarray(inp["L0_ffn_h2_w"], np.float32).T)
    d["w3t"] = np.ascontiguousarray(np.asarray(inp["L0_ffn_w3_w"], np.float32).T)
    d["sc1"] = np.ascontiguousarray(
        np.asarray(inp["L0_ffn_w1_s"], np.float32).reshape(32, 128).T)
    d["sc2"] = np.ascontiguousarray(
        np.asarray(inp["L0_ffn_w2_s"], np.float32).reshape(32, 128).T)
    d["sch1"] = np.ascontiguousarray(
        np.asarray(inp["L0_ffn_h1_s"], np.float32).reshape(32, 128).T)
    d["sch2"] = np.ascontiguousarray(
        np.asarray(inp["L0_ffn_h2_s"], np.float32).reshape(32, 128).T)
    d["sc3"] = np.ascontiguousarray(
        np.asarray(inp["L0_ffn_w3_s"], np.float32).reshape(8, 128).T)
    d["idn_b"] = np.eye(128).astype(BF)
    d["triu1"] = np.triu(np.ones((128, 128), np.float32))
    d["tril0"] = np.tril(np.ones((128, 128), np.float32), -1)
    csq = np.zeros((2, 128), np.float32)
    csq[0, 0:64] = 0.125
    csq[1, 64:128] = 0.125
    d["csq"] = csq
    cskv = np.zeros((2, 128), np.float32)
    cskv[0, 0:64] = 1.0
    cskv[1, 64:128] = 1.0
    d["cskv"] = cskv
    d["onesrow"] = np.ones((1, TW), np.float32)
    return d


def kernel(**inputs):
    inp = {k: np.asarray(v) for k, v in inputs.items()}
    T = inp["L0_attn_x"].shape[1]
    TOWN = T // 4
    nc = build_program(T=T)
    shared = _shared_inputs(inp)
    in_maps = []
    for core in range(8):
        d = _core_inputs(inp, core)
        d.update(shared)
        in_maps.append(d)
    res = run_bass_kernel_spmd(nc, in_maps, core_ids=list(range(8)))
    out = np.empty((2, T, 1024), np.float32)
    for core in range(8):
        b, g = core // 4, core % 4
        out[b, TOWN * g:TOWN * (g + 1), :] = res.results[core]["out_fm"].T
    return out


# version 17
# speedup vs baseline: 1.0500x; 1.0031x over previous
import contextlib

import numpy as np
import ml_dtypes

import concourse.bass as bass
import concourse.mybir as mybir
from concourse.bass_utils import run_bass_kernel_spmd

F32 = mybir.dt.float32
BF16 = mybir.dt.bfloat16
ALU = mybir.AluOpType
ACT = mybir.ActivationFunctionType
AX = mybir.AxisListType
BF = ml_dtypes.bfloat16

C = 1024
TW = 512

ENGS = ("sync", "tensor", "vector", "scalar", "gpsimd")
OWN_SEM = {"tensor": "T", "vector": "V", "scalar": "S", "gpsimd": "G", "sync": "Y"}


class Ctx:
    def __init__(self, nc, sems):
        self.nc = nc
        self.sems = sems
        self.cur = None
        self.e = None
        self.counts = {}
        self.waited = {}
        self.base_dep = None

    def begin(self, name, engobj):
        self.cur = name
        self.e = engobj
        self.counts = {k: 0 for k in self.sems}
        self.waited = {k: -1 for k in self.sems}
        self.base_dep = None

    def op(self, eng, fn, deps=(), inc=None, inc_n=1):
        tok = None
        if inc is not None:
            self.counts[inc] += inc_n
            tok = (inc, self.counts[inc])
        if self.cur == eng:
            own = OWN_SEM.get(eng)
            deps = list(deps)
            if self.base_dep is not None:
                deps.append(self.base_dep)
            nw = 0
            for dep in deps:
                if dep is None:
                    continue
                s, v = dep
                if v > self.waited[s]:
                    if nw and nw % 2 == 0:
                        self.e.nop(nofuse=True)
                    self.e.wait_ge(self.sems[s], v)
                    self.waited[s] = v
                    nw += 1
            inst = fn(self.e)
            if inc is not None and inst is not None:
                inst.then_inc(self.sems[inc], inc_n)
        return tok

    def mm(self, fn, deps=(), inc=False):
        return self.op("tensor", fn, deps, "T" if inc else None)

    def vec(self, fn, deps=(), inc=True):
        return self.op("vector", fn, deps, "V" if inc else None)

    def sca(self, fn, deps=(), inc=True):
        return self.op("scalar", fn, deps, "S" if inc else None)

    def dma(self, queue, fn, deps=()):
        q = {"sync": "DQS", "scalar": "DQA", "gpsimd": "DQG"}[queue]
        return self.op(queue, fn, deps, q, 16)

    def tok(self, sem):
        return (sem, self.counts[sem])

    def guard(self, engs=("tensor", "vector", "scalar", "sync"),
              sems=("T", "V", "S", "DQS", "DQA")):
        snap = dict(self.counts)
        for eng in engs:
            deps = [(s, snap[s]) for s in sems if s != OWN_SEM.get(eng)]
            self.op(eng, lambda e: None, deps, None)

    def barrier(self):
        self.guard(engs=ENGS, sems=("T", "V", "S", "DQS", "DQA", "DQG"))


class Banks:
    def __init__(self):
        self.war = {}

    def acquire(self, b):
        d = self.war.get(b, [])
        self.war[b] = []
        return d

    def read(self, b, tok):
        if tok is not None:
            self.war.setdefault(b, []).append(tok)


SCHED = "full"


def build_program(T=8192, use_cc=True, stop_after=None, taps=False):
    nc = bass.Bass("TRN2", target_bir_lowering=False,
                   detect_race_conditions=False)
    NT = T // TW
    TOWN = T // 4
    PT = TOWN // 2
    PTW = min(512, PT)
    NTG = PT // PTW
    NPRE = 16

    x_fm = nc.dram_tensor("x_fm", [C, T], F32, kind="ExternalInput")
    x_own = nc.dram_tensor("x_own", [C, TOWN], F32, kind="ExternalInput")
    wq_t = nc.dram_tensor("wq_t", [C, 256], F32, kind="ExternalInput")
    wkv_t = nc.dram_tensor("wkv_t", [C, 128], F32, kind="ExternalInput")
    wgbg_t = nc.dram_tensor("wgbg_t", [C, 258], F32, kind="ExternalInput")
    cwq_d = nc.dram_tensor("cwq", [256, 4], F32, kind="ExternalInput")
    cwkv_d = nc.dram_tensor("cwkv", [128, 4], F32, kind="ExternalInput")
    scal_d = nc.dram_tensor("scal", [1, 3], F32, kind="ExternalInput")
    rmsw_d = nc.dram_tensor("rmsw_row", [1, 256], F32, kind="ExternalInput")
    wpr_t = nc.dram_tensor("wpr_t", [256, C], F32, kind="ExternalInput")
    w1t = nc.dram_tensor("w1t", [C, 4096], F32, kind="ExternalInput")
    w2t = nc.dram_tensor("w2t", [C, 4096], F32, kind="ExternalInput")
    h1t = nc.dram_tensor("h1t", [4096, 4096], F32, kind="ExternalInput")
    h2t = nc.dram_tensor("h2t", [4096, 4096], F32, kind="ExternalInput")
    w3t = nc.dram_tensor("w3t", [4096, C], F32, kind="ExternalInput")
    sc_d = {}
    for nm, w in (("sc1", 32), ("sc2", 32), ("sch1", 32), ("sch2", 32), ("sc3", 8)):
        sc_d[nm] = nc.dram_tensor(nm, [128, w], F32, kind="ExternalInput")
    idn_d = nc.dram_tensor("idn_b", [128, 128], BF16, kind="ExternalInput")
    triu1_d = nc.dram_tensor("triu1", [128, 128], F32, kind="ExternalInput")
    tril0_d = nc.dram_tensor("tril0", [128, 128], F32, kind="ExternalInput")
    csq_d = nc.dram_tensor("csq", [2, 128], F32, kind="ExternalInput")
    cskv_d = nc.dram_tensor("cskv", [2, 128], F32, kind="ExternalInput")
    onesrow_d = nc.dram_tensor("onesrow", [1, TW], F32, kind="ExternalInput")

    out_fm = nc.dram_tensor("out_fm", [C, TOWN], F32, kind="ExternalOutput")
    py_in = nc.dram_tensor("py_in", [4, C, TOWN], BF16, kind="Internal")
    py_out = nc.dram_tensor("py_out", [C, TOWN], BF16, kind="Internal")
    PT_ = TOWN // 2
    tap_d = {}
    if isinstance(taps, (tuple, list, set)):
        shapes = {"A1": ([128, 32 * PT_], BF16), "B2": ([128, 32 * PT_], BF16),
                  "A3": ([128, 32 * PT_], BF16), "xn": ([128, 8 * TOWN], BF16),
                  "py": ([C, TOWN], BF16)}
        for nm in taps:
            tap_d[nm] = nc.dram_tensor("tap_" + nm, list(shapes[nm][0]),
                                       shapes[nm][1], kind="ExternalOutput")
    elif taps == "min":
        for nm in ("A1", "B2", "A3"):
            tap_d[nm] = nc.dram_tensor("tap_" + nm, [128, 32 * PT_], BF16,
                                       kind="ExternalOutput")
        tap_d["xn"] = nc.dram_tensor("tap_xn", [128, 8 * TOWN], BF16,
                                     kind="ExternalOutput")
        tap_d["py"] = nc.dram_tensor("tap_py", [C, TOWN], BF16,
                                     kind="ExternalOutput")
    elif taps:
        tap_d["xn"] = nc.dram_tensor("tap_xn", [128, 8 * TOWN], BF16,
                                     kind="ExternalOutput")
        tap_d["A1"] = nc.dram_tensor("tap_A1", [128, 32 * PT_], BF16,
                                     kind="ExternalOutput")
        tap_d["B2"] = nc.dram_tensor("tap_B2", [128, 32 * PT_], BF16,
                                     kind="ExternalOutput")
        tap_d["A3"] = nc.dram_tensor("tap_A3", [128, 32 * PT_], BF16,
                                     kind="ExternalOutput")
        tap_d["py"] = nc.dram_tensor("tap_py", [C, TOWN], BF16,
                                     kind="ExternalOutput")
        tap_d["pyin"] = nc.dram_tensor("tap_pyin", [4, C, TOWN], BF16,
                                       kind="ExternalOutput")
        for nm, shape, dt_ in (("qb0", [128, TW], BF16), ("kvb", [128, TW], BF16),
                               ("gate", [128, 1024], BF16), ("bg", [128, 8], F32),
                               ("Sf", [64, 64], F32), ("Of", [128, 256], F32),
                               ("ogfm", [128, 1024], BF16), ("xraw", [128, TW], F32),
                               ("rr", [128, TW], F32), ("nrc", [128, 4], F32),
                               ("nbb", [128, 1], F32), ("screp", [128, 3], F32),
                               ("cp0", [128, 8], F32), ("cp1", [128, 8], F32)):
            tap_d[nm] = nc.dram_tensor("tap_" + nm, shape, dt_,
                                       kind="ExternalOutput")

    st = contextlib.ExitStack()
    with st:
        sems = {}
        for s in ("T", "V", "S", "G", "Y", "DQS", "DQA", "DQG", "CC"):
            sems[s] = st.enter_context(nc.semaphore(s))
        ctx = Ctx(nc, sems)
        bk = Banks()

        def sb(stk, name, shape, dtype):
            return stk.enter_context(nc.sbuf_tensor(name, list(shape), dtype))

        PF = [st.enter_context(nc.psum_tensor(f"pf{i}", [128, 512], F32))
              for i in range(6)]
        PG = [st.enter_context(nc.psum_tensor(f"pg{i}", [128, 256], BF16))
              for i in range(2)]

        constpool = sb(st, "constpool", (128, 8), F32)
        CONSTS = {1e-5: 0, 1e-12: 1, 1.0: 2, -1.0: 3, -0.5: 4,
                  1.0 / C: 5, 1.0 / 64: 6, 0.5: 7}
        for v, c in CONSTS.items():
            nc.const_aps.register(F32, v, constpool[:, c:c + 1])
        ffn_scales = sb(st, "ffn_scales", (128, 136), F32)
        xn_own = sb(st, "xn_own", (128, 8 * TOWN), BF16)
        S_f = sb(st, "S_f", (64, 64), F32)
        S_b = sb(st, "S_b", (64, 64), BF16)

        p1 = contextlib.ExitStack()
        idb = sb(p1, "idb", (128, 128), BF16)
        triu1_f = sb(p1, "triu1_f", (128, 128), F32)
        tril0_f = sb(p1, "tril0_f", (128, 128), F32)
        ones_f = sb(p1, "ones_f", (1, 128), F32)
        onescol_b = sb(p1, "onescol_b", (128, 1), BF16)
        blk2_b = sb(p1, "blk2_b", (128, 2), BF16)
        colsel_q = sb(p1, "colsel_q", (2, 128), F32)
        colsel_kv = sb(p1, "colsel_kv", (2, 128), F32)
        kv_rhs = sb(p1, "kv_rhs", (2, TW), F32)
        rmsw_rep = sb(p1, "rmsw_rep", (128, 256), F32)
        srow = sb(p1, "srow", (1, 3), F32)
        screp = sb(p1, "screp", (128, 3), F32)
        negbb_rep = sb(p1, "negbb_rep", (128, 1), F32)
        negexpA = sb(p1, "negexpA", (128, 1), F32)
        wqt_b = sb(p1, "wqt_b", (128, 2048), BF16)
        wkvt_b = sb(p1, "wkvt_b", (128, 1024), BF16)
        wgbg_b = sb(p1, "wgbg_b", (128, 8 * 258), BF16)
        wpr_b = sb(p1, "wpr_b", (128, 2048), BF16)
        cwq0 = sb(p1, "cwq0", (128, 4), F32)
        cwq1 = sb(p1, "cwq1", (128, 4), F32)
        cwkv_s = sb(p1, "cwkv_s", (128, 4), F32)
        wsum_fm = sb(p1, "wsum_fm", (1, 384), BF16)
        wsum_gbg = sb(p1, "wsum_gbg", (1, 258), BF16)
        xq = [sb(p1, f"xq{p}", (128, 8 * TW), F32) for p in range(2)]
        xbf = [sb(p1, f"xbf{p}", (128, 8 * TW), BF16) for p in range(2)]
        xsqb = sb(p1, "xsqb", (128, 8 * TW), BF16)
        m_row = sb(p1, "m_row", (1, TW), F32)
        var_row = sb(p1, "var_row", (1, TW), F32)
        rstd_row = sb(p1, "rstd_row", (1, TW), F32)
        mr_row = sb(p1, "mr_row", (1, TW), F32)
        negm_b = sb(p1, "negm_b", (1, TW), BF16)
        rrep = sb(p1, "rrep", (128, TW), F32)
        mrrep = sb(p1, "mrrep", (128, TW), F32)
        qraw0 = sb(p1, "qraw0", (128, TW + 3), F32)
        qraw1 = sb(p1, "qraw1", (128, TW + 3), F32)
        kvraw = sb(p1, "kvraw", (128, TW + 3), F32)
        qc0 = sb(p1, "qc0", (128, TW), F32)
        qc1 = sb(p1, "qc1", (128, TW), F32)
        kvc = sb(p1, "kvc", (128, TW), F32)
        rsq01 = sb(p1, "rsq01", (2, TW), F32)
        rsq23 = sb(p1, "rsq23", (2, TW), F32)
        TB = []
        for ti in range(2):
            tb = {}
            for nm, shape, dt_ in (
                    ("qb0", (128, TW), BF16), ("qb1", (128, TW), BF16),
                    ("kvb", (128, TW), BF16), ("qodd0", (64, TW), BF16),
                    ("qodd1", (64, TW), BF16), ("gate_tm", (128, 1024), BF16),
                    ("beta_c", (128, 4), F32), ("g_c", (128, 4), F32),
                    ("rcol", (128, 4), F32), ("nrcol", (128, 4), F32),
                    ("ogfm", (128, 1024), BF16)):
                tb[nm] = sb(p1, f"{nm}_t{ti}", shape, dt_)
            TB.append(tb)
        CB = []
        for pi in range(4):
            cb = {}
            for nm, shape, dt_ in (
                    ("KKb", (128, 128), F32), ("dect", (128, 128), F32),
                    ("dmst", (128, 128), F32), ("tmpf", (128, 128), F32),
                    ("M_b", (128, 128), BF16), ("B0_b", (128, 128), BF16),
                    ("Y_b", (128, 128), BF16),
                    ("AB0", (128, 256), BF16), ("AB1", (128, 256), BF16),
                    ("RHS2", (128, 128), BF16), ("khat", (128, 64), BF16),
                    ("U2f", (128, 128), F32), ("amsk", (128, 512), BF16),
                    ("WtU", (128, 128), BF16),
                    ("U_b", (128, 64), BF16), ("O_f", (128, 256), F32),
                    ("Osq", (128, 256), F32), ("rms_in", (128, 4), F32),
                    ("rms4", (128, 4), F32), ("grw", (128, 256), F32),
                    ("og_bf", (128, 256), BF16), ("ebcol", (128, 1), F32),
                    ("dlcol", (128, 1), F32), ("eblcol", (128, 1), F32),
                    ("bcol", (128, 1), F32), ("nbcol", (128, 1), F32),
                    ("bebe", (128, 1), F32), ("brow", (1, 128), F32),
                    ("ebtmp", (128, 256), F32)):
                cb[nm] = sb(p1, f"{nm}{pi}", shape, dt_)
            CB.append(cb)
        pystage = sb(p1, "pystage", (128, 1024), BF16)
        p1.close()

        ffnA = sb(st, "ffnA", (128, 32 * PT), BF16)
        ffnB = sb(st, "ffnB", (128, 32 * PT), BF16)
        Wst = [sb(st, f"fw{p}", (128, 4096), BF16) for p in range(2)]
        ft1 = sb(st, "ffnt1", (128, PTW), F32)
        ft2 = sb(st, "ffnt2", (128, PTW), F32)
        fxs = sb(st, "xstage", (128, PT), F32)
        fys_b = sb(st, "ystage_b", (128, PT), BF16)
        fys = sb(st, "ystage", (128, PT), F32)
        fos = sb(st, "ostage", (128, 2 * PT), F32)

        kv = {}

        def setup():
            dma, mm, vec, sca = ctx.dma, ctx.mm, ctx.vec, ctx.sca
            dma("sync", lambda e: e.dma_start(out=idb[:], in_=idn_d.ap()))
            dma("sync", lambda e: e.dma_start(out=triu1_f[:], in_=triu1_d.ap()))
            dma("sync", lambda e: e.dma_start(out=tril0_f[:], in_=tril0_d.ap()))
            dma("sync", lambda e: e.dma_start(out=srow[:], in_=scal_d.ap()))
            dma("sync", lambda e: e.dma_start(out=cwq0[:], in_=cwq_d.ap()[0:128, :]))
            dma("sync", lambda e: e.dma_start(out=cwq1[:], in_=cwq_d.ap()[128:256, :]))
            dma("sync", lambda e: e.dma_start(out=cwkv_s[:], in_=cwkv_d.ap()))
            dma("sync", lambda e: e.dma_start(out=rmsw_rep[0:1, 0:256], in_=rmsw_d.ap()))
            for j, (nm, c0, w) in enumerate([("sc1", 0, 32), ("sc2", 32, 32),
                                             ("sch1", 64, 32), ("sch2", 96, 32),
                                             ("sc3", 128, 8)]):
                dma("sync", lambda e, nm=nm, c0=c0, w=w: e.dma_start(
                    out=ffn_scales[:, c0:c0 + w], in_=sc_d[nm].ap()))
            dma("sync", lambda e: e.dma_start(out=colsel_q[:], in_=csq_d.ap()))
            dma("sync", lambda e: e.dma_start(out=colsel_kv[:], in_=cskv_d.ap()))
            dma("sync", lambda e: e.dma_start(out=kv_rhs[1:2, 0:TW], in_=onesrow_d.ap()))
            t_cl = ctx.tok("DQS")
            kv["t_scales"] = t_cl
            for r in range(8):
                dma("gpsimd", lambda e, r=r: e.dma_start(
                    out=wqt_b[:, 256 * r:256 * (r + 1)],
                    in_=wq_t.ap()[128 * r:128 * (r + 1), :]))
                dma("gpsimd", lambda e, r=r: e.dma_start(
                    out=wkvt_b[:, 128 * r:128 * (r + 1)],
                    in_=wkv_t.ap()[128 * r:128 * (r + 1), :]))
                dma("gpsimd", lambda e, r=r: e.dma_start(
                    out=wgbg_b[:, 258 * r:258 * (r + 1)],
                    in_=wgbg_t.ap()[128 * r:128 * (r + 1), :]))
            for r in range(2):
                dma("gpsimd", lambda e, r=r: e.dma_start(
                    out=wpr_b[:, 1024 * r:1024 * (r + 1)],
                    in_=wpr_t.ap()[128 * r:128 * (r + 1), :]))
            kv["t_wload"] = ctx.tok("DQG")
            for f in (
                lambda e: e.memset(ones_f[:], 1.0),
                lambda e: e.memset(onescol_b[:], 1.0),
                lambda e: e.memset(blk2_b[:], 0.0),
                lambda e: e.memset(blk2_b[0:64, 0:1], 1.0),
                lambda e: e.memset(blk2_b[64:128, 1:2], 1.0),
                lambda e: e.memset(S_f[:], 0.0),
                lambda e: e.memset(S_b[:], 0.0),
                lambda e: e.memset(qraw0[:, 0:3], 0.0),
                lambda e: e.memset(qraw1[:, 0:3], 0.0),
                lambda e: e.memset(kvraw[:, 0:3], 0.0),
            ):
                vec(f, inc=False)
            for v, c in CONSTS.items():
                vec(lambda e, v=v, c=c: e.memset(constpool[:, c:c + 1], v), inc=False)
            if taps:
                tcp = vec(lambda e: e.tensor_copy(rrep[:, 0:8], constpool[:, :]))
                tap("cp0", lambda: rrep[:, 0:8], [tcp])
            vec(lambda e: e.memset(TB[0]["rcol"][:], 0.0), inc=False)
            t_ms = vec(lambda e: e.memset(TB[1]["rcol"][:], 0.0))
            kv["t_consts"] = t_ms
            kv["t_Sb"] = t_ms
            kv["t_Sf"] = t_ms
            ctx.base_dep = t_ms
            t_mm1 = mm(lambda e: e.matmul(PF[1][:, 0:3], ones_f[:, :], srow[:, :],
                                          start=True, stop=True),
                       deps=[t_cl, t_ms] + bk.acquire(1), inc=True)
            t_ev = vec(lambda e: e.tensor_copy(screp[:], PF[1][:, 0:3]), deps=[t_mm1])
            bk.read(1, t_ev)
            sca(lambda e: e.mul(negbb_rep[:], screp[:, 0:1], -1.0), deps=[t_ev],
                inc=False)
            t_e1 = sca(lambda e: e.activation(negexpA[:], screp[:, 1:2], ACT.Exp),
                       deps=[t_ev])
            t_e2 = vec(lambda e: e.tensor_scalar_mul(negexpA[:], negexpA[:], -1.0),
                       deps=[t_e1])
            kv["t_reps"] = t_e2
            t_mm2 = mm(lambda e: e.matmul(PF[2][:, 0:256], ones_f[:, :],
                                          rmsw_rep[0:1, 0:256], start=True, stop=True),
                       deps=[t_cl, t_ms] + bk.acquire(2), inc=True)
            t_rw = vec(lambda e: e.tensor_copy(rmsw_rep[:], PF[2][:, 0:256]),
                       deps=[t_mm2])
            bk.read(2, t_rw)
            kv["t_rmsw"] = t_rw
            d0 = bk.acquire(0)
            for r in range(8):
                mm(lambda e, r=r: e.matmul(PF[0][0:1, 0:256], onescol_b[:, :],
                                           wqt_b[:, 256 * r:256 * (r + 1)],
                                           start=(r == 0), stop=(r == 7)),
                   deps=([kv["t_wload"], t_ms] + d0) if r == 0 else ())
            for r in range(8):
                mm(lambda e, r=r: e.matmul(PF[0][0:1, 256:384], onescol_b[:, :],
                                           wkvt_b[:, 128 * r:128 * (r + 1)],
                                           start=(r == 0), stop=(r == 7)),
                   inc=(r == 7))
            t_q = ctx.tok("T")
            d3 = bk.acquire(3)
            for r in range(8):
                mm(lambda e, r=r: e.matmul(PF[3][0:1, 0:258], onescol_b[:, :],
                                           wgbg_b[:, 258 * r:258 * (r + 1)],
                                           start=(r == 0), stop=(r == 7)),
                   deps=d3 if r == 0 else (), inc=(r == 7))
            t_g = ctx.tok("T")
            t_w1 = vec(lambda e: e.tensor_copy(wsum_fm[:], PF[0][0:1, 0:384]),
                       deps=[t_q])
            bk.read(0, t_w1)
            t_w2 = vec(lambda e: e.tensor_copy(wsum_gbg[:], PF[3][0:1, 0:258]),
                       deps=[t_g])
            bk.read(3, t_w2)
            kv["t_wsum"] = t_w2
            kv["t_wsumf"] = t_w1
            kv["xq_war"] = [[], []]
            kv["xbf_war"] = [[], []]
            kv["py_done"] = []
            kv["og_done"] = []
            kv["py_war"] = [None, None]
            kv["qscr_war"] = []

        def emit_ln(par, t_xload, t_xn_dep=None):
            mm, vec, sca = ctx.mm, ctx.vec, ctx.sca
            t_sq = sca(lambda e: e.activation(xsqb[:], xq[par][:], ACT.Square),
                       deps=[t_xload])
            t_cast = sca(lambda e: e.copy(xbf[par][:], xq[par][:]),
                         deps=[t_xload] + kv["xbf_war"][par])
            kv["xbf_war"][par] = []
            kv["xq_war"][par] += [t_cast, t_sq]
            d0 = bk.acquire(0)
            d1 = bk.acquire(1)
            for r in range(8):
                mm(lambda e, r=r, par=par: e.matmul(
                    PF[0][0:1, 0:TW], onescol_b[:, :], xbf[par][:, TW * r:TW * (r + 1)],
                    start=(r == 0), stop=(r == 7)),
                    deps=([t_cast, kv["t_consts"]] + d0) if r == 0 else (),
                    inc=(r == 7))
            t_s1 = ctx.tok("T")
            for r in range(8):
                mm(lambda e, r=r: e.matmul(
                    PF[1][0:1, 0:TW], onescol_b[:, :], xsqb[:, TW * r:TW * (r + 1)],
                    start=(r == 0), stop=(r == 7)),
                    deps=([t_sq] + d1) if r == 0 else (), inc=(r == 7))
            t_s2 = ctx.tok("T")
            t_m = sca(lambda e: e.activation(m_row[:], PF[0][0:1, 0:TW], ACT.Copy,
                                             scale=1.0 / C), deps=[t_s1])
            bk.read(0, t_m)
            t_v0 = sca(lambda e: e.activation(var_row[:], PF[1][0:1, 0:TW], ACT.Copy,
                                              scale=1.0 / C), deps=[t_s2])
            bk.read(1, t_v0)
            t_msq = vec(lambda e: e.tensor_mul(mr_row[:], m_row[:], m_row[:]),
                        deps=[t_m])
            t_var = vec(lambda e: e.tensor_sub(var_row[:], var_row[:], mr_row[:]),
                        deps=[t_v0, t_msq])
            t_ln = sca(lambda e: e.activation(rstd_row[:], var_row[:], ACT.Ln,
                                              bias=1e-5), deps=[t_var])
            t_r = sca(lambda e: e.activation(rstd_row[:], rstd_row[:], ACT.Exp,
                                             scale=-0.5), deps=[t_ln])
            return t_cast, t_m, t_r

        def gen_front(i):
            mm, vec, sca, dma = ctx.mm, ctx.vec, ctx.sca, ctx.dma
            par = i % 2
            tb = TB[par]
            kv["tb"] = tb
            qb0, qb1, kvb = tb["qb0"], tb["qb1"], tb["kvb"]
            qodd0, qodd1 = tb["qodd0"], tb["qodd1"]
            gate_tm, beta_c, g_c = tb["gate_tm"], tb["beta_c"], tb["g_c"]
            rcol, nrcol, ogfm = tb["rcol"], tb["nrcol"], tb["ogfm"]
            snap = kv.get(f"tsnap{par}")
            if snap is not None:
                for eng in ("tensor", "vector", "scalar", "sync"):
                    deps = [(sm, snap[sm]) for sm in ("T", "V", "S", "DQS", "DQA")]
                    ctx.op(eng, lambda e: None, deps, None)
            t_xload = kv[f"t_x{i}"]
            if i + 1 < NT:
                npar = (i + 1) % 2
                war = kv["xq_war"][npar]
                kv["xq_war"][npar] = []
                for r in range(8):
                    dma("sync", lambda e, r=r, i=i, npar=npar: e.dma_start(
                        out=xq[npar][:, TW * r:TW * (r + 1)],
                        in_=x_fm.ap()[128 * r:128 * (r + 1), TW * (i + 1):TW * (i + 2)]),
                        deps=war)
                kv[f"t_x{i+1}"] = ctx.tok("DQS")
            t_cast, t_m, t_r = emit_ln(par, t_xload)
            yield
            t_nm = sca(lambda e: e.activation(negm_b[:], m_row[:], ACT.Copy,
                                              scale=-1.0), deps=[t_m])
            d5 = bk.acquire(5)
            t_rrmm = mm(lambda e: e.matmul(PF[5][:, 0:TW], ones_f[:, :], rstd_row[:, :],
                                           start=True, stop=True),
                        deps=[t_r] + d5, inc=True)
            t_rrep = vec(lambda e: e.tensor_copy(rrep[:], PF[5][:, 0:TW]),
                         deps=[t_rrmm])
            bk.read(5, t_rrep)
            d3 = bk.acquire(5)
            for cc in range(4):
                mm(lambda e, cc=cc: e.matmul(PF[5][:, cc:cc + 1],
                                             rstd_row[0:1, 128 * cc:128 * (cc + 1)],
                                             ones_f[0:1, 0:1], start=True, stop=True),
                   deps=([t_r] + d3) if cc == 0 else (), inc=(cc == 3))
            t_rcmm = ctx.tok("T")
            t_rc = vec(lambda e: e.tensor_copy(rcol[:], PF[5][:, 0:4]), deps=[t_rcmm])
            bk.read(5, t_rc)
            t_nrc = vec(lambda e: e.tensor_scalar_mul(nrcol[:], rcol[:], -1.0),
                        deps=[t_rc])
            yield
            proj = []
            for (bnk, coff) in ((0, 0), (1, 128), (2, None)):
                dd = bk.acquire(bnk)
                for r in range(8):
                    if coff is None:
                        lhs = lambda e, r=r: wkvt_b[:, 128 * r:128 * (r + 1)]
                    else:
                        lhs = lambda e, r=r, coff=coff: wqt_b[:, 256 * r + coff:256 * r + coff + 128]
                    mm(lambda e, r=r, par=par, bnk=bnk, lhs=lhs: e.matmul(
                        PF[bnk][:, 0:TW], lhs(e), xbf[par][:, TW * r:TW * (r + 1)],
                        start=(r == 0), stop=False),
                        deps=(dd + [t_cast, kv["t_wsumf"], kv["t_wload"]]) if r == 0 else ())
                woff = {0: (0, 128), 1: (128, 256), 2: (256, 384)}[bnk]
                mm(lambda e, bnk=bnk, woff=woff: e.matmul(
                    PF[bnk][:, 0:TW], wsum_fm[0:1, woff[0]:woff[1]], negm_b[:, :],
                    start=False, stop=True), deps=[t_nm], inc=True)
                proj.append(ctx.tok("T"))
            t_q0p, t_q1p, t_kvp = proj
            evs = []
            for (buf, bnk, tq) in ((qraw0, 0, t_q0p), (qraw1, 1, t_q1p), (kvraw, 2, t_kvp)):
                if i > 0:
                    vec(lambda e, buf=buf: e.tensor_copy(buf[:, 0:3], buf[:, TW:TW + 3]),
                        deps=kv["qscr_war"], inc=False)
                t_e = vec(lambda e, buf=buf, bnk=bnk: e.tensor_mul(
                    buf[:, 3:TW + 3], PF[bnk][:, 0:TW], rrep[:]), deps=[tq, t_rrep])
                bk.read(bnk, t_e)
                evs.append(t_e)
            t_evq0, t_evq1, t_evkv = evs
            kv["qscr_war"] = []
            yield
            gtoks = []
            for cc in range(4):
                bnk = (5, 0, 1, 2)[cc]
                dd = bk.acquire(bnk)
                for r in range(8):
                    mm(lambda e, r=r, cc=cc, par=par, bnk=bnk: e.matmul(
                        PF[bnk][:, 0:258],
                        xbf[par][:, TW * r + 128 * cc:TW * r + 128 * cc + 128],
                        wgbg_b[:, 258 * r:258 * (r + 1)], start=(r == 0), stop=False),
                        deps=(dd + [kv["t_wsum"]]) if r == 0 else ())
                mm(lambda e, cc=cc, bnk=bnk: e.matmul(
                    PF[bnk][:, 0:258], negm_b[0:1, 128 * cc:128 * (cc + 1)],
                    wsum_gbg[:, :], start=False, stop=True), deps=[t_nm], inc=True)
                gtoks.append((cc, bnk, ctx.tok("T")))
            for (cc, bnk, t_gc) in gtoks:
                t_si = sca(lambda e, cc=cc, bnk=bnk: e.activation(
                    gate_tm[:, 256 * cc:256 * (cc + 1)], PF[bnk][:, 0:256], ACT.Silu,
                    scale=rcol[:, cc:cc + 1]), deps=[t_gc, t_rc])
                bk.read(bnk, t_si)
            for (cc, bnk, t_gc) in gtoks:
                t_be = sca(lambda e, cc=cc, bnk=bnk: e.activation(
                    beta_c[:, cc:cc + 1], PF[bnk][:, 256:257], ACT.Exp,
                    scale=nrcol[:, cc:cc + 1], bias=negbb_rep[:, :]),
                    deps=[t_gc, t_nrc, kv["t_reps"]])
                t_b2 = vec(lambda e, cc=cc: e.tensor_scalar_add(
                    beta_c[:, cc:cc + 1], beta_c[:, cc:cc + 1], 1.0), deps=[t_be])
                t_b3 = vec(lambda e, cc=cc: e.reciprocal(
                    beta_c[:, cc:cc + 1], beta_c[:, cc:cc + 1]), deps=[t_b2])
                t_ge = sca(lambda e, cc=cc, bnk=bnk: e.activation(
                    g_c[:, cc:cc + 1], PF[bnk][:, 257:258], ACT.Exp,
                    scale=rcol[:, cc:cc + 1], bias=screp[:, 2:3]), deps=[t_gc])
                t_g2 = sca(lambda e, cc=cc: e.activation(
                    g_c[:, cc:cc + 1], g_c[:, cc:cc + 1], ACT.Ln, bias=1.0),
                    deps=[t_ge])
                t_g3 = vec(lambda e, cc=cc: e.tensor_mul(
                    g_c[:, cc:cc + 1], g_c[:, cc:cc + 1], negexpA[:]), deps=[t_g2])
                bk.read(bnk, t_ge)
                kv[f"t_beta{cc}"] = t_b3
                kv[f"t_g{cc}"] = t_g3
            yield
            convs = []
            for (cin, cout, w, tdep) in ((qraw0, qc0, cwq0, t_evq0),
                                         (qraw1, qc1, cwq1, t_evq1),
                                         (kvraw, kvc, cwkv_s, t_evkv)):
                t_cj = vec(lambda e, cin=cin, cout=cout, w=w: e.tensor_scalar_mul(
                    cout[:], cin[:, 0:TW], w[:, 0:1]),
                    deps=[tdep, kv["t_scales"]])
                for j in range(1, 4):
                    t_cj = vec(lambda e, cin=cin, cout=cout, w=w, j=j: e.scalar_tensor_tensor(
                        cout[:], cin[:, j:TW + j], w[:, j:j + 1], cout[:],
                        op0=ALU.mult, op1=ALU.add), deps=[t_cj])
                convs.append(t_cj)
            t_cq0, t_cq1, t_ckv = convs
            kv["qscr_war"] = list(convs)
            yield
            t_sq0 = sca(lambda e: e.activation(qb0[:], qc0[:], ACT.Square),
                        deps=[t_cq0])
            t_sq1 = sca(lambda e: e.activation(qb1[:], qc1[:], ACT.Square),
                        deps=[t_cq1])
            t_sqk = sca(lambda e: e.activation(kvb[0:64, :], kvc[0:64, :], ACT.Square),
                        deps=[t_ckv])
            d0 = bk.acquire(0)
            t_ssq0 = mm(lambda e: e.matmul(PF[0][0:2, 0:TW], blk2_b[:, :], qb0[:, :],
                                           start=True, stop=True),
                        deps=[t_sq0] + d0, inc=True)
            d1 = bk.acquire(1)
            t_ssq1 = mm(lambda e: e.matmul(PF[1][0:2, 0:TW], blk2_b[:, :], qb1[:, :],
                                           start=True, stop=True),
                        deps=[t_sq1] + d1, inc=True)
            d4 = bk.acquire(2)
            t_ssqk = mm(lambda e: e.matmul(PF[2][0:1, 0:TW], onescol_b[0:64, :],
                                           kvb[0:64, :], start=True, stop=True),
                        deps=[t_sqk] + d4, inc=True)
            t_l0 = sca(lambda e: e.activation(rsq01[:], PF[0][0:2, 0:TW], ACT.Ln,
                                              bias=1e-12), deps=[t_ssq0])
            bk.read(0, t_l0)
            t_r0 = sca(lambda e: e.activation(rsq01[:], rsq01[:], ACT.Exp, scale=-0.5),
                       deps=[t_l0])
            t_l1 = sca(lambda e: e.activation(rsq23[:], PF[1][0:2, 0:TW], ACT.Ln,
                                              bias=1e-12), deps=[t_ssq1])
            bk.read(1, t_l1)
            t_r1 = sca(lambda e: e.activation(rsq23[:], rsq23[:], ACT.Exp, scale=-0.5),
                       deps=[t_l1])
            t_lk = sca(lambda e: e.activation(kv_rhs[0:1, 0:TW], PF[2][0:1, 0:TW],
                                              ACT.Ln, bias=1e-12), deps=[t_ssqk])
            bk.read(2, t_lk)
            t_rk = sca(lambda e: e.activation(kv_rhs[0:1, 0:TW], kv_rhs[0:1, 0:TW],
                                              ACT.Exp, scale=-0.5), deps=[t_lk])
            qbt = []
            for (t_seed, rr, sel, src, dst, t_scr) in (
                    (t_r0, rsq01, colsel_q, qc0, qb0, t_sq0),
                    (t_r1, rsq23, colsel_q, qc1, qb1, t_sq1),
                    (t_rk, kv_rhs, colsel_kv, kvc, kvb, t_sqk)):
                d5 = bk.acquire(5)
                t_bq = mm(lambda e, sel=sel, rr=rr: e.matmul(
                    PF[5][:, 0:TW], sel[:, :], rr[:, :], start=True, stop=True),
                    deps=[t_seed] + d5, inc=True)
                t_qb = vec(lambda e, src=src, dst=dst: e.tensor_mul(
                    dst[:], src[:], PF[5][:, 0:TW]), deps=[t_bq, t_scr])
                bk.read(5, t_qb)
                qbt.append(t_qb)
            t_qb0, t_qb1, t_kvb = qbt
            t_o0 = dma("sync", lambda e: e.dma_start(out=qodd0[:, :], in_=qb0[64:128, :]),
                       deps=[t_qb0])
            t_o1 = dma("sync", lambda e: e.dma_start(out=qodd1[:, :], in_=qb1[64:128, :]),
                       deps=[t_qb1])
            kv["t_qb"] = (t_qb0, t_qb1, t_kvb, t_o0, t_o1)
            kv["xbf_war"][par].append(ctx.tok("T"))
            if i == 0 and taps:
                tap("cp1", lambda: constpool[:, :], [t_nrc])
                tap("nrc", lambda: nrcol[:, :], [t_nrc])
                tap("nbb", lambda: negbb_rep[:, :], [kv["t_reps"]])
                tap("screp", lambda: screp[:, :], [kv["t_reps"]])
                tap("qb0", lambda: qb0[:, :], [t_qb0])
                tap("kvb", lambda: kvb[:, :], [t_kvb])
                tap("xraw", lambda: qc0[:, :], [t_cq0])
                tap("rr", lambda: rrep[:, :], [t_rrep])
                tap("gate", lambda: gate_tm[:, :],
                    [kv["t_beta3"], kv["t_g3"], ctx.tok("S")])
                tap("bg", lambda: beta_c[:, :], [kv["t_beta3"]],
                    dst=lambda: tap_d["bg"].ap()[:, 0:4])
                tap("bg", lambda: g_c[:, :], [kv["t_g3"]],
                    dst=lambda: tap_d["bg"].ap()[:, 4:8])
        def emit_tile_rest(i):
            mm, vec, sca, dma = ctx.mm, ctx.vec, ctx.sca, ctx.dma
            par = i % 2
            tb = TB[par]
            ogfm = tb["ogfm"]
            if SCHED == "seq":
                for cc in range(4):
                    drive(gen_indep(i, cc, tb))
                    drive(gen_chain(i, cc, tb))
            elif SCHED == "pair":
                drive(gen_indep(i, 0, tb), gen_indep(i, 1, tb))
                drive(gen_chain(i, 0, tb))
                drive(gen_indep(i, 2, tb))
                drive(gen_chain(i, 1, tb))
                drive(gen_indep(i, 3, tb))
                drive(gen_chain(i, 2, tb))
                drive(gen_chain(i, 3, tb))
            else:
                drive(gen_indep(i, 0, tb), gen_indep(i, 1, tb))
                drive(gen_chain(i, 0, tb), gen_indep(i, 2, tb))
                drive(gen_chain(i, 1, tb), gen_indep(i, 3, tb))
                fnext = gen_front(i + 1) if i + 1 < NT else None
                fnext = drive2(gen_chain(i, 2, tb), fnext)
                fnext = drive2(gen_chain(i, 3, tb), fnext)
                if fnext is not None:
                    drive(fnext)
            if i == 0 and taps:
                tap("Sf", lambda: S_f[:, :], [kv["t_Sf"]])
                tap("Of", lambda: CB[1]["O_f"][:, :], [kv["t_of0"]])
                tap("ogfm", lambda: ogfm[:, :], list(kv["og_done"]))
            quarter, qoff = (TW * i) // TOWN, (TW * i) % TOWN
            ogdeps = kv["og_done"]
            kv["og_done"] = []
            for r in range(8):
                bnk = 1 if r % 2 == 0 else 2
                half = r % 2
                dd = bk.acquire(bnk)
                for j in range(2):
                    mm(lambda e, r=r, j=j, bnk=bnk: e.matmul(
                        PF[bnk][:, 0:TW],
                        wpr_b[:, 1024 * j + 128 * r:1024 * j + 128 * (r + 1)],
                        ogfm[:, TW * j:TW * (j + 1)], start=(j == 0), stop=(j == 1)),
                        deps=(dd + ogdeps + [kv["t_wload"]]) if j == 0 else (),
                        inc=(j == 1))
                t_pm = ctx.tok("T")
                t_pe = vec(lambda e, bnk=bnk, half=half: e.tensor_copy(
                    pystage[:, TW * half:TW * (half + 1)], PF[bnk][:, 0:TW]),
                    deps=[t_pm, kv["py_war"][half]])
                bk.read(bnk, t_pe)
                t_pd = dma("scalar", lambda e, r=r, quarter=quarter, qoff=qoff, half=half:
                           e.dma_start(
                               out=py_in.ap()[quarter, 128 * r:128 * (r + 1),
                                              qoff:qoff + TW],
                               in_=pystage[:, TW * half:TW * (half + 1)]),
                           deps=[t_pe])
                kv["py_war"][half] = t_pd
                kv["py_done"].append(t_pd)
            kv[f"tsnap{par}"] = dict(ctx.counts)

        def drive2(primary, filler):
            while True:
                try:
                    next(primary)
                except StopIteration:
                    break
                if filler is not None:
                    try:
                        next(filler)
                    except StopIteration:
                        filler = None
            return filler

        def drive(*gens):
            gens = [g for g in gens if g is not None]
            while gens:
                nxt = []
                for g in gens:
                    try:
                        next(g)
                        nxt.append(g)
                    except StopIteration:
                        pass
                gens = nxt

        def gen_indep(i, cc, tb):
            mm, vec, sca = ctx.mm, ctx.vec, ctx.sca
            qb0, qb1, kvb = tb["qb0"], tb["qb1"], tb["kvb"]
            qodd0, qodd1 = tb["qodd0"], tb["qodd1"]
            gate_tm, beta_c, g_c = tb["gate_tm"], tb["beta_c"], tb["g_c"]
            b = CB[cc]
            BA, BB = (0, 1) if cc % 2 == 0 else (2, 5)
            BC = BB
            gpi = cc % 2
            sl = slice(128 * cc, 128 * (cc + 1))
            t_qb0, t_qb1, t_kvb, t_o0, t_o1 = kv["t_qb"]
            t_beta = kv[f"t_beta{cc}"]
            t_g = kv[f"t_g{cc}"]
            d0 = bk.acquire(BA)
            t_kk = mm(lambda e: e.matmul(PF[BA][:, 0:128], kvb[0:64, sl], kvb[0:64, sl],
                                         start=True, stop=True),
                      deps=[t_kvb] + d0, inc=True)
            t_kkb = sca(lambda e: e.activation(b["KKb"][:], PF[BA][:, 0:128], ACT.Copy,
                                               scale=beta_c[:, cc:cc + 1]),
                        deps=[t_kk, t_beta])
            bk.read(BA, t_kkb)
            yield
            d1 = bk.acquire(BB)
            mm(lambda e: e.matmul(PF[BB][:, 0:1], triu1_f[:, :], g_c[:, cc:cc + 1],
                                  start=True, stop=True), deps=[t_g] + d1)
            mm(lambda e: e.matmul(PF[BB][:, 64:65], tril0_f[:, :], g_c[:, cc:cc + 1],
                                  start=True, stop=True))
            t_cs = mm(lambda e: e.matmul(PF[BB][0:1, 128:256], g_c[:, cc:cc + 1],
                                         triu1_f[:, :], start=True, stop=True), inc=True)
            t_eb = sca(lambda e: e.activation(b["ebcol"][:], PF[BB][:, 0:1], ACT.Exp),
                       deps=[t_cs])
            t_dl = sca(lambda e: e.activation(b["dlcol"][:], PF[BB][:, 64:65], ACT.Exp),
                       deps=[t_cs])
            t_bc = sca(lambda e: e.copy(b["bcol"][:], PF[BB][:, 0:1]), deps=[t_cs])
            t_nbc = sca(lambda e: e.mul(b["nbcol"][:], PF[BB][:, 0:1], -1.0), deps=[t_cs])
            t_brr = sca(lambda e: e.copy(b["brow"][:], PF[BB][0:1, 128:256]), deps=[t_cs])
            bk.read(BB, t_brr)
            bk.read(BB, t_nbc)
            yield
            t_ebl = vec(lambda e: e.tensor_mul(b["eblcol"][:], b["ebcol"][:], b["dlcol"][:]),
                        deps=[t_eb, t_dl])
            t_bb = vec(lambda e: e.tensor_mul(b["bebe"][:], beta_c[:, cc:cc + 1],
                                              b["ebcol"][:]), deps=[t_eb, t_beta])
            d2 = bk.acquire(BC)
            t_df = mm(lambda e: e.matmul(PF[BC][:, 0:128], ones_f[:, :], b["brow"][:, :],
                                         start=True, stop=True),
                      deps=[t_brr] + d2, inc=True)
            t_dts = sca(lambda e: e.activation(b["dect"][:], PF[BC][:, 0:128], ACT.Exp,
                                               scale=-1.0, bias=b["bcol"][:, :]),
                        deps=[t_df, t_bc])
            t_dst = sca(lambda e: e.activation(b["dmst"][:], PF[BC][:, 0:128], ACT.Exp,
                                               scale=1.0, bias=b["nbcol"][:, :]),
                        deps=[t_df, t_nbc])
            bk.read(BC, t_dts)
            bk.read(BC, t_dst)
            yield
            t_dm = vec(lambda e: e.tensor_mul(b["dmst"][:], b["dmst"][:], triu1_f[:]),
                       deps=[t_dst])
            t_m1 = vec(lambda e: e.tensor_mul(b["tmpf"][:], b["dect"][:], b["KKb"][:]),
                       deps=[t_dts, t_kkb])
            t_m2 = vec(lambda e: e.tensor_mul(b["M_b"][:], b["tmpf"][:], tril0_f[:]),
                       deps=[t_m1])
            yield
            dg0 = bk.acquire(6 + gpi)
            t_tm = mm(lambda e: e.transpose(PG[gpi][:, 0:128], b["M_b"][:, :], idb[:, :]),
                      deps=[t_m2] + dg0, inc=True)
            t_b0 = vec(lambda e: e.tensor_copy(b["B0_b"][:], PG[gpi][:, 0:128]),
                       deps=[t_tm])
            bk.read(6 + gpi, t_b0)
            t_Y = vec(lambda e: e.tensor_sub(b["Y_b"][:], idb[:], b["B0_b"][:]),
                      deps=[t_b0])
            yield
            dg1 = bk.acquire(6 + gpi)
            t_tk = mm(lambda e: e.transpose(PG[gpi][:, 0:128], kvb[:, sl], idb[:, :]),
                      deps=[t_kvb] + dg1, inc=True)
            t_r1 = vec(lambda e: e.tensor_scalar_mul(b["RHS2"][:, 0:64], PG[gpi][:, 0:64],
                                                     b["bebe"][:, :]), deps=[t_tk, t_bb])
            t_r2 = vec(lambda e: e.tensor_scalar_mul(b["RHS2"][:, 64:128],
                                                     PG[gpi][:, 64:128],
                                                     beta_c[:, cc:cc + 1]),
                       deps=[t_tk, t_beta])
            t_kh = vec(lambda e: e.tensor_scalar_mul(b["khat"][:], PG[gpi][:, 0:64],
                                                     b["dlcol"][:, :]), deps=[t_tk, t_dl])
            bk.read(6 + gpi, t_r1)
            bk.read(6 + gpi, t_r2)
            bk.read(6 + gpi, t_kh)
            yield
            Acur, Bcur = b["M_b"][:, 0:128], b["B0_b"][:, 0:128]
            t_A = [t_m2, t_b0]
            for kk in range(1, 7):
                ABn = (b["AB0"], b["AB1"])[kk % 2]
                d0 = bk.acquire(BA)
                t_am = mm(lambda e, Bcur=Bcur, Acur=Acur: e.matmul(
                    PF[BA][:, 0:128], Bcur, Acur, start=True, stop=True),
                    deps=t_A + d0, inc=(kk == 6))
                if kk < 6:
                    t_am = mm(lambda e, Acur=Acur, Bcur=Bcur: e.matmul(
                        PF[BA][:, 128:256], Acur, Bcur, start=True, stop=True),
                        inc=True)
                    t_An = vec(lambda e, ABn=ABn: e.tensor_copy(ABn[:, 0:256],
                                                               PF[BA][:, 0:256]),
                               deps=[t_am])
                else:
                    t_An = vec(lambda e, ABn=ABn: e.tensor_copy(ABn[:, 0:128],
                                                               PF[BA][:, 0:128]),
                               deps=[t_am])
                bk.read(BA, t_An)
                yield
                d2 = bk.acquire(BC)
                t_ym = mm(lambda e, ABn=ABn: e.matmul(
                    PF[BC][:, 0:128], ABn[:, 0:128], b["Y_b"][:, :],
                    start=True, stop=True),
                    deps=[t_An, t_Y] + d2, inc=True)
                t_Y = vec(lambda e: e.tensor_add(b["Y_b"][:], b["Y_b"][:],
                                                 PF[BC][:, 0:128]),
                          deps=[t_ym])
                bk.read(BC, t_Y)
                Acur, Bcur, t_A = ABn[:, 0:128], ABn[:, 128:256], [t_An]
                yield
            d0 = bk.acquire(BA)
            t_u2m = mm(lambda e: e.matmul(PF[BA][:, 0:128], b["Y_b"][:, :], b["RHS2"][:, :],
                                          start=True, stop=True),
                       deps=[t_Y, t_r1, t_r2] + d0, inc=True)
            t_u2f = vec(lambda e: e.tensor_copy(b["U2f"][:], PF[BA][:, 0:128]),
                        deps=[t_u2m])
            bk.read(BA, t_u2f)
            t_u2fin = t_u2f
            yield
            d1 = bk.acquire(BB)
            t_u2t = mm(lambda e: e.matmul(PF[BB][:, 0:128], b["RHS2"][:, :], b["Y_b"][:, :],
                                          start=True, stop=True), deps=d1, inc=True)
            t_wt = vec(lambda e: e.tensor_copy(b["WtU"][0:64, :], PF[BB][0:64, 0:128]),
                       deps=[t_u2t])
            bk.read(BB, t_wt)
            yield
            qsrc = ((qb0, t_qb0), (qodd0, t_o0), (qb1, t_qb1), (qodd1, t_o1))
            d0 = bk.acquire(BC)
            for j in range(4):
                qt, t_qs = qsrc[j]
                mm(lambda e, qt=qt, j=j: e.matmul(
                    PF[BC][:, 128 * j:128 * (j + 1)], kvb[0:64, sl], qt[0:64, sl],
                    start=True, stop=True),
                    deps=([t_qs, t_kvb] + d0) if j == 0 else [t_qs], inc=(j == 3))
            t_kq = ctx.tok("T")
            yield
            t_am = vec(lambda e: e.tensor_tensor(
                out=b["amsk"][:, 0:512].rearrange("p (r c) -> p r c", r=4),
                in0=PF[BC][:, 0:512].rearrange("p (r c) -> p r c", r=4),
                in1=b["dmst"][:].rearrange("p (x c) -> p x c", x=1).broadcast_to(
                    [128, 4, 128]),
                op=ALU.mult), deps=[t_kq, t_dm])
            bk.read(BC, t_am)
            for j in range(4):
                kv[f"t_amsk{cc}_{j}"] = t_am
            yield
            t_grw = vec(lambda e, cc=cc: e.tensor_mul(
                b["grw"][:], gate_tm[:, 256 * cc:256 * (cc + 1)], rmsw_rep[:]),
                deps=[kv["t_rmsw"], kv[f"t_beta{cc}"]])
            kv[f"t_ind{cc}"] = dict(t_wt=t_wt, t_u2fin=t_u2fin, t_kh=t_kh,
                                    t_eb=t_eb, t_ebl=t_ebl, t_grw=t_grw,
                                    t_dm=t_dm)

        def gen_chain(i, cc, tb):
            mm, vec, sca = ctx.mm, ctx.vec, ctx.sca
            qb0, qb1, kvb = tb["qb0"], tb["qb1"], tb["kvb"]
            qodd0, qodd1 = tb["qodd0"], tb["qodd1"]
            gate_tm, beta_c, g_c = tb["gate_tm"], tb["beta_c"], tb["g_c"]
            ogfm = tb["ogfm"]
            b = CB[cc]
            BA, BB = 3, 4
            gpi = (cc + 1) % 2
            ind = kv[f"t_ind{cc}"]
            sl = slice(128 * cc, 128 * (cc + 1))
            t_qb0, t_qb1, t_kvb, t_o0, t_o1 = kv["t_qb"]
            d0 = bk.acquire(BA)
            t_ws = mm(lambda e: e.matmul(PF[BA][:, 0:64], b["WtU"][0:64, :],
                                         S_b[0:64, 0:64], start=True, stop=True),
                      deps=[ind["t_wt"], kv["t_Sb"]] + d0, inc=True)
            t_ub = vec(lambda e: e.scalar_tensor_tensor(
                b["U_b"][:], PF[BA][:, 0:64], -1.0, b["U2f"][:, 64:128],
                op0=ALU.mult, op1=ALU.add), deps=[t_ws, ind["t_u2fin"]])
            bk.read(BA, t_ub)
            yield
            qsrc = ((qb0, t_qb0), (qodd0, t_o0), (qb1, t_qb1), (qodd1, t_o1))
            d4 = bk.acquire(BB)
            d5 = bk.acquire(BA)
            for j in range(4):
                qt, t_qs = qsrc[j]
                mm(lambda e, j=j: e.matmul(
                    PF[BB][:, 64 * j:64 * (j + 1)], b["amsk"][:, 128 * j:128 * (j + 1)],
                    b["U_b"][:, :], start=True, stop=True),
                    deps=([kv[f"t_amsk{cc}_0"], t_ub] + d4) if j == 0 else (),
                    inc=(j == 3))
                if j == 0:
                    t_o1m = None
                mm(lambda e, qt=qt, j=j: e.matmul(
                    PF[BA][:, 64 * j:64 * (j + 1)], qt[0:64, sl], S_b[0:64, 0:64],
                    start=True, stop=True),
                    deps=([kv["t_Sb"]] + d5) if j == 0 else (), inc=(j == 3))
                if j == 1:
                    yield
            t_omm = ctx.tok("T")
            yield
            t_ebs = sca(lambda e: e.activation(b["ebtmp"][:, 0:256], PF[BA][:, 0:256],
                                               ACT.Copy, scale=b["ebcol"][:, :]),
                        deps=[t_omm, ind["t_eb"]])
            t_of = vec(lambda e: e.tensor_add(b["O_f"][:, 0:256], PF[BB][:, 0:256],
                                              b["ebtmp"][:, 0:256]),
                       deps=[t_omm, t_ebs])
            bk.read(BA, t_ebs)
            bk.read(BB, t_of)
            for j in range(4):
                kv[f"t_of{j}"] = t_of
            yield
            d0 = bk.acquire(BA)
            t_ku = mm(lambda e: e.matmul(PF[BA][0:64, 0:64], b["khat"][:, 0:64],
                                         b["U_b"][:, :], start=True, stop=True),
                      deps=[ind["t_kh"], t_ub] + d0, inc=True)
            t_Sf = vec(lambda e: e.scalar_tensor_tensor(
                S_f[:], S_f[:], b["eblcol"][0:64, :], PF[BA][0:64, 0:64],
                op0=ALU.mult, op1=ALU.add), deps=[t_ku, ind["t_ebl"], kv["t_Sf"]])
            bk.read(BA, t_Sf)
            t_Sb = vec(lambda e: e.tensor_copy(S_b[:], S_f[:]), deps=[t_Sf])
            kv["t_Sf"] = t_Sf
            kv["t_Sb"] = t_Sb
            yield
            t_osq = sca(lambda e: e.activation(b["Osq"][:], b["O_f"][:], ACT.Square),
                        deps=[kv["t_of0"]])
            t_red = vec(lambda e: e.reduce_sum(
                b["rms_in"][:, 0:4].rearrange("p (r x) -> p r x", x=1),
                b["Osq"][:, 0:256].rearrange("p (r c) -> p r c", r=4), AX.X),
                deps=[t_osq])
            t_rl = sca(lambda e: e.activation(b["rms4"][:], b["rms_in"][:], ACT.Ln,
                                              scale=1.0 / 64, bias=1e-5), deps=[t_red])
            t_rr = sca(lambda e: e.activation(b["rms4"][:], b["rms4"][:], ACT.Exp,
                                              scale=-0.5), deps=[t_rl])
            yield
            t_og1 = vec(lambda e: e.tensor_tensor(
                out=b["ebtmp"][:, 0:256].rearrange("p (r c) -> p r c", r=4),
                in0=b["O_f"][:, 0:256].rearrange("p (r c) -> p r c", r=4),
                in1=b["rms4"][:, 0:4].rearrange("p (r x) -> p r x", x=1).broadcast_to(
                    [128, 4, 64]),
                op=ALU.mult), deps=[t_rr, kv["t_of0"]])
            t_og = vec(lambda e: e.tensor_mul(b["og_bf"][:, 0:256],
                                              b["ebtmp"][:, 0:256], b["grw"][:, 0:256]),
                       deps=[t_og1, ind["t_grw"]])
            yield
            for half in range(2):
                dg = bk.acquire(6 + gpi)
                t_t = mm(lambda e, half=half: e.transpose(
                    PG[gpi][:, 0:128], b["og_bf"][:, 128 * half:128 * (half + 1)],
                    idb[:, :]), deps=[t_og] + dg, inc=True)
                t_e = vec(lambda e, half=half, cc=cc: e.tensor_copy(
                    ogfm[:, 512 * half + 128 * cc:512 * half + 128 * (cc + 1)],
                    PG[gpi][:, 0:128]), deps=[t_t])
                bk.read(6 + gpi, t_e)
                kv["og_done"].append(t_e)
                yield

        def phase15():
            mm, vec, sca, dma = ctx.mm, ctx.vec, ctx.sca, ctx.dma
            for ii in range(TOWN // TW):
                par = ii % 2
                ctx.guard()
                war = kv["xq_war"][par]
                kv["xq_war"][par] = []
                for r in range(8):
                    dma("sync", lambda e, r=r, ii=ii, par=par: e.dma_start(
                        out=xq[par][:, TW * r:TW * (r + 1)],
                        in_=x_own.ap()[128 * r:128 * (r + 1), TW * ii:TW * (ii + 1)]),
                        deps=war)
                t_xl = ctx.tok("DQS")
                t_cast, t_m, t_r = emit_ln(par, t_xl)
                t_mr = vec(lambda e: e.tensor_mul(mr_row[:], m_row[:], rstd_row[:]),
                           deps=[t_r])
                d5 = bk.acquire(5)
                t_rr = mm(lambda e: e.matmul(PF[5][:, 0:TW], ones_f[:, :],
                                             rstd_row[:, :], start=True, stop=True),
                          deps=[t_r] + d5, inc=True)
                t_rrep = vec(lambda e: e.tensor_copy(rrep[:], PF[5][:, 0:TW]),
                             deps=[t_rr])
                bk.read(5, t_rrep)
                d5 = bk.acquire(5)
                t_mm2 = mm(lambda e: e.matmul(PF[5][:, 0:TW], ones_f[:, :],
                                              mr_row[:, :], start=True, stop=True),
                           deps=[t_mr] + d5, inc=True)
                t_mrep = vec(lambda e: e.tensor_copy(mrrep[:], PF[5][:, 0:TW]),
                             deps=[t_mm2])
                bk.read(5, t_mrep)
                for r in range(8):
                    t_mx = vec(lambda e, r=r, par=par: e.tensor_mul(
                        qc0[:], xq[par][:, TW * r:TW * (r + 1)], rrep[:]),
                        deps=[t_rrep])
                    t_x = vec(lambda e, r=r, ii=ii: e.tensor_sub(
                        xn_own[:, TOWN * r + TW * ii:TOWN * r + TW * (ii + 1)],
                        qc0[:], mrrep[:]), deps=[t_mrep, t_mx])
                    kv["xq_war"][par].append(t_x)
            kv["t_xn"] = ctx.tok("V")

        def phase2():
            pf_w = kv.setdefault("pf_w", [])
            for ob in range(NPRE):
                for wi, wdr in enumerate((w1t, w2t)):
                    pf_w.append(ctx.dma(
                        "gpsimd", lambda e, wdr=wdr, ob=ob, wi=wi: e.dma_start(
                            out=ffnB[:, 2048 * ob + 1024 * wi:
                                     2048 * ob + 1024 * wi + 1024].rearrange(
                                "p (s c) -> p s c", s=8),
                            in_=wdr.ap()[:, 128 * ob:128 * (ob + 1)].rearrange(
                                "(s p) c -> p s c", p=128)),
                        deps=kv["py_done"]))
            if use_cc:
                kv["t_cc"] = ctx.op("gpsimd", lambda e: e.collective_compute(
                    "ReduceScatter", ALU.add,
                    replica_groups=[[0, 1, 2, 3], [4, 5, 6, 7]],
                    ins=[py_in.ap().opt()], outs=[py_out.ap().opt()]),
                    deps=kv["py_done"] + pf_w, inc="CC", inc_n=1)
            else:
                kv["t_cc"] = ctx.op("gpsimd", lambda e: e.dma_start(
                    out=py_out.ap()[:, :], in_=py_in.ap()[0, :, :]),
                    deps=kv["py_done"] + pf_w, inc="CC", inc_n=1)

        def tap(name, src_ap_fn, deps, dst=None):
            if not taps or name not in tap_d:
                return None
            return ctx.dma("sync", lambda e: e.dma_start(
                out=tap_d[name].ap() if dst is None else dst(),
                in_=src_ap_fn()), deps=deps)

        def emit_ffn_pass(p):
            mm, vec, sca, dma = ctx.mm, ctx.vec, ctx.sca, ctx.dma
            war = kv.setdefault("fw_war", [[], []])
            scr_war = kv.setdefault("fscr_war", [None, None, None, None])
            l1war0 = kv.setdefault("l1_war", [[], [], [], []])
            if p == 1:
                for wi in range(2):
                    for slot in range(2):
                        l1war0[2 * wi + slot] = l1war0[2 * wi + slot] + war[wi]
            l1war = kv.setdefault("l1_war", [[], [], [], []])
            for ob in range(32):
                pre = (p == 0 and ob < NPRE)
                slot = ob % 2
                co = 1024 * slot
                wt = [None, None]
                if not pre:
                    for wi, wdr in enumerate((w1t, w2t)):
                        wt[wi] = dma("gpsimd", lambda e, wdr=wdr, ob=ob, wi=wi, co=co:
                                     e.dma_start(
                                         out=Wst[wi][:, co:co + 1024].rearrange(
                                             "p (s c) -> p s c", s=8),
                                         in_=wdr.ap()[:, 128 * ob:128 * (ob + 1)].rearrange(
                                             "(s p) c -> p s c", p=128)),
                                     deps=l1war[2 * wi + slot] + [kv["t_cc"]])
                        l1war[2 * wi + slot] = []
                nb = 2 * NTG
                dd = []
                for b in range(nb):
                    dd += bk.acquire(b)
                for wi in range(2):
                    for tg in range(NTG):
                        b = wi * NTG + tg
                        for k in range(8):
                            last = (k == 7 and wi == 1 and tg == NTG - 1)
                            mm(lambda e, k=k, wi=wi, tg=tg, b=b, p=p, co=co,
                               pre=pre, ob=ob: e.matmul(
                                PF[b][:, 0:PTW],
                                ffnB[:, 2048 * ob + 1024 * wi + 128 * k:
                                     2048 * ob + 1024 * wi + 128 * (k + 1)]
                                if pre else
                                Wst[wi][:, co + 128 * k:co + 128 * (k + 1)],
                                xn_own[:, TOWN * k + PT * p + PTW * tg:
                                       TOWN * k + PT * p + PTW * (tg + 1)],
                                start=(k == 0), stop=(k == 7)),
                                deps=(dd + (kv["pf_w"][-1:] if pre else [wt[0], wt[1]])
                                      + [kv["t_xn"]]) if (k == 0 and wi == 0 and tg == 0) else (),
                                inc=last)
                t_l1 = ctx.tok("T")
                l1war[2 * 0 + slot].append(t_l1)
                l1war[2 * 1 + slot].append(t_l1)
                war[0].append(t_l1) if ob == 31 else None
                war[1].append(t_l1) if ob == 31 else None
                for tg in range(NTG):
                    t_e1 = sca(lambda e, ob=ob, tg=tg: e.activation(
                        ft1[:], PF[tg][:, 0:PTW], ACT.Silu,
                        scale=ffn_scales[:, ob:ob + 1]),
                        deps=[t_l1, kv["t_scales"], scr_war[0]])
                    t_e2 = sca(lambda e, ob=ob, tg=tg: e.activation(
                        ft2[:], PF[NTG + tg][:, 0:PTW], ACT.Copy,
                        scale=ffn_scales[:, 32 + ob:33 + ob]),
                        deps=[scr_war[1]])
                    t_h = vec(lambda e, ob=ob, tg=tg: e.tensor_mul(
                        ffnA[:, PT * ob + PTW * tg:PT * ob + PTW * (tg + 1)],
                        ft1[:], ft2[:]), deps=[t_e1, t_e2])
                    bk.read(tg, t_e1)
                    bk.read(NTG + tg, t_e2)
                    scr_war[0] = t_h
                    scr_war[1] = t_h
            t_src = ctx.tok("V")
            kv["tapA1"] = tap("A1", lambda: ffnA[:, :], [t_src]) if p == 0 else None
            xs, osg = fxs, fos
            for lay, (wdr, src, dst, scoff, nob) in enumerate(
                    ((h1t, ffnA, ffnB, 64, 32), (h2t, ffnB, ffnA, 96, 32),
                     (w3t, ffnA, None, 128, 8))):
                nxt = None
                for ob in range(nob):
                    par = ob % 2
                    t_wl = dma("gpsimd", lambda e, wdr=wdr, ob=ob, par=par: e.dma_start(
                        out=Wst[par][:, :].rearrange("p (s c) -> p s c", s=32),
                        in_=wdr.ap()[:, 128 * ob:128 * (ob + 1)].rearrange(
                            "(s p) c -> p s c", p=128)), deps=war[par] + [kv["t_cc"]])
                    war[par] = []
                    if lay == 2:
                        t_xl = dma("sync", lambda e, ob=ob, p=p: e.dma_start(
                            out=xs[:, :], in_=x_own.ap()[128 * ob:128 * (ob + 1),
                                                         PT * p:PT * (p + 1)]),
                            deps=[kv.get("t_l4adds")])
                        t_yl0 = dma("sync", lambda e, ob=ob, p=p: e.dma_start(
                            out=fys_b[:, :], in_=py_out.ap()[128 * ob:128 * (ob + 1),
                                                             PT * p:PT * (p + 1)]),
                            deps=[kv["t_cc"], kv.get("t_fysc")])
                        t_yl = vec(lambda e: e.tensor_copy(fys[:, :], fys_b[:, :]),
                                   deps=[t_yl0, kv.get("t_l4adds")])
                        kv["t_fysc"] = t_yl
                    dd = []
                    for tg in range(NTG):
                        dd += bk.acquire(tg)
                    t_srcdep = t_src if lay != 2 else kv["l3_done"]
                    for tg in range(NTG):
                        for k in range(32):
                            last = (k == 31 and tg == NTG - 1)
                            mm(lambda e, k=k, tg=tg, par=par, src=src: e.matmul(
                                PF[tg][:, 0:PTW], Wst[par][:, 128 * k:128 * (k + 1)],
                                src[:, PT * k + PTW * tg:PT * k + PTW * (tg + 1)],
                                start=(k == 0), stop=(k == 31)),
                                deps=(dd + [t_wl, t_srcdep]) if (k == 0 and tg == 0) else (),
                                inc=last)
                    t_mmk = ctx.tok("T")
                    war[par].append(t_mmk)
                    for tg in range(NTG):
                        if lay != 2:
                            scr = (ft1, ft2)[tg % 2]
                            t_gx = sca(lambda e, ob=ob, tg=tg, scoff=scoff, scr=scr:
                                       e.activation(scr[:], PF[tg][:, 0:PTW], ACT.Gelu,
                                                    scale=ffn_scales[:, scoff + ob:scoff + ob + 1]),
                                       deps=[t_mmk, scr_war[2 + tg % 2]])
                            t_cx = vec(lambda e, ob=ob, tg=tg, dst=dst, scr=scr: e.tensor_copy(
                                dst[:, PT * ob + PTW * tg:PT * ob + PTW * (tg + 1)], scr[:]),
                                deps=[t_gx, kv["tapA1"] if lay == 1 else None])
                            bk.read(tg, t_gx)
                            scr_war[2 + tg % 2] = t_cx
                            nxt = t_cx
                        else:
                            oso = PT * (ob % 2)
                            fw = kv.setdefault("fos_war", [None, None])
                            t_y = sca(lambda e, ob=ob, tg=tg, oso=oso: e.activation(
                                osg[:, oso + PTW * tg:oso + PTW * (tg + 1)],
                                PF[tg][:, 0:PTW], ACT.Copy,
                                scale=ffn_scales[:, 128 + ob:129 + ob]),
                                deps=[t_mmk, fw[ob % 2]])
                            bk.read(tg, t_y)
                            t_a1 = vec(lambda e, tg=tg, oso=oso: e.tensor_add(
                                osg[:, oso + PTW * tg:oso + PTW * (tg + 1)],
                                osg[:, oso + PTW * tg:oso + PTW * (tg + 1)],
                                xs[:, PTW * tg:PTW * (tg + 1)]), deps=[t_y, t_xl])
                            t_a2 = vec(lambda e, tg=tg, oso=oso: e.tensor_add(
                                osg[:, oso + PTW * tg:oso + PTW * (tg + 1)],
                                osg[:, oso + PTW * tg:oso + PTW * (tg + 1)],
                                fys[:, PTW * tg:PTW * (tg + 1)]), deps=[t_a1, t_yl])
                    if lay == 2:
                        oso = PT * (ob % 2)
                        kv["t_l4adds"] = ctx.tok("V")
                        kv["fos_war"][ob % 2] = dma(
                            "scalar", lambda e, ob=ob, p=p, oso=oso: e.dma_start(
                                out=out_fm.ap()[128 * ob:128 * (ob + 1),
                                                PT * p:PT * (p + 1)],
                                in_=osg[:, oso:oso + PT]), deps=[kv["t_l4adds"]])
                        kv["t_l4dma"] = kv["fos_war"][ob % 2]
                if lay == 0:
                    if p == 0:
                        tap("B2", lambda: ffnB[:, :], [nxt])
                    t_src = nxt
                elif lay == 1:
                    if p == 0:
                        tap("A3", lambda: ffnA[:, :], [nxt])
                    kv["l3_done"] = nxt
            kv["out_done"] = ctx.tok("DQA")

        def emit_all():
            setup()
            if stop_after == "setup":
                pass
            else:
                phase15()
                war = kv["xq_war"][0]
                kv["xq_war"][0] = []
                for r in range(8):
                    ctx.dma("sync", lambda e, r=r: e.dma_start(
                        out=xq[0][:, TW * r:TW * (r + 1)],
                        in_=x_fm.ap()[128 * r:128 * (r + 1), 0:TW]), deps=war)
                kv["t_x0"] = ctx.tok("DQS")
                n_tiles = 1 if stop_after == "tile0" else NT
                drive(gen_front(0))
                for i in range(n_tiles):
                    emit_tile_rest(i)
                if stop_after not in ("tile0", "p1"):
                    tap("pyin", lambda: py_in.ap(), kv["py_done"])
                    tap("xn", lambda: xn_own[:, :], [kv["t_xn"]])
                    if stop_after != "p15":
                        if stop_after != "baronly":
                            phase2()
                        else:
                            kv["t_cc"] = kv["t_consts"]
                        tap("py", lambda: py_out.ap(), [kv["t_cc"]])
                        if stop_after != "cc":
                            ctx.barrier()
                            if stop_after not in ("bar", "baronly"):
                                emit_ffn_pass(0)
                                if stop_after != "ffn0":
                                    emit_ffn_pass(1)
            if stop_after is not None:
                kv["out_done"] = ctx.dma("scalar", lambda e: e.dma_start(
                    out=out_fm.ap()[0:128, 0:TW], in_=pystage[:, 0:TW]),
                    deps=kv.get("py_done", []))
            for eng in ENGS:
                ctx.op(eng, lambda e: None, [kv["out_done"]], None)

        with nc.Block() as block:
            for engname in ENGS:
                def run_pass(engobj, engname=engname):
                    ctx.begin(engname, engobj)
                    bk.war = {}
                    kv.clear()
                    emit_all()
                getattr(block, engname)(run_pass)
    return nc


def _core_inputs(inp, core):
    b, g = core // 4, core % 4
    x = np.asarray(inp["L0_attn_x"], np.float32)
    T = x.shape[1]
    TOWN = T // 4
    d = {}
    xt = np.ascontiguousarray(x[b].T)
    d["x_fm"] = xt
    d["x_own"] = np.ascontiguousarray(xt[:, TOWN * g:TOWN * (g + 1)])
    d["wq_t"] = np.ascontiguousarray(
        np.asarray(inp["L0_attn_Wq"], np.float32).T[:, 256 * g:256 * (g + 1)])
    wk = np.asarray(inp["L0_attn_Wk"], np.float32).T[:, 64 * g:64 * (g + 1)]
    wv = np.asarray(inp["L0_attn_Wv"], np.float32).T[:, 64 * g:64 * (g + 1)]
    d["wkv_t"] = np.ascontiguousarray(np.concatenate([wk, wv], axis=1))
    wg = np.asarray(inp["L0_attn_Wgate"], np.float32).T[:, 256 * g:256 * (g + 1)]
    wb = np.asarray(inp["L0_attn_Wbeta"], np.float32).T[:, g:g + 1]
    wgk = np.asarray(inp["L0_attn_Wgk"], np.float32).T[:, g:g + 1]
    d["wgbg_t"] = np.ascontiguousarray(np.concatenate([wg, wb, wgk], axis=1))
    d["cwq"] = np.ascontiguousarray(
        np.asarray(inp["L0_attn_cwq"], np.float32)[256 * g:256 * (g + 1)])
    cwk = np.asarray(inp["L0_attn_cwk"], np.float32)[64 * g:64 * (g + 1)]
    cwv = np.asarray(inp["L0_attn_cwv"], np.float32)[64 * g:64 * (g + 1)]
    d["cwkv"] = np.ascontiguousarray(np.concatenate([cwk, cwv], axis=0))
    d["scal"] = np.array([[inp["L0_attn_b_beta"][g], inp["L0_attn_A_log"][g],
                           inp["L0_attn_dt_bias"][g]]], np.float32)
    d["rmsw_row"] = np.ascontiguousarray(
        np.tile(np.asarray(inp["L0_attn_rms_w"], np.float32), 4)[None, :])
    d["wpr_t"] = np.ascontiguousarray(
        np.asarray(inp["L0_attn_Wproj"], np.float32).T[256 * g:256 * (g + 1), :])
    return d


def _shared_inputs(inp):
    d = {}
    d["w1t"] = np.ascontiguousarray(np.asarray(inp["L0_ffn_w1_w"], np.float32).T)
    d["w2t"] = np.ascontiguousarray(np.asarray(inp["L0_ffn_w2_w"], np.float32).T)
    d["h1t"] = np.ascontiguousarray(np.asarray(inp["L0_ffn_h1_w"], np.float32).T)
    d["h2t"] = np.ascontiguousarray(np.asarray(inp["L0_ffn_h2_w"], np.float32).T)
    d["w3t"] = np.ascontiguousarray(np.asarray(inp["L0_ffn_w3_w"], np.float32).T)
    d["sc1"] = np.ascontiguousarray(
        np.asarray(inp["L0_ffn_w1_s"], np.float32).reshape(32, 128).T)
    d["sc2"] = np.ascontiguousarray(
        np.asarray(inp["L0_ffn_w2_s"], np.float32).reshape(32, 128).T)
    d["sch1"] = np.ascontiguousarray(
        np.asarray(inp["L0_ffn_h1_s"], np.float32).reshape(32, 128).T)
    d["sch2"] = np.ascontiguousarray(
        np.asarray(inp["L0_ffn_h2_s"], np.float32).reshape(32, 128).T)
    d["sc3"] = np.ascontiguousarray(
        np.asarray(inp["L0_ffn_w3_s"], np.float32).reshape(8, 128).T)
    d["idn_b"] = np.eye(128).astype(BF)
    d["triu1"] = np.triu(np.ones((128, 128), np.float32))
    d["tril0"] = np.tril(np.ones((128, 128), np.float32), -1)
    csq = np.zeros((2, 128), np.float32)
    csq[0, 0:64] = 0.125
    csq[1, 64:128] = 0.125
    d["csq"] = csq
    cskv = np.zeros((2, 128), np.float32)
    cskv[0, 0:64] = 1.0
    cskv[1, 64:128] = 1.0
    d["cskv"] = cskv
    d["onesrow"] = np.ones((1, TW), np.float32)
    return d


def kernel(**inputs):
    inp = {k: np.asarray(v) for k, v in inputs.items()}
    T = inp["L0_attn_x"].shape[1]
    TOWN = T // 4
    nc = build_program(T=T)
    shared = _shared_inputs(inp)
    in_maps = []
    for core in range(8):
        d = _core_inputs(inp, core)
        d.update(shared)
        in_maps.append(d)
    res = run_bass_kernel_spmd(nc, in_maps, core_ids=list(range(8)))
    out = np.empty((2, T, 1024), np.float32)
    for core in range(8):
        b, g = core // 4, core % 4
        out[b, TOWN * g:TOWN * (g + 1), :] = res.results[core]["out_fm"].T
    return out
